# version 09
# baseline (speedup 1.0000x reference)
import numpy as np
import ml_dtypes
from contextlib import ExitStack

import concourse.bass as bass
import concourse.mybir as mybir
from concourse.bass_utils import run_bass_kernel_spmd

F32 = mybir.dt.float32
BF16 = mybir.dt.bfloat16
I8 = mybir.dt.int8
AF = mybir.ActivationFunctionType
ALU = mybir.AluOpType
AX = mybir.AxisListType
AP = bass.AP

NCORES = 8
T = 8192
D = 2048
TOK = T // NCORES
MCn = 16
ICn = 64
OCn = 16
NH = 4
NJ = 32
NCH = 64
EPS = 1.0e-5
NEGBIG = -1.0e30


def _fin(nc, stack, st_sy):
    with nc.Block("blkFin") as blk:
        @blk.sync
        def _(sy):
            sy.wait_ge(st_sy.s, st_sy.n)


def build(taps=(), max_phase=9):
    import os as _os
    C_LVL = int(_os.environ.get("C_LVL", "9"))
    nc = bass.Bass("TRN2", target_bir_lowering=False)

    x_own = nc.dram_tensor("x_own", [TOK, D], F32, kind="ExternalInput")
    w_all = nc.dram_tensor("w_all", [D, 384], F32, kind="ExternalInput")
    wo_in = nc.dram_tensor("wo_in", [256, D], F32, kind="ExternalInput")
    wg_t = nc.dram_tensor("wg_t", [ICn, 128, MCn, 128], I8, kind="ExternalInput")
    wu_t = nc.dram_tensor("wu_t", [ICn, 128, MCn, 128], I8, kind="ExternalInput")
    wd_t = nc.dram_tensor("wd_t", [OCn, 128, ICn, 128], I8, kind="ExternalInput")
    sg_in = nc.dram_tensor("sg_in", [128, ICn], F32, kind="ExternalInput")
    su_in = nc.dram_tensor("su_in", [128, ICn], F32, kind="ExternalInput")
    sd_in = nc.dram_tensor("sd_in", [128, OCn], F32, kind="ExternalInput")
    rc_in = nc.dram_tensor("rc_in", [128, NCH * 32], F32, kind="ExternalInput")
    rs_in = nc.dram_tensor("rs_in", [128, NCH * 32], F32, kind="ExternalInput")
    mk_in = nc.dram_tensor("mk_in", [2, 128, 1024], BF16, kind="ExternalInput")
    idb_in = nc.dram_tensor("idb_in", [128, 128], BF16, kind="ExternalInput")
    idf_in = nc.dram_tensor("idf_in", [128, 128], F32, kind="ExternalInput")

    out_own = nc.dram_tensor("out_own", [TOK, D], F32, kind="ExternalOutput")

    xn_own_d = nc.dram_tensor("xn_own_d", [D, TOK], BF16)
    xn_full_d = nc.dram_tensor(
        "xn_full_d", [NCORES * D, TOK], BF16,
        addr_space="Local" if _os.environ.get("SIM_LOCAL") else "Shared")
    sp_a = nc.dram_tensor("sp_a", [T // 2, D], F32)
    sp_b = nc.dram_tensor("sp_b", [T // 2, D], F32)
    s_own_a = nc.dram_tensor("s_own_a", [TOK // 2, D], F32)
    s_own_b = nc.dram_tensor("s_own_b", [TOK // 2, D], F32)

    tap_t = {}
    for name, shape, dt in (
        ("t_xn", [128, MCn * TOK], BF16),
        ("t_g", [128, OCn * TOK], BF16),
        ("t_qa", [128, T], BF16),
        ("t_qb", [128, T], BF16),
        ("t_kh", [128, T], BF16),
        ("t_v", [128, NCH * 65], BF16),
        ("t_nm", [128, 1], F32),
        ("t_ctx", [128, NCH * 256], BF16),
        ("t_sp", [TOK, D], F32),
    ):
        if name in taps:
            tap_t[name] = nc.dram_tensor(name, shape, dt, kind="ExternalOutput")

    st = ExitStack()

    def sem(name):
        class Ctr:
            pass
        c = Ctr()
        c.s = st.enter_context(nc.semaphore(name))
        c.n = 0
        return c

    ld_sy = sem("ld_sy")
    ld_gp = sem("ld_gp")
    st_sy = sem("st_sy")
    st_tap = sem("st_tap")
    cc = sem("cc")
    vsy = sem("vsy")
    ssy = sem("ssy")

    def VS(eng, ins):
        ins.then_inc(vsy.s, 1)
        vsy.n += 1
        eng.wait_ge(vsy.s, vsy.n)

    def SS(eng, ins):
        ins.then_inc(ssy.s, 1)
        ssy.n += 1
        eng.wait_ge(ssy.s, ssy.n)

    g_fm = st.enter_context(nc.sbuf_tensor("g_fm", [128, OCn * TOK], BF16))
    WALL = st.enter_context(nc.sbuf_tensor("WALL", [128, MCn * 384], BF16))
    SG = st.enter_context(nc.sbuf_tensor("SG", [128, ICn], F32))
    SU = st.enter_context(nc.sbuf_tensor("SU", [128, ICn], F32))
    SD = st.enter_context(nc.sbuf_tensor("SD", [128, OCn], F32))
    IDB = st.enter_context(nc.sbuf_tensor("IDB", [128, 128], BF16))
    IDF = st.enter_context(nc.sbuf_tensor("IDF", [128, 128], F32))
    NMB = st.enter_context(nc.sbuf_tensor("NMB", [128, 1], F32))

    xn_stack = ExitStack()
    xn_fm = xn_stack.enter_context(
        nc.sbuf_tensor("xn_fm", [128, MCn * TOK], BF16))

    NT = TOK // 128
    pA = ExitStack()
    XT = pA.enter_context(nc.sbuf_tensor("XT", [128, 2 * D], F32))
    XSQ = pA.enter_context(nc.sbuf_tensor("XSQ", [128, 2 * D], F32))
    XN = pA.enter_context(nc.sbuf_tensor("XN", [128, 2 * D], BF16))
    LNS = pA.enter_context(nc.sbuf_tensor("LNS", [128, 10], F32))
    TPS0 = pA.enter_context(nc.psum_tensor("TPS0", [128, 1024], BF16))
    TPS1 = pA.enter_context(nc.psum_tensor("TPS1", [128, 1024], BF16))
    if True:
        TPS = (TPS0, TPS1)
        ld0 = ld_sy.n

        with nc.Block("blkA") as blk:
            pe1 = sem("pe1")
            act1 = sem("act1")
            act1r = sem("act1r")
            dve1v = sem("dve1v")
            dve1n = sem("dve1n")
            dve1e = sem("dve1e")
            xs = (sem("xs0"), sem("xs1"))

            @blk.sync
            def _(sy):
                for buf, src in ((SG, sg_in), (SU, su_in), (SD, sd_in),
                                 (IDB, idb_in), (IDF, idf_in)):
                    sy.dma_start(out=buf[:, :], in_=src[:, :]).then_inc(ld_sy.s, 16)
                    ld_sy.n += 16
                for t in range(NT):
                    if t >= 2:
                        sy.wait_ge(dve1n.s, t - 1)
                        sy.wait_ge(act1.s, t - 1)
                    sy.dma_start(
                        out=XT[:, (t % 2) * D : (t % 2 + 1) * D],
                        in_=x_own[128 * t : 128 * (t + 1), :],
                    ).then_inc(xs[t % 2].s, 16)
                    xs[t % 2].n += 16

            @blk.gpsimd
            def _(gp):
                gp.dma_start(
                    out=WALL[:, :],
                    in_=AP(w_all, 0, [[384, 128], [128 * 384, MCn], [1, 384]]),
                ).then_inc(ld_gp.s, 16)
                ld_gp.n += 16

            @blk.scalar
            def _(sc):
                for t in range(NT):
                    p = t % 2
                    sc.wait_ge(xs[p].s, 16 * (t // 2 + 1))
                    if t >= 2:
                        sc.wait_ge(dve1v.s, t - 1)
                    sc.activation(
                        out=XSQ[:, p * D : (p + 1) * D],
                        in_=XT[:, p * D : (p + 1) * D], func=AF.Square,
                    ).then_inc(act1.s, 1)
                    act1.n += 1
                    sc.wait_ge(dve1v.s, t + 1)
                    SS(sc, sc.activation(out=LNS[:, 6 + p : 7 + p],
                                         in_=LNS[:, 4 + p : 5 + p], func=AF.Ln))
                    sc.activation(out=LNS[:, 6 + p : 7 + p],
                                  in_=LNS[:, 6 + p : 7 + p],
                                  func=AF.Exp, scale=-0.5).then_inc(act1r.s, 1)
                    act1r.n += 1

            @blk.vector
            def _(v):
                for t in range(NT):
                    p = t % 2
                    v.wait_ge(xs[p].s, 16 * (t // 2 + 1))
                    if t >= 2:
                        v.wait_ge(act1r.s, t - 1)
                    VS(v, v.reduce_sum(LNS[:, p : p + 1],
                                       XT[:, p * D : (p + 1) * D], axis=AX.X))
                    VS(v, v.tensor_scalar_mul(LNS[:, p : p + 1],
                                              LNS[:, p : p + 1], 1.0 / D))
                    v.wait_ge(act1.s, t + 1)
                    VS(v, v.reduce_sum(LNS[:, 2 + p : 3 + p],
                                       XSQ[:, p * D : (p + 1) * D], axis=AX.X))
                    VS(v, v.tensor_scalar_mul(LNS[:, 2 + p : 3 + p],
                                              LNS[:, 2 + p : 3 + p], 1.0 / D))
                    VS(v, v.tensor_mul(LNS[:, 4 + p : 5 + p], LNS[:, p : p + 1],
                                       LNS[:, p : p + 1]))
                    VS(v, v.tensor_sub(LNS[:, 4 + p : 5 + p],
                                       LNS[:, 2 + p : 3 + p],
                                       LNS[:, 4 + p : 5 + p]))
                    v.tensor_scalar_add(LNS[:, 4 + p : 5 + p],
                                        LNS[:, 4 + p : 5 + p],
                                        EPS).then_inc(dve1v.s, 1)
                    dve1v.n += 1
                    v.wait_ge(act1r.s, t + 1)
                    VS(v, v.scalar_tensor_tensor(
                        LNS[:, 8 + p : 9 + p], LNS[:, p : p + 1], -1.0,
                        LNS[:, 6 + p : 7 + p], ALU.mult, ALU.mult))
                    if t >= 2:
                        v.wait_ge(pe1.s, 16 * (t - 1))
                    v.tensor_scalar(
                        XN[:, p * D : (p + 1) * D],
                        XT[:, p * D : (p + 1) * D],
                        LNS[:, 6 + p : 7 + p], LNS[:, 8 + p : 9 + p],
                        ALU.mult, ALU.add,
                    ).then_inc(dve1n.s, 1)
                    dve1n.n += 1
                    for j in range(MCn):
                        v.wait_ge(pe1.s, 16 * t + j + 1)
                        v.tensor_copy(
                            xn_fm[:, TOK * j + 128 * t : TOK * j + 128 * (t + 1)],
                            TPS[j % 2][:, 0:128],
                        ).then_inc(dve1e.s, 1)
                        dve1e.n += 1

            @blk.tensor
            def _(te):
                te.wait_ge(ld_sy.s, ld0 + 5 * 16)
                for t in range(NT):
                    p = t % 2
                    te.wait_ge(dve1n.s, t + 1)
                    for j in range(MCn):
                        k = 16 * t + j
                        if k >= 2:
                            te.wait_ge(dve1e.s, k - 1)
                        te.transpose(
                            TPS[j % 2][:, 0:128],
                            XN[:, p * D + 128 * j : p * D + 128 * (j + 1)],
                            IDB[:, :],
                        ).then_inc(pe1.s, 1)
                        pe1.n += 1

    pA.close()
    pB = ExitStack()
    fuse = pB.enter_context(nc.sbuf_tensor("fuse", [128, ICn * 512], BF16))
    WGT = pB.enter_context(nc.sbuf_tensor("WGT", [128, 2 * MCn * 128], BF16))
    WUT = pB.enter_context(nc.sbuf_tensor("WUT", [128, 2 * MCn * 128], BF16))
    GS = pB.enter_context(nc.sbuf_tensor("GS", [128, 2 * 512], F32))
    US = pB.enter_context(nc.sbuf_tensor("US", [128, 2 * 512], F32))
    GATE = pB.enter_context(nc.sbuf_tensor("GATE", [128, 2 * 512], BF16))
    WDT = pB.enter_context(nc.sbuf_tensor("WDT", [128, 2 * ICn * 128], BF16))
    DSC = pB.enter_context(nc.sbuf_tensor("DSC", [128, 2 * 512], F32))
    GPS0 = pB.enter_context(nc.psum_tensor("GPS0", [128, 512], F32))
    GPS1 = pB.enter_context(nc.psum_tensor("GPS1", [128, 512], F32))
    UPS0 = pB.enter_context(nc.psum_tensor("UPS0", [128, 512], F32))
    UPS1 = pB.enter_context(nc.psum_tensor("UPS1", [128, 512], F32))
    DPS0 = pB.enter_context(nc.psum_tensor("DPS0", [128, 512], F32))
    DPS1 = pB.enter_context(nc.psum_tensor("DPS1", [128, 512], F32))
    DPS2 = pB.enter_context(nc.psum_tensor("DPS2", [128, 512], F32))
    DPS3 = pB.enter_context(nc.psum_tensor("DPS3", [128, 512], F32))
    if True:
        GPS = (GPS0, GPS1)
        UPS = (UPS0, UPS1)
        DPS = (DPS0, DPS1, DPS2, DPS3)

        st0 = st_sy.n
        gp_final = ld_gp.n

        with nc.Block("blkB") as blk:
            ws = (sem("ws0"), sem("ws1"))
            ds = (sem("ds0"), sem("ds1"))
            pe2 = sem("pe2")
            dve2a = sem("dve2a")
            act2 = sem("act2")
            dve2b = sem("dve2b")
            pe2d = sem("pe2d")
            dve2d = sem("dve2d")
            act2d = sem("act2d")

            @blk.sync
            def _(sy):
                for mc in range(MCn):
                    sy.dma_start(
                        out=xn_own_d[128 * mc : 128 * (mc + 1), :],
                        in_=xn_fm[:, TOK * mc : TOK * (mc + 1)],
                    ).then_inc(st_sy.s, 16)
                    st_sy.n += 16
                if "t_xn" in tap_t:
                    sy.dma_start(out=tap_t["t_xn"][:, :],
                                 in_=xn_fm[:, :]).then_inc(st_tap.s, 16)
                    st_tap.n += 16

            @blk.gpsimd
            def _(gp):
                first = True
                for tb in range(2):
                    for ic in range(ICn):
                        k = tb * ICn + ic
                        if k >= 2:
                            gp.wait_ge(pe2.s, k - 1)
                        sl = k % 2
                        gp.dma_start(
                            out=WGT[:, sl * MCn * 128 : (sl + 1) * MCn * 128],
                            in_=wg_t.ap()[ic].rearrange("p m j -> p (m j)"),
                        ).then_inc(ws[sl].s, 16)
                        gp.dma_start(
                            out=WUT[:, sl * MCn * 128 : (sl + 1) * MCn * 128],
                            in_=wu_t.ap()[ic].rearrange("p m j -> p (m j)"),
                        ).then_inc(ws[sl].s, 16)
                        ws[sl].n += 32
                        if first:
                            first = False
                            import os as _os
                            gp.wait_ge(st_sy.s, st0 + 16 * MCn)
                            if _os.environ.get("C_NO_CC"):
                                gp.sem_inc(cc.s, 1)
                            else:
                                gp.collective_compute(
                                    "AllGather", ALU.bypass,
                                    replica_groups=[list(range(NCORES))],
                                    ins=[xn_own_d.ap().opt()],
                                    outs=[xn_full_d.ap().opt()],
                                ).then_inc(cc.s, 1)
                            cc.n += 1
                            if _os.environ.get("SERIAL_CC"):
                                gp.wait_ge(cc.s, 1)
                    for oc in range(OCn):
                        k = tb * OCn + oc
                        if k >= 2:
                            gp.wait_ge(pe2d.s, k - 1)
                        gp.dma_start(
                            out=WDT[:, (k % 2) * ICn * 128 : (k % 2 + 1) * ICn * 128],
                            in_=wd_t.ap()[oc].rearrange("p i j -> p (i j)"),
                        ).then_inc(ds[k % 2].s, 16)
                        ds[k % 2].n += 16

            @blk.tensor
            def _(te):
                for tb in range(2):
                    for ic in range(ICn):
                        k = tb * ICn + ic
                        sl = k % 2
                        te.wait_ge(ws[sl].s, 32 * (k // 2 + 1))
                        if k >= 2:
                            te.wait_ge(dve2a.s, k - 1)
                        woff = sl * MCn * 128
                        for mc in range(MCn):
                            te.matmul(
                                GPS[sl][:, :],
                                WGT[:, woff + 128 * mc : woff + 128 * (mc + 1)],
                                xn_fm[:, TOK * mc + 512 * tb : TOK * mc + 512 * (tb + 1)],
                                start=(mc == 0), stop=(mc == MCn - 1),
                            )
                        for mc in range(MCn):
                            ins = te.matmul(
                                UPS[sl][:, :],
                                WUT[:, woff + 128 * mc : woff + 128 * (mc + 1)],
                                xn_fm[:, TOK * mc + 512 * tb : TOK * mc + 512 * (tb + 1)],
                                start=(mc == 0), stop=(mc == MCn - 1),
                            )
                        ins.then_inc(pe2.s, 1)
                        pe2.n += 1
                    for oc in range(OCn):
                        k = tb * OCn + oc
                        te.wait_ge(ds[k % 2].s, 16 * (k // 2 + 1))
                        if oc == 0:
                            te.wait_ge(dve2b.s, ICn * (tb + 1))
                        if k >= 4:
                            te.wait_ge(dve2d.s, k - 3)
                        for ic in range(ICn):
                            ins = te.matmul(
                                DPS[k % 4][:, :],
                                WDT[:, (k % 2) * ICn * 128 + 128 * ic :
                                    (k % 2) * ICn * 128 + 128 * (ic + 1)],
                                fuse[:, 512 * ic : 512 * (ic + 1)],
                                start=(ic == 0), stop=(ic == ICn - 1),
                            )
                        ins.then_inc(pe2d.s, 1)
                        pe2d.n += 1

            @blk.vector
            def _(v):
                for tb in range(2):
                    for ic in range(ICn):
                        k = tb * ICn + ic
                        sl = k % 2
                        if ic > 0:
                            v.wait_ge(act2.s, k)
                            v.tensor_mul(
                                fuse[:, 512 * (ic - 1) : 512 * ic],
                                GATE[:, 512 * ((k - 1) % 2) : 512 * ((k - 1) % 2 + 1)],
                                US[:, 512 * ((k - 1) % 2) : 512 * ((k - 1) % 2 + 1)],
                            ).then_inc(dve2b.s, 1)
                            dve2b.n += 1
                        v.wait_ge(pe2.s, k + 1)
                        if ic == 0 and tb == 1:
                            v.wait_ge(pe2d.s, OCn)
                        v.tensor_scalar(
                            GS[:, 512 * sl : 512 * (sl + 1)],
                            GPS[sl][:, :], SG[:, ic : ic + 1], None, ALU.mult)
                        v.tensor_scalar(
                            US[:, 512 * sl : 512 * (sl + 1)],
                            UPS[sl][:, :], SU[:, ic : ic + 1], None,
                            ALU.mult).then_inc(dve2a.s, 1)
                        dve2a.n += 1
                    k = tb * ICn + ICn - 1
                    v.wait_ge(act2.s, k + 1)
                    v.tensor_mul(
                        fuse[:, 512 * (ICn - 1) : 512 * ICn],
                        GATE[:, 512 * (k % 2) : 512 * (k % 2 + 1)],
                        US[:, 512 * (k % 2) : 512 * (k % 2 + 1)],
                    ).then_inc(dve2b.s, 1)
                    dve2b.n += 1
                    for oc in range(OCn):
                        k = tb * OCn + oc
                        v.wait_ge(pe2d.s, k + 1)
                        if k >= 2:
                            v.wait_ge(act2d.s, k - 1)
                        v.tensor_scalar(
                            DSC[:, 512 * (k % 2) : 512 * (k % 2 + 1)],
                            DPS[k % 4][:, :], SD[:, oc : oc + 1], None,
                            ALU.mult).then_inc(dve2d.s, 1)
                        dve2d.n += 1

            @blk.scalar
            def _(sc):
                for tb in range(2):
                    for ic in range(ICn):
                        k = tb * ICn + ic
                        sl = k % 2
                        sc.wait_ge(dve2a.s, k + 1)
                        if k >= 2:
                            sc.wait_ge(dve2b.s, k - 1)
                        sc.activation(
                            out=GATE[:, 512 * sl : 512 * (sl + 1)],
                            in_=GS[:, 512 * sl : 512 * (sl + 1)],
                            func=AF.Silu).then_inc(act2.s, 1)
                        act2.n += 1
                    for oc in range(OCn):
                        k = tb * OCn + oc
                        sc.wait_ge(dve2d.s, k + 1)
                        sc.activation(
                            out=g_fm[:, TOK * oc + 512 * tb : TOK * oc + 512 * (tb + 1)],
                            in_=DSC[:, 512 * (k % 2) : 512 * (k % 2 + 1)],
                            func=AF.Silu).then_inc(act2d.s, 1)
                        act2d.n += 1

        if "t_g" in tap_t:
            with nc.Block("blkBtap") as blk:
                @blk.sync
                def _(sy):
                    sy.dma_start(out=tap_t["t_g"][:, :],
                                 in_=g_fm[:, :]).then_inc(st_tap.s, 16)
                    st_tap.n += 16

    pB.close()
    xn_stack.close()

    if max_phase <= 2:
        pF0 = ExitStack()
        _fin(nc, pF0, st_sy)
        pF0.close()
        st.close()
        return nc

    qS = ExitStack()
    QhatA = qS.enter_context(nc.sbuf_tensor("QhatA", [128, T], BF16))
    QhatB = qS.enter_context(nc.sbuf_tensor("QhatB", [128, T], BF16))
    Khat = qS.enter_context(nc.sbuf_tensor("Khat", [128, T], BF16))
    VT = qS.enter_context(nc.sbuf_tensor("VT", [128, NCH * 65], BF16))
    ctx_tm = qS.enter_context(nc.sbuf_tensor("ctx_tm", [128, NCH * 256], BF16))
    if True:
        pC = ExitStack()
        RC = pC.enter_context(nc.sbuf_tensor("RC", [128, NCH * 32], F32))
        RS_ = pC.enter_context(nc.sbuf_tensor("RS_", [128, NCH * 32], F32))
        SLAB = pC.enter_context(nc.sbuf_tensor("SLAB", [128, 2 * MCn * 512], BF16))
        QTM = pC.enter_context(nc.sbuf_tensor("QTM", [128, 2 * 256], BF16))
        KTM = pC.enter_context(nc.sbuf_tensor("KTM", [128, 2 * 128], BF16))
        TT1 = pC.enter_context(nc.sbuf_tensor("TT1", [128, 128], F32))
        TT2 = pC.enter_context(nc.sbuf_tensor("TT2", [128, 128], F32))
        SQS = pC.enter_context(nc.sbuf_tensor("SQS", [128, 2 * 256], F32))
        SQK = pC.enter_context(nc.sbuf_tensor("SQK", [128, 2 * 64], F32))
        QN = pC.enter_context(nc.sbuf_tensor("QN", [128, NCH * 4], F32))
        KN = pC.enter_context(nc.sbuf_tensor("KN", [128, NCH], F32))
        MS = pC.enter_context(nc.sbuf_tensor("MS", [128, 1], F32))
        MV = pC.enter_context(nc.sbuf_tensor("MV", [1, 8], F32))
        ONES1 = pC.enter_context(nc.sbuf_tensor("ONES1", [1, 128], F32))
        QKVP0 = pC.enter_context(nc.psum_tensor("QKVP0", [128, 512], F32))
        QKVP1 = pC.enter_context(nc.psum_tensor("QKVP1", [128, 512], F32))
        TPS30 = pC.enter_context(nc.psum_tensor("TPS30", [128, 1024], BF16))
        TPS31 = pC.enter_context(nc.psum_tensor("TPS31", [128, 1024], BF16))
        TPS32 = pC.enter_context(nc.psum_tensor("TPS32", [128, 1024], BF16))
        PSMF = pC.enter_context(nc.psum_tensor("PSMF", [128, 512], F32))
        if True:
            QKVP = (QKVP0, QKVP1)
            TPS3 = (TPS30, TPS31, TPS32)
            NHS = 16
            ld1 = ld_sy.n

            with nc.Block("blkC") as blk:
                ini3 = sem("ini3")
                ssl = (sem("ssl0"), sem("ssl1"))
                pe3 = sem("pe3")
                dve3r = sem("dve3r")
                act3v = sem("act3v")
                dve3n = sem("dve3n")
                pe3t = sem("pe3t")
                dve3e = sem("dve3e")
                mseq = sem("mseq")

                @blk.sync
                def _(sy):
                    sy.dma_start(out=RC[:, :],
                                 in_=rc_in[:, :]).then_inc(ld_sy.s, 16)
                    sy.dma_start(out=RS_[:, :],
                                 in_=rs_in[:, :]).then_inc(ld_sy.s, 16)
                    ld_sy.n += 32
                    sy.wait_ge(cc.s, 1)
                    for hs in range(NHS):
                        r, hh = hs // 2, hs % 2
                        if hs >= 2:
                            sy.wait_ge(pe3.s, 4 * (hs - 1))
                        for q in range(4):
                            sy.dma_start(
                                out=SLAB[:, (hs % 2) * MCn * 512 + 2048 * q :
                                     (hs % 2) * MCn * 512 + 2048 * (q + 1)],
                                in_=AP(xn_full_d,
                                       (2048 * r + 512 * q) * TOK + 512 * hh,
                                       [[TOK, 128], [128 * TOK, 4], [1, 512]]),
                            ).then_inc(ssl[hs % 2].s, 16)
                            ssl[hs % 2].n += 16

                @blk.tensor
                def _(te):
                    te.wait_ge(ld_gp.s, gp_final)
                    for g in range(NCH):
                        hs, tc, pp = g // 4, g % 4, g % 2
                        te.wait_ge(ssl[hs % 2].s, 64 * (hs // 2 + 1))
                        if g >= 2:
                            te.wait_ge(dve3r.s, g - 1)
                            if C_LVL >= 2:
                                te.wait_ge(act3v.s, 2 * (g - 2) + 1)
                        for mc in range(MCn):
                            ins = te.matmul(
                                QKVP[pp][:, 0:384],
                                SLAB[:, (hs % 2) * MCn * 512 + 512 * mc + 128 * tc :
                                     (hs % 2) * MCn * 512 + 512 * mc + 128 * (tc + 1)],
                                WALL[:, 384 * mc : 384 * (mc + 1)],
                                start=(mc == 0), stop=(mc == MCn - 1),
                            )
                        ins.then_inc(pe3.s, 1)
                        pe3.n += 1
                        if g >= 1:
                            gg = g - 1
                            te.wait_ge(dve3r.s, gg + 1)
                            if gg >= 1:
                                te.wait_ge(dve3e.s, 3 * gg)
                            qb = (gg % 2) * 256
                            kb = (gg % 2) * 128
                            te.transpose(TPS3[0][:, 0:128],
                                         QTM[:, qb : qb + 128],
                                         IDB[:, :]).then_inc(pe3t.s, 1)
                            te.transpose(TPS3[1][:, 0:128],
                                         QTM[:, qb + 128 : qb + 256],
                                         IDB[:, :]).then_inc(pe3t.s, 1)
                            te.transpose(TPS3[2][:, 0:128],
                                         KTM[:, kb : kb + 128],
                                         IDB[:, :]).then_inc(pe3t.s, 1)
                            pe3t.n += 3
                    gg = NCH - 1
                    te.wait_ge(dve3r.s, gg + 1)
                    te.wait_ge(dve3e.s, 3 * gg)
                    qb = (gg % 2) * 256
                    kb = (gg % 2) * 128
                    te.transpose(TPS3[0][:, 0:128], QTM[:, qb : qb + 128],
                                 IDB[:, :]).then_inc(pe3t.s, 1)
                    te.transpose(TPS3[1][:, 0:128], QTM[:, qb + 128 : qb + 256],
                                 IDB[:, :]).then_inc(pe3t.s, 1)
                    te.transpose(TPS3[2][:, 0:128], KTM[:, kb : kb + 128],
                                 IDB[:, :]).then_inc(pe3t.s, 1)
                    pe3t.n += 3
                    if C_LVL < 5 or _os.environ.get("C_NO_M"):
                        return
                    te.wait_ge(mseq.s, 1)
                    te.transpose(PSMF[0:1, 0:128], MS[:, 0:1], IDF[:, :]
                                 ).then_inc(mseq.s, 1)
                    te.wait_ge(mseq.s, 3)
                    te.transpose(PSMF[0:1, 0:128], MS[:, 0:1], IDF[:, :]
                                 ).then_inc(mseq.s, 1)
                    te.wait_ge(mseq.s, 7)
                    te.matmul(PSMF[:, 0:1], ONES1[0:1, :], MV[0:1, 4:5],
                              start=True, stop=True).then_inc(mseq.s, 1)
                    mseq.n = 8

                @blk.vector
                def _(v):
                    v.memset(VT[:, :], 1.0)
                    v.memset(ONES1[0:1, :], 1.0).then_inc(ini3.s, 1)
                    ini3.n += 1
                    v.wait_ge(ld_sy.s, ld1 + 32)
                    for g in range(NCH):
                        pp = g % 2
                        fh = AP(QKVP[pp], 0, [[512, 128], [64, 4], [1, 32]])
                        sh = AP(QKVP[pp], 32, [[512, 128], [64, 4], [1, 32]])
                        C4 = AP(RC, 32 * g, [[NCH * 32, 128], [0, 4], [1, 32]])
                        S4 = AP(RS_, 32 * g, [[NCH * 32, 128], [0, 4], [1, 32]])
                        qoff = (g % 2) * 256
                        qfh = AP(QTM, qoff, [[2 * 256, 128], [64, 4], [1, 32]])
                        qsh = AP(QTM, qoff + 32, [[2 * 256, 128], [64, 4], [1, 32]])
                        v.wait_ge(pe3.s, g + 1)
                        if g >= 2:
                            v.wait_ge(pe3t.s, 3 * (g - 1))
                            if C_LVL >= 2:
                                v.wait_ge(act3v.s, 2 * (g - 2) + 2)
                        kb = (g % 2) * 128
                        if C_LVL < 3:
                            v.tensor_copy(QTM[:, qoff : qoff + 256],
                                          QKVP[pp][:, 0:256])
                            v.tensor_copy(KTM[:, kb : kb + 128],
                                          QKVP[pp][:, 256:384]).then_inc(dve3r.s, 1)
                            dve3r.n += 1
                            if C_LVL < 4:
                                v.tensor_copy(KN[:, g : g + 1],
                                              QN[:, 0:1]).then_inc(dve3n.s, 1)
                                dve3n.n += 1
                            _skip_rope = True
                        else:
                            _skip_rope = False
                        if not _skip_rope:
                            v.tensor_mul(TT1[:, :], fh, C4)
                            VS(v, v.tensor_mul(TT2[:, :], sh, S4))
                            VS(v, v.tensor_sub(qfh, TT1[:, :], TT2[:, :]))
                            v.tensor_mul(TT1[:, :], sh, C4)
                            VS(v, v.tensor_mul(TT2[:, :], fh, S4))
                            VS(v, v.tensor_add(qsh, TT1[:, :], TT2[:, :]))
                            kfh = AP(QKVP[pp], 256, [[512, 128], [1, 32]])
                            ksh = AP(QKVP[pp], 288, [[512, 128], [1, 32]])
                            C1 = AP(RC, 32 * g, [[NCH * 32, 128], [1, 32]])
                            S1 = AP(RS_, 32 * g, [[NCH * 32, 128], [1, 32]])
                            v.tensor_mul(TT1[:, 0:32], kfh, C1)
                            VS(v, v.tensor_mul(TT2[:, 0:32], ksh, S1))
                            VS(v, v.tensor_sub(KTM[:, kb : kb + 32],
                                               TT1[:, 0:32], TT2[:, 0:32]))
                            v.tensor_mul(TT1[:, 0:32], ksh, C1)
                            VS(v, v.tensor_mul(TT2[:, 0:32], kfh, S1))
                            VS(v, v.tensor_add(KTM[:, kb + 32 : kb + 64],
                                               TT1[:, 0:32], TT2[:, 0:32]))
                            v.tensor_copy(KTM[:, kb + 64 : kb + 128],
                                          KTM[:, kb : kb + 64]).then_inc(dve3r.s, 1)
                            dve3r.n += 1
                        if C_LVL >= 4:
                            v.wait_ge(act3v.s, 2 * g + 2)
                            v.reduce_sum(
                                QN[:, 4 * g : 4 * (g + 1)],
                                AP(SQS, (g % 2) * 256,
                                   [[2 * 256, 128], [64, 4], [1, 64]]),
                                axis=AX.X)
                            _ins = v.reduce_sum(
                                KN[:, g : g + 1],
                                AP(SQK, (g % 2) * 64, [[2 * 64, 128], [1, 64]]),
                                axis=AX.X)
                            if g == NCH - 1:
                                VS(v, _ins)
                            else:
                                _ins.then_inc(dve3n.s, 1)
                                dve3n.n += 1
                        if g >= 1:
                            gg = g - 1
                            v.wait_ge(pe3t.s, 3 * gg + 3)
                            v.tensor_copy(QhatA[:, 128 * gg : 128 * (gg + 1)],
                                          TPS3[0][:, 0:128]).then_inc(dve3e.s, 1)
                            v.tensor_copy(QhatB[:, 128 * gg : 128 * (gg + 1)],
                                          TPS3[1][:, 0:128]).then_inc(dve3e.s, 1)
                            v.tensor_copy(Khat[:, 128 * gg : 128 * (gg + 1)],
                                          TPS3[2][:, 0:128]).then_inc(dve3e.s, 1)
                            dve3e.n += 3
                    gg = NCH - 1
                    v.wait_ge(pe3t.s, 3 * gg + 3)
                    v.tensor_copy(QhatA[:, 128 * gg : 128 * (gg + 1)],
                                  TPS3[0][:, 0:128]).then_inc(dve3e.s, 1)
                    v.tensor_copy(QhatB[:, 128 * gg : 128 * (gg + 1)],
                                  TPS3[1][:, 0:128]).then_inc(dve3e.s, 1)
                    v.tensor_copy(Khat[:, 128 * gg : 128 * (gg + 1)],
                                  TPS3[2][:, 0:128]).then_inc(dve3e.s, 1)
                    dve3e.n += 3
                    if C_LVL < 5 or _os.environ.get("C_NO_M"):
                        v.memset(NMB[:, 0:1], -10.0)
                        return
                    v.reduce_max(MS[:, 0:1], QN[:, :],
                                 axis=AX.X).then_inc(mseq.s, 1)
                    v.wait_ge(mseq.s, 2)
                    VS(v, v.reduce_max(MV[0:1, 0:1], PSMF[0:1, 0:128],
                                       axis=AX.X))
                    v.reduce_max(MS[:, 0:1], KN[:, :],
                                 axis=AX.X).then_inc(mseq.s, 1)
                    v.wait_ge(mseq.s, 4)
                    VS(v, v.reduce_max(MV[0:1, 1:2], PSMF[0:1, 0:128],
                                       axis=AX.X))
                    v.tensor_mul(MV[0:1, 2:3], MV[0:1, 0:1],
                                 MV[0:1, 1:2]).then_inc(mseq.s, 1)
                    v.wait_ge(mseq.s, 6)
                    v.tensor_scalar_mul(MV[0:1, 4:5], MV[0:1, 3:4],
                                        -0.125).then_inc(mseq.s, 1)
                    v.wait_ge(mseq.s, 8)
                    v.tensor_copy(NMB[:, 0:1], PSMF[:, 0:1])

                @blk.scalar
                def _(sc):
                    if C_LVL < 2:
                        return
                    sc.wait_ge(ini3.s, 1)
                    for g in range(NCH):
                        pp = g % 2
                        sc.wait_ge(pe3.s, g + 1)
                        if g >= 2 and C_LVL >= 4:
                            sc.wait_ge(dve3n.s, g - 1)
                        C_SUB = _os.environ.get("C_SUB", "")
                        if C_SUB == "ps1":
                            sc.activation(out=SQK[:, 0:64],
                                          in_=AP(QKVP[pp], 320, [[512, 128], [1, 64]]),
                                          func=AF.Copy).then_inc(act3v.s, 1)
                            sc.wait_ge(dve3r.s, g + 1)
                            sc.activation(out=SQK[0:1, 0:1], in_=SQK[0:1, 1:2],
                                          func=AF.Copy).then_inc(act3v.s, 1)
                            act3v.n += 2
                            continue
                        if C_SUB == "sb1":
                            sc.activation(out=SQK[:, 0:64],
                                          in_=QTM[:, (g % 2) * 256 : (g % 2) * 256 + 64],
                                          func=AF.Copy).then_inc(act3v.s, 1)
                            sc.wait_ge(dve3r.s, g + 1)
                            sc.activation(out=SQK[0:1, 0:1], in_=SQK[0:1, 1:2],
                                          func=AF.Copy).then_inc(act3v.s, 1)
                            act3v.n += 2
                            continue
                        if C_SUB == "min":
                            sc.activation(out=SQK[0:1, 0:1], in_=SQK[0:1, 1:2],
                                          func=AF.Copy).then_inc(act3v.s, 1)
                            sc.wait_ge(dve3r.s, g + 1)
                            sc.activation(out=SQK[0:1, 0:1], in_=SQK[0:1, 1:2],
                                          func=AF.Copy).then_inc(act3v.s, 1)
                            act3v.n += 2
                            continue
                        sc.wait_ge(dve3r.s, g + 1)
                        if C_SUB != "sq":
                            sc.activation(
                                out=AP(VT, 65 * g, [[NCH * 65, 128], [1, 64]]),
                                in_=AP(QKVP[pp], 320, [[512, 128], [1, 64]]),
                                func=AF.Copy).then_inc(act3v.s, 1)
                        else:
                            sc.activation(
                                out=SQK[:, (g % 2) * 64 : (g % 2 + 1) * 64],
                                in_=AP(QKVP[pp], 320, [[512, 128], [1, 64]]),
                                func=AF.Copy).then_inc(act3v.s, 1)
                        if C_SUB != "v":
                            sc.activation(
                                out=SQS[:, (g % 2) * 256 : (g % 2 + 1) * 256],
                                in_=QTM[:, (g % 2) * 256 : (g % 2 + 1) * 256],
                                func=AF.Square)
                            sc.activation(
                                out=SQK[:, (g % 2) * 64 : (g % 2 + 1) * 64],
                                in_=KTM[:, (g % 2) * 128 : (g % 2) * 128 + 64],
                                func=AF.Square).then_inc(act3v.s, 1)
                        else:
                            sc.activation(
                                out=SQK[:, (g % 2) * 64 : (g % 2 + 1) * 64],
                                in_=KTM[:, (g % 2) * 128 : (g % 2) * 128 + 64],
                                func=AF.Square).then_inc(act3v.s, 1)
                        act3v.n += 2
                    if C_LVL < 5 or _os.environ.get("C_NO_M"):
                        return
                    sc.wait_ge(mseq.s, 5)
                    SS(sc, sc.activation(out=MV[0:1, 3:4], in_=MV[0:1, 2:3],
                                         func=AF.Ln))
                    sc.activation(out=MV[0:1, 3:4], in_=MV[0:1, 3:4],
                                  func=AF.Exp, scale=0.5).then_inc(mseq.s, 1)

            if any(t in tap_t for t in ("t_qa", "t_qb", "t_kh", "t_v", "t_nm")):
                with nc.Block("blkCtap") as blk:
                    @blk.sync
                    def _(sy):
                        for nm, buf in (("t_qa", QhatA), ("t_qb", QhatB),
                                        ("t_kh", Khat), ("t_v", VT),
                                        ("t_nm", NMB)):
                            if nm in tap_t:
                                sy.dma_start(out=tap_t[nm][:, :],
                                             in_=buf[:, :]).then_inc(st_tap.s, 16)
                                st_tap.n += 16

        pC.close()
        if max_phase <= 3:
            qS.close()
            pF0 = ExitStack()
            _fin(nc, pF0, st_sy)
            pF0.close()
            st.close()
            return nc
        Wo2 = qS.enter_context(nc.sbuf_tensor("Wo2", [128, 2 * D], BF16))

        pD = ExitStack()
        MSK = pD.enter_context(nc.sbuf_tensor("MSK", [128, 2 * 1024], BF16))
        PT = pD.enter_context(nc.sbuf_tensor("PT", [128, 2 * 1024], BF16))
        SM = pD.enter_context(nc.sbuf_tensor("SM", [128, 1024], F32))
        INV = pD.enter_context(nc.sbuf_tensor("INV", [128, 8], F32))
        CFM = pD.enter_context(nc.sbuf_tensor("CFM", [128, 2 * T], BF16))
        SPB = pD.enter_context(nc.sbuf_tensor("SPB", [128, 2 * D], F32))
        SPS0 = pD.enter_context(nc.psum_tensor("SPS0", [128, 1024], F32))
        SPS1 = pD.enter_context(nc.psum_tensor("SPS1", [128, 1024], F32))
        CPS0 = pD.enter_context(nc.psum_tensor("CPS0", [128, 512], F32))
        CPS1 = pD.enter_context(nc.psum_tensor("CPS1", [128, 512], F32))
        WPS5a = pD.enter_context(nc.psum_tensor("WPS5a", [128, 512], F32))
        WPS5b = pD.enter_context(nc.psum_tensor("WPS5b", [128, 512], F32))
        if True:
            SPS = (SPS0, SPS1)
            CPS = (CPS0, CPS1)
            SCOL = (0, 512, 256, 768)
            jorder = [J for J in range(NJ) if J % 4 < 2] + \
                     [J for J in range(NJ) if J % 4 >= 2]
            jpos = {J: idx for idx, J in enumerate(jorder)}
            sched = []
            kbi = 0
            for J in jorder:
                for kb in range(2 * J + 2):
                    sched.append((J, kb, kbi))
                    kbi += 1
            last_of = {}
            for (J, kb, i) in sched:
                if kb == 2 * J + 1:
                    last_of[i] = jpos[J]

            def chunk_tc(cidx):
                return 2 * jorder[cidx // 2] + (cidx % 2)

            def sp_target(tc):
                cdst, w = tc // 8, tc % 8
                if w < 4:
                    return sp_a, 512 * cdst + 128 * w
                return sp_b, 512 * cdst + 128 * (w - 4)

            a_n = [0, 0]
            for cidx in range(NCH // 2):
                a_n[cidx % 2] += 4 * 16

            ld2 = ld_sy.n
            gp_wo2 = ld_gp.n + 32

            with nc.Block("blkD") as blk:
                pe4s = sem("pe4s")
                act4 = sem("act4")
                pe4a = sem("pe4a")
                dve4m = sem("dve4m")
                dve4c = sem("dve4c")
                sts = (sem("sts0"), sem("sts1"))
                pe5t = sem("pe5t")
                dve5e = sem("dve5e")
                pe5w = sem("pe5w")
                ev5v = sem("ev5v")

                @blk.gpsimd
                def _(gp):
                    for cc2 in range(2):
                        gp.dma_start(
                            out=Wo2[:, D * cc2 : D * (cc2 + 1)],
                            in_=wo_in[128 * cc2 : 128 * (cc2 + 1), :],
                        ).then_inc(ld_gp.s, 16)
                        ld_gp.n += 16
                    gp.wait_ge(sts[0].s, a_n[0])
                    gp.wait_ge(sts[1].s, a_n[1])
                    gp.collective_compute(
                        "ReduceScatter", ALU.add,
                        replica_groups=[list(range(NCORES))],
                        ins=[sp_a.ap().opt()],
                        outs=[s_own_a.ap().opt()],
                    ).then_inc(cc.s, 1)
                    cc.n += 1
                    gp.wait_ge(sts[0].s, a_n[0] + 4 * 16 * (NCH // 4))
                    gp.wait_ge(sts[1].s, a_n[1] + 4 * 16 * (NCH // 4))
                    gp.collective_compute(
                        "ReduceScatter", ALU.add,
                        replica_groups=[list(range(NCORES))],
                        ins=[sp_b.ap().opt()],
                        outs=[s_own_b.ap().opt()],
                    ).then_inc(cc.s, 1)
                    cc.n += 1

                @blk.sync
                def _(sy):
                    sy.dma_start(
                        out=MSK[:, :],
                        in_=AP(mk_in, 0, [[1024, 128], [128 * 1024, 2], [1, 1024]]),
                    ).then_inc(ld_sy.s, 16)
                    ld_sy.n += 16
                    for cidx in range(NCH):
                        tc = chunk_tc(cidx)
                        tgt, row = sp_target(tc)
                        for nb in range(4):
                            sy.wait_ge(ev5v.s, 4 * cidx + nb + 1)
                            sy.dma_start(
                                out=tgt[row : row + 128,
                                        512 * nb : 512 * (nb + 1)],
                                in_=SPB[:, (cidx % 2) * D + 512 * nb :
                                        (cidx % 2) * D + 512 * (nb + 1)],
                            ).then_inc(sts[cidx % 2].s, 16)
                            sts[cidx % 2].n += 16

                @blk.tensor
                def _(te):
                    WPS5 = (WPS5a, WPS5b)

                    def e_pe_J(jidx):
                        te.wait_ge(dve5e.s, 4 * (jidx + 1))
                        for hf2 in range(2):
                            cidx = 2 * jidx + hf2
                            tc = chunk_tc(cidx)
                            for nb in range(4):
                                if 4 * cidx + nb >= 2:
                                    te.wait_ge(ev5v.s, 4 * cidx + nb - 1)
                                for cc2 in range(2):
                                    ins = te.matmul(
                                        WPS5[nb % 2][:, :],
                                        CFM[:, T * cc2 + 128 * tc : T * cc2 + 128 * (tc + 1)],
                                        Wo2[:, D * cc2 + 512 * nb : D * cc2 + 512 * (nb + 1)],
                                        start=(cc2 == 0), stop=(cc2 == 1),
                                        skip_group_check=True,
                                    )
                                ins.then_inc(pe5w.s, 1)
                                pe5w.n += 1

                    def av_group(J, kb, i):
                        pp = i % 2
                        if kb == 0 and jpos[J] >= 1:
                            te.wait_ge(dve4c.s, jpos[J])
                        te.wait_ge(act4.s, i + 1)
                        for h in range(NH):
                            for qs in range(2):
                                ins = te.matmul(
                                    CPS[qs][:, 65 * h : 65 * (h + 1)],
                                    PT[:, 1024 * pp + SCOL[h] + 128 * qs :
                                       1024 * pp + SCOL[h] + 128 * (qs + 1)],
                                    VT[:, 65 * kb : 65 * (kb + 1)],
                                    start=(kb == 0 and h == 0),
                                    stop=(kb == 2 * J + 1),
                                    skip_group_check=True,
                                )
                        ins.then_inc(pe4a.s, 1)
                        pe4a.n += 1

                    te.wait_ge(ld_gp.s, gp_wo2)
                    pe_todo = []
                    for (J, kb, i) in sched:
                        p = i % 2
                        if i >= 2:
                            te.wait_ge(act4.s, i - 1)
                        for h in range(NH):
                            rb = 64 * (h % 2)
                            Q = QhatA if h < 2 else QhatB
                            ins = te.matmul(
                                SPS[p][:, SCOL[h] : SCOL[h] + 256],
                                Khat[rb : rb + 64, 128 * kb : 128 * (kb + 1)],
                                Q[rb : rb + 64, 256 * J : 256 * (J + 1)],
                                start=True, stop=True,
                                tile_position=(rb, 0),
                            )
                        ins.then_inc(pe4s.s, 1)
                        pe4s.n += 1
                        if i >= 1:
                            av_group(*sched[i - 1])
                            if (i - 1) in last_of:
                                pe_todo.append(last_of[i - 1])
                        if kb == min(4, 2 * J + 1):
                            while pe_todo:
                                e_pe_J(pe_todo.pop(0))
                    av_group(*sched[-1])
                    pe_todo.append(NJ - 1)
                    while pe_todo:
                        e_pe_J(pe_todo.pop(0))

                @blk.vector
                def _(v):
                    WPS5 = (WPS5a, WPS5b)

                    def e_dve_tp_J(jidx):
                        for hf2 in range(2):
                            cidx = 2 * jidx + hf2
                            tc = chunk_tc(cidx)
                            for hf in range(2):
                                ins = None
                                for a in range(4):
                                    for b in range(4):
                                        ins = v.transpose(
                                            CFM[32 * a : 32 * (a + 1),
                                                T * hf + 128 * tc + 32 * b :
                                                T * hf + 128 * tc + 32 * (b + 1)],
                                            ctx_tm[32 * b : 32 * (b + 1),
                                                   256 * tc + 128 * hf + 32 * a :
                                                   256 * tc + 128 * hf + 32 * (a + 1)],
                                        )
                                ins.then_inc(dve5e.s, 1)
                                dve5e.n += 1

                    def e_dve_J(jidx):
                        for hf2 in range(2):
                            cidx = 2 * jidx + hf2
                            for nb in range(4):
                                v.wait_ge(pe5w.s, 4 * cidx + nb + 1)
                                if cidx >= 2 and nb == 0:
                                    v.wait_ge(sts[cidx % 2].s, 64 * (cidx // 2))
                                v.tensor_copy(
                                    SPB[:, (cidx % 2) * D + 512 * nb :
                                        (cidx % 2) * D + 512 * (nb + 1)],
                                    WPS5[nb % 2][:, :],
                                ).then_inc(ev5v.s, 1)
                                ev5v.n += 1

                    v.wait_ge(ld_sy.s, ld2 + 16)
                    sm_last = -1
                    dve_todo = []
                    for (J, kb, i) in sched:
                        p = i % 2
                        if kb >= 2 * J:
                            rel = kb - 2 * J
                            v.wait_ge(pe4s.s, i + 1)
                            if sm_last >= 0:
                                v.wait_ge(act4.s, sm_last + 1)
                            v.tensor_add(
                                SM[:, :],
                                SPS[p][:, :],
                                MSK[:, 1024 * rel : 1024 * (rel + 1)],
                            ).then_inc(dve4m.s, 1)
                            dve4m.n += 1
                            sm_last = i
                        if kb == min(5, 2 * J):
                            while dve_todo:
                                e_dve_J(dve_todo.pop(0))
                        if kb == 2 * J + 1:
                            jidx = jpos[J]
                            v.wait_ge(pe4a.s, i + 1)
                            for qs in range(2):
                                cp = CPS[qs]
                                VS(v, v.reciprocal(
                                    INV[:, 4 * qs : 4 * (qs + 1)],
                                    AP(cp, 64, [[512, 128], [65, 4]])))
                                for h in range(NH):
                                    ins = v.tensor_scalar(
                                        ctx_tm[:, 256 * (2 * J + qs) + 64 * h :
                                               256 * (2 * J + qs) + 64 * (h + 1)],
                                        cp[:, 65 * h : 65 * h + 64],
                                        INV[:, 4 * qs + h : 4 * qs + h + 1],
                                        None, ALU.mult)
                            ins.then_inc(dve4c.s, 1)
                            dve4c.n += 1
                            VS(v, v.tensor_scalar_mul(INV[:, 0:1], INV[:, 0:1],
                                                      1.0))
                            e_dve_tp_J(jidx)
                            dve_todo.append(jidx)
                    while dve_todo:
                        e_dve_J(dve_todo.pop(0))

                @blk.scalar
                def _(sc):
                    sc.wait_ge(ld_sy.s, ld2 + 16)
                    dcnt = 0
                    for (J, kb, i) in sched:
                        p = i % 2
                        if i >= 2:
                            sc.wait_ge(pe4a.s, i - 1)
                        if kb >= 2 * J:
                            dcnt += 1
                            sc.wait_ge(dve4m.s, dcnt)
                            src = SM[:, :]
                        else:
                            sc.wait_ge(pe4s.s, i + 1)
                            src = SPS[p][:, :]
                        sc.activation(
                            out=PT[:, 1024 * p : 1024 * (p + 1)],
                            in_=src, func=AF.Exp,
                            bias=NMB[:, 0:1], scale=0.125,
                        ).then_inc(act4.s, 1)
                        act4.n += 1

            if "t_ctx" in tap_t:
                with nc.Block("blkDtap") as blk:
                    @blk.sync
                    def _(sy):
                        sy.dma_start(out=tap_t["t_ctx"][:, :],
                                     in_=ctx_tm[:, :]).then_inc(st_tap.s, 16)
                        st_tap.n += 16

        pD.close()
        if max_phase <= 4:
            qS.close()
            pF0 = ExitStack()
            _fin(nc, pF0, st_sy)
            pF0.close()
            st.close()
            return nc
    qS.close()
    if max_phase <= 5:
        pF0 = ExitStack()
        _fin(nc, pF0, st_sy)
        pF0.close()
        st.close()
        return nc
    pF = ExitStack()
    SOB = pF.enter_context(nc.sbuf_tensor("SOB", [128, 2 * D], F32))
    XR = pF.enter_context(nc.sbuf_tensor("XR", [128, 2 * D], F32))
    TGW = pF.enter_context(nc.sbuf_tensor("TGW", [128, D], F32))
    OUTB = pF.enter_context(nc.sbuf_tensor("OUTB", [128, 2 * D], F32))
    CTPF0 = pF.enter_context(nc.psum_tensor("CTPF0", [128, 1024], BF16))
    CTPF1 = pF.enter_context(nc.psum_tensor("CTPF1", [128, 1024], BF16))
    if True:
        CTPF = (CTPF0, CTPF1)
        ld3 = ld_sy.n
        st2 = st_sy.n

        with nc.Block("blkF") as blk:
            pe6 = sem("pe6")
            dve6 = sem("dve6")
            dve6o = sem("dve6o")
            fsl = (sem("fsl0"), sem("fsl1"))
            sto = (sem("sto0"), sem("sto1"))

            @blk.sync
            def _(sy):
                for t in range(NT):
                    if t == 0:
                        sy.wait_ge(cc.s, 2)
                    if t == 4:
                        sy.wait_ge(cc.s, 3)
                    if t >= 2:
                        sy.wait_ge(dve6o.s, 16 * (t - 1))
                    src_d = s_own_a if t < 4 else s_own_b
                    sy.dma_start(
                        out=SOB[:, (t % 2) * D : (t % 2 + 1) * D],
                        in_=src_d[128 * (t % 4) : 128 * (t % 4 + 1), :],
                    ).then_inc(fsl[t % 2].s, 16)
                    sy.dma_start(
                        out=XR[:, (t % 2) * D : (t % 2 + 1) * D],
                        in_=x_own[128 * t : 128 * (t + 1), :],
                    ).then_inc(fsl[t % 2].s, 16)
                    fsl[t % 2].n += 32
                    sy.wait_ge(dve6o.s, 16 * (t + 1))
                    sy.dma_start(
                        out=out_own[128 * t : 128 * (t + 1), :],
                        in_=OUTB[:, (t % 2) * D : (t % 2 + 1) * D],
                    ).then_inc(sto[t % 2].s, 16)
                    sto[t % 2].n += 16
                if "t_sp" in tap_t:
                    sy.dma_start(out=tap_t["t_sp"][:, :],
                                 in_=s_own_d[:, :]).then_inc(st_tap.s, 16)
                    st_tap.n += 16
                sy.wait_ge(sto[0].s, sto[0].n)
                sy.wait_ge(sto[1].s, sto[1].n)
                sy.wait_ge(st_sy.s, st_sy.n)
                if st_tap.n:
                    sy.wait_ge(st_tap.s, st_tap.n)

            @blk.tensor
            def _(te):
                for t in range(NT):
                    for oc in range(OCn):
                        k = OCn * t + oc
                        if k >= 2:
                            te.wait_ge(dve6.s, k - 1)
                        te.transpose(
                            CTPF[k % 2][:, 0:128],
                            g_fm[:, TOK * oc + 128 * t : TOK * oc + 128 * (t + 1)],
                            IDB[:, :]).then_inc(pe6.s, 1)
                        pe6.n += 1

            @blk.vector
            def _(v):
                for t in range(NT):
                    p = t % 2
                    v.wait_ge(fsl[p].s, 32 * (t // 2 + 1))
                    if t >= 2:
                        v.wait_ge(sto[p].s, 16 * (t // 2))
                    for oc in range(OCn):
                        k = OCn * t + oc
                        v.wait_ge(pe6.s, k + 1)
                        ins = v.tensor_mul(
                            TGW[:, 128 * oc : 128 * (oc + 1)],
                            SOB[:, p * D + 128 * oc : p * D + 128 * (oc + 1)],
                            CTPF[k % 2][:, 0:128])
                        ins.then_inc(dve6.s, 1)
                        dve6.n += 1
                    for oc in range(OCn):
                        v.wait_ge(dve6.s, OCn * t + oc + 1)
                        v.tensor_add(
                            OUTB[:, p * D + 128 * oc : p * D + 128 * (oc + 1)],
                            XR[:, p * D + 128 * oc : p * D + 128 * (oc + 1)],
                            TGW[:, 128 * oc : 128 * (oc + 1)]).then_inc(dve6o.s, 1)
                        dve6o.n += 1
                    dve6o.n += 0

    pF.close()
    st.close()
    return nc


def _host_prep(inputs):
    x = np.ascontiguousarray(np.asarray(inputs["L0_attn_x"]).reshape(T, D))
    Wq = np.asarray(inputs["L0_attn_Wq"])
    Wk = np.asarray(inputs["L0_attn_Wk"])
    Wv = np.asarray(inputs["L0_attn_Wv"])
    Wo = np.asarray(inputs["L0_attn_Wo"])
    wg = np.asarray(inputs["L0_ffn_w_gate"])
    wu = np.asarray(inputs["L0_ffn_w_up"])
    wd = np.asarray(inputs["L0_ffn_w_down"])
    sg = np.asarray(inputs["L0_ffn_s_gate"])
    su = np.asarray(inputs["L0_ffn_s_up"])
    sd = np.asarray(inputs["L0_ffn_s_down"])

    wg_t = np.ascontiguousarray(
        wg.reshape(MCn, 128, ICn, 128).transpose(2, 1, 0, 3))
    wu_t = np.ascontiguousarray(
        wu.reshape(MCn, 128, ICn, 128).transpose(2, 1, 0, 3))
    wd_t = np.ascontiguousarray(
        wd.reshape(ICn, 128, OCn, 128).transpose(2, 1, 0, 3))
    sgT = np.ascontiguousarray(sg.reshape(ICn, 128).T)
    suT = np.ascontiguousarray(su.reshape(ICn, 128).T)
    sdT = np.ascontiguousarray(sd.reshape(OCn, 128).T)

    theta = (1.0 / np.power(
        10000.0, np.arange(0, 64, 2, dtype=np.float32) / 64.0)).astype(np.float32)
    pos = np.arange(T, dtype=np.float32)
    ang = np.outer(pos, theta)
    rc = np.cos(ang).astype(np.float32)
    rs = np.sin(ang).astype(np.float32)
    rc_t = np.ascontiguousarray(
        rc.reshape(NCH, 128, 32).transpose(1, 0, 2).reshape(128, NCH * 32))
    rs_t = np.ascontiguousarray(
        rs.reshape(NCH, 128, 32).transpose(1, 0, 2).reshape(128, NCH * 32))

    kk = np.arange(128)[:, None]
    qq = np.arange(256)[None, :]
    mk = np.zeros((2, 128, 1024), dtype=ml_dtypes.bfloat16)
    for rel in range(2):
        m = np.where(128 * rel + kk > qq, NEGBIG, 0.0).astype(ml_dtypes.bfloat16)
        mk[rel] = np.tile(m, (1, NH))

    idb = np.eye(128, dtype=ml_dtypes.bfloat16)
    idf = np.eye(128, dtype=np.float32)

    in_maps = []
    for c in range(NCORES):
        w_all_c = np.ascontiguousarray(np.concatenate(
            [Wq[:, 256 * c : 256 * (c + 1)],
             Wk[:, 64 * c : 64 * (c + 1)],
             Wv[:, 64 * c : 64 * (c + 1)]], axis=1))
        in_maps.append({
            "x_own": np.ascontiguousarray(x[TOK * c : TOK * (c + 1)]),
            "w_all": w_all_c,
            "wo_in": np.ascontiguousarray(Wo[256 * c : 256 * (c + 1), :]),
            "wg_t": wg_t, "wu_t": wu_t, "wd_t": wd_t,
            "sg_in": sgT, "su_in": suT, "sd_in": sdT,
            "rc_in": rc_t, "rs_in": rs_t, "mk_in": mk,
            "idb_in": idb, "idf_in": idf,
        })
    return in_maps


def run(inputs, taps=(), max_phase=9):
    in_maps = _host_prep(inputs)
    nc = build(taps=taps, max_phase=max_phase)
    res = run_bass_kernel_spmd(nc, in_maps, core_ids=list(range(NCORES)))
    return res.results


def kernel(**inputs):
    results = run(inputs)
    out = np.concatenate([results[c]["out_own"] for c in range(NCORES)], axis=0)
    return out.reshape(1, T, D).astype(np.float32)


# version 10
# speedup vs baseline: 1.0571x; 1.0571x over previous
import numpy as np
import ml_dtypes
from contextlib import ExitStack

import concourse.bass as bass
import concourse.mybir as mybir
from concourse.bass_utils import run_bass_kernel_spmd

F32 = mybir.dt.float32
BF16 = mybir.dt.bfloat16
I8 = mybir.dt.int8
AF = mybir.ActivationFunctionType
ALU = mybir.AluOpType
AX = mybir.AxisListType
AP = bass.AP

NCORES = 8
T = 8192
D = 2048
TOK = T // NCORES
MCn = 16
ICn = 64
OCn = 16
NH = 4
NJ = 32
NCH = 64
EPS = 1.0e-5
NEGBIG = -1.0e30


def _fin(nc, stack, st_sy):
    with nc.Block("blkFin") as blk:
        @blk.sync
        def _(sy):
            sy.wait_ge(st_sy.s, st_sy.n)


def build(taps=(), max_phase=9):
    import os as _os
    C_LVL = int(_os.environ.get("C_LVL", "9"))
    nc = bass.Bass("TRN2", target_bir_lowering=False)

    x_own = nc.dram_tensor("x_own", [TOK, D], F32, kind="ExternalInput")
    w_all = nc.dram_tensor("w_all", [D, 384], F32, kind="ExternalInput")
    wo_in = nc.dram_tensor("wo_in", [256, D], F32, kind="ExternalInput")
    wg_t = nc.dram_tensor("wg_t", [ICn, 128, MCn, 128], I8, kind="ExternalInput")
    wu_t = nc.dram_tensor("wu_t", [ICn, 128, MCn, 128], I8, kind="ExternalInput")
    wd_t = nc.dram_tensor("wd_t", [OCn, 128, ICn, 128], I8, kind="ExternalInput")
    sg_in = nc.dram_tensor("sg_in", [128, ICn], F32, kind="ExternalInput")
    su_in = nc.dram_tensor("su_in", [128, ICn], F32, kind="ExternalInput")
    sd_in = nc.dram_tensor("sd_in", [128, OCn], F32, kind="ExternalInput")
    rc_in = nc.dram_tensor("rc_in", [128, NCH * 32], F32, kind="ExternalInput")
    rs_in = nc.dram_tensor("rs_in", [128, NCH * 32], F32, kind="ExternalInput")
    mk_in = nc.dram_tensor("mk_in", [2, 128, 1024], F32, kind="ExternalInput")
    idb_in = nc.dram_tensor("idb_in", [128, 128], BF16, kind="ExternalInput")
    idf_in = nc.dram_tensor("idf_in", [128, 128], F32, kind="ExternalInput")

    out_own = nc.dram_tensor("out_own", [TOK, D], F32, kind="ExternalOutput")

    xn_own_d = nc.dram_tensor("xn_own_d", [D, TOK], BF16)
    xn_full_d = nc.dram_tensor(
        "xn_full_d", [NCORES * D, TOK], BF16,
        addr_space="Local" if _os.environ.get("SIM_LOCAL") else "Shared")
    sp_a = nc.dram_tensor("sp_a", [T // 2, D], F32)
    sp_b = nc.dram_tensor("sp_b", [T // 2, D], F32)
    s_own_a = nc.dram_tensor("s_own_a", [TOK // 2, D], F32)
    s_own_b = nc.dram_tensor("s_own_b", [TOK // 2, D], F32)

    tap_t = {}
    for name, shape, dt in (
        ("t_xn", [128, MCn * TOK], BF16),
        ("t_g", [128, OCn * TOK], BF16),
        ("t_qa", [128, T], BF16),
        ("t_qb", [128, T], BF16),
        ("t_kh", [128, T], BF16),
        ("t_v", [128, NCH * 65], BF16),
        ("t_nm", [128, 1], F32),
        ("t_ctx", [128, NCH * 256], BF16),
        ("t_sp", [TOK, D], F32),
    ):
        if name in taps:
            tap_t[name] = nc.dram_tensor(name, shape, dt, kind="ExternalOutput")

    st = ExitStack()

    def sem(name):
        class Ctr:
            pass
        c = Ctr()
        c.s = st.enter_context(nc.semaphore(name))
        c.n = 0
        return c

    ld_sy = sem("ld_sy")
    ld_gp = sem("ld_gp")
    st_sy = sem("st_sy")
    st_tap = sem("st_tap")
    cc = sem("cc")
    vsy = sem("vsy")
    ssy = sem("ssy")

    def VS(eng, ins):
        ins.then_inc(vsy.s, 1)
        vsy.n += 1
        eng.wait_ge(vsy.s, vsy.n)

    def SS(eng, ins):
        ins.then_inc(ssy.s, 1)
        ssy.n += 1
        eng.wait_ge(ssy.s, ssy.n)

    g_fm = st.enter_context(nc.sbuf_tensor("g_fm", [128, OCn * TOK], BF16))
    WALL = st.enter_context(nc.sbuf_tensor("WALL", [128, MCn * 384], BF16))
    SG = st.enter_context(nc.sbuf_tensor("SG", [128, ICn], F32))
    SU = st.enter_context(nc.sbuf_tensor("SU", [128, ICn], F32))
    SD = st.enter_context(nc.sbuf_tensor("SD", [128, OCn], F32))
    IDB = st.enter_context(nc.sbuf_tensor("IDB", [128, 128], BF16))
    IDF = st.enter_context(nc.sbuf_tensor("IDF", [128, 128], F32))
    NMB = st.enter_context(nc.sbuf_tensor("NMB", [128, 1], F32))

    xn_stack = ExitStack()
    xn_fm = xn_stack.enter_context(
        nc.sbuf_tensor("xn_fm", [128, MCn * TOK], BF16))

    NT = TOK // 128
    pA = ExitStack()
    XT = pA.enter_context(nc.sbuf_tensor("XT", [128, 2 * D], F32))
    XSQ = pA.enter_context(nc.sbuf_tensor("XSQ", [128, 2 * D], F32))
    XN = pA.enter_context(nc.sbuf_tensor("XN", [128, 2 * D], BF16))
    LNS = pA.enter_context(nc.sbuf_tensor("LNS", [128, 10], F32))
    TPS0 = pA.enter_context(nc.psum_tensor("TPS0", [128, 1024], BF16))
    TPS1 = pA.enter_context(nc.psum_tensor("TPS1", [128, 1024], BF16))
    if True:
        TPS = (TPS0, TPS1)
        ld0 = ld_sy.n

        with nc.Block("blkA") as blk:
            pe1 = sem("pe1")
            act1 = sem("act1")
            act1r = sem("act1r")
            dve1v = sem("dve1v")
            dve1n = sem("dve1n")
            dve1e = sem("dve1e")
            xs = (sem("xs0"), sem("xs1"))

            @blk.sync
            def _(sy):
                for buf, src in ((SG, sg_in), (SU, su_in), (SD, sd_in),
                                 (IDB, idb_in), (IDF, idf_in)):
                    sy.dma_start(out=buf[:, :], in_=src[:, :]).then_inc(ld_sy.s, 16)
                    ld_sy.n += 16
                for t in range(NT):
                    if t >= 2:
                        sy.wait_ge(dve1n.s, t - 1)
                        sy.wait_ge(act1.s, t - 1)
                    sy.dma_start(
                        out=XT[:, (t % 2) * D : (t % 2 + 1) * D],
                        in_=x_own[128 * t : 128 * (t + 1), :],
                    ).then_inc(xs[t % 2].s, 16)
                    xs[t % 2].n += 16

            @blk.gpsimd
            def _(gp):
                gp.dma_start(
                    out=WALL[:, :],
                    in_=AP(w_all, 0, [[384, 128], [128 * 384, MCn], [1, 384]]),
                ).then_inc(ld_gp.s, 16)
                ld_gp.n += 16

            @blk.scalar
            def _(sc):
                for t in range(NT):
                    p = t % 2
                    sc.wait_ge(xs[p].s, 16 * (t // 2 + 1))
                    if t >= 2:
                        sc.wait_ge(dve1v.s, t - 1)
                    sc.activation(
                        out=XSQ[:, p * D : (p + 1) * D],
                        in_=XT[:, p * D : (p + 1) * D], func=AF.Square,
                    ).then_inc(act1.s, 1)
                    act1.n += 1
                    sc.wait_ge(dve1v.s, t + 1)
                    SS(sc, sc.activation(out=LNS[:, 6 + p : 7 + p],
                                         in_=LNS[:, 4 + p : 5 + p], func=AF.Ln))
                    sc.activation(out=LNS[:, 6 + p : 7 + p],
                                  in_=LNS[:, 6 + p : 7 + p],
                                  func=AF.Exp, scale=-0.5).then_inc(act1r.s, 1)
                    act1r.n += 1

            @blk.vector
            def _(v):
                for t in range(NT):
                    p = t % 2
                    v.wait_ge(xs[p].s, 16 * (t // 2 + 1))
                    if t >= 2:
                        v.wait_ge(act1r.s, t - 1)
                    VS(v, v.reduce_sum(LNS[:, p : p + 1],
                                       XT[:, p * D : (p + 1) * D], axis=AX.X))
                    VS(v, v.tensor_scalar_mul(LNS[:, p : p + 1],
                                              LNS[:, p : p + 1], 1.0 / D))
                    v.wait_ge(act1.s, t + 1)
                    VS(v, v.reduce_sum(LNS[:, 2 + p : 3 + p],
                                       XSQ[:, p * D : (p + 1) * D], axis=AX.X))
                    VS(v, v.tensor_scalar_mul(LNS[:, 2 + p : 3 + p],
                                              LNS[:, 2 + p : 3 + p], 1.0 / D))
                    VS(v, v.tensor_mul(LNS[:, 4 + p : 5 + p], LNS[:, p : p + 1],
                                       LNS[:, p : p + 1]))
                    VS(v, v.tensor_sub(LNS[:, 4 + p : 5 + p],
                                       LNS[:, 2 + p : 3 + p],
                                       LNS[:, 4 + p : 5 + p]))
                    v.tensor_scalar_add(LNS[:, 4 + p : 5 + p],
                                        LNS[:, 4 + p : 5 + p],
                                        EPS).then_inc(dve1v.s, 1)
                    dve1v.n += 1
                    v.wait_ge(act1r.s, t + 1)
                    VS(v, v.scalar_tensor_tensor(
                        LNS[:, 8 + p : 9 + p], LNS[:, p : p + 1], -1.0,
                        LNS[:, 6 + p : 7 + p], ALU.mult, ALU.mult))
                    if t >= 2:
                        v.wait_ge(pe1.s, 16 * (t - 1))
                    v.tensor_scalar(
                        XN[:, p * D : (p + 1) * D],
                        XT[:, p * D : (p + 1) * D],
                        LNS[:, 6 + p : 7 + p], LNS[:, 8 + p : 9 + p],
                        ALU.mult, ALU.add,
                    ).then_inc(dve1n.s, 1)
                    dve1n.n += 1
                    for j in range(MCn):
                        v.wait_ge(pe1.s, 16 * t + j + 1)
                        v.tensor_copy(
                            xn_fm[:, TOK * j + 128 * t : TOK * j + 128 * (t + 1)],
                            TPS[j % 2][:, 0:128],
                        ).then_inc(dve1e.s, 1)
                        dve1e.n += 1

            @blk.tensor
            def _(te):
                te.wait_ge(ld_sy.s, ld0 + 5 * 16)
                for t in range(NT):
                    p = t % 2
                    te.wait_ge(dve1n.s, t + 1)
                    for j in range(MCn):
                        k = 16 * t + j
                        if k >= 2:
                            te.wait_ge(dve1e.s, k - 1)
                        te.transpose(
                            TPS[j % 2][:, 0:128],
                            XN[:, p * D + 128 * j : p * D + 128 * (j + 1)],
                            IDB[:, :],
                        ).then_inc(pe1.s, 1)
                        pe1.n += 1

    pA.close()
    pB = ExitStack()
    fuse = pB.enter_context(nc.sbuf_tensor("fuse", [128, ICn * 512], BF16))
    WGT = pB.enter_context(nc.sbuf_tensor("WGT", [128, 2 * MCn * 128], BF16))
    WUT = pB.enter_context(nc.sbuf_tensor("WUT", [128, 2 * MCn * 128], BF16))
    GS = pB.enter_context(nc.sbuf_tensor("GS", [128, 2 * 512], F32))
    US = pB.enter_context(nc.sbuf_tensor("US", [128, 2 * 512], F32))
    GATE = pB.enter_context(nc.sbuf_tensor("GATE", [128, 2 * 512], BF16))
    WDT = pB.enter_context(nc.sbuf_tensor("WDT", [128, 2 * ICn * 128], BF16))
    DSC = pB.enter_context(nc.sbuf_tensor("DSC", [128, 2 * 512], F32))
    GPS0 = pB.enter_context(nc.psum_tensor("GPS0", [128, 512], F32))
    GPS1 = pB.enter_context(nc.psum_tensor("GPS1", [128, 512], F32))
    UPS0 = pB.enter_context(nc.psum_tensor("UPS0", [128, 512], F32))
    UPS1 = pB.enter_context(nc.psum_tensor("UPS1", [128, 512], F32))
    DPS0 = pB.enter_context(nc.psum_tensor("DPS0", [128, 512], F32))
    DPS1 = pB.enter_context(nc.psum_tensor("DPS1", [128, 512], F32))
    DPS2 = pB.enter_context(nc.psum_tensor("DPS2", [128, 512], F32))
    DPS3 = pB.enter_context(nc.psum_tensor("DPS3", [128, 512], F32))
    if True:
        GPS = (GPS0, GPS1)
        UPS = (UPS0, UPS1)
        DPS = (DPS0, DPS1, DPS2, DPS3)

        st0 = st_sy.n
        gp_final = ld_gp.n

        with nc.Block("blkB") as blk:
            ws = (sem("ws0"), sem("ws1"))
            ds = (sem("ds0"), sem("ds1"))
            pe2 = sem("pe2")
            dve2a = sem("dve2a")
            act2 = sem("act2")
            dve2b = sem("dve2b")
            pe2d = sem("pe2d")
            dve2d = sem("dve2d")
            act2d = sem("act2d")

            @blk.sync
            def _(sy):
                for mc in range(MCn):
                    sy.dma_start(
                        out=xn_own_d[128 * mc : 128 * (mc + 1), :],
                        in_=xn_fm[:, TOK * mc : TOK * (mc + 1)],
                    ).then_inc(st_sy.s, 16)
                    st_sy.n += 16
                if "t_xn" in tap_t:
                    sy.dma_start(out=tap_t["t_xn"][:, :],
                                 in_=xn_fm[:, :]).then_inc(st_tap.s, 16)
                    st_tap.n += 16

            @blk.gpsimd
            def _(gp):
                first = True
                for tb in range(2):
                    for ic in range(ICn):
                        k = tb * ICn + ic
                        if k >= 2:
                            gp.wait_ge(pe2.s, k - 1)
                        sl = k % 2
                        gp.dma_start(
                            out=WGT[:, sl * MCn * 128 : (sl + 1) * MCn * 128],
                            in_=wg_t.ap()[ic].rearrange("p m j -> p (m j)"),
                        ).then_inc(ws[sl].s, 16)
                        gp.dma_start(
                            out=WUT[:, sl * MCn * 128 : (sl + 1) * MCn * 128],
                            in_=wu_t.ap()[ic].rearrange("p m j -> p (m j)"),
                        ).then_inc(ws[sl].s, 16)
                        ws[sl].n += 32
                        if first:
                            first = False
                            import os as _os
                            gp.wait_ge(st_sy.s, st0 + 16 * MCn)
                            if _os.environ.get("C_NO_CC"):
                                gp.sem_inc(cc.s, 1)
                            else:
                                gp.collective_compute(
                                    "AllGather", ALU.bypass,
                                    replica_groups=[list(range(NCORES))],
                                    ins=[xn_own_d.ap().opt()],
                                    outs=[xn_full_d.ap().opt()],
                                ).then_inc(cc.s, 1)
                            cc.n += 1
                            if _os.environ.get("SERIAL_CC"):
                                gp.wait_ge(cc.s, 1)
                    for oc in range(OCn):
                        k = tb * OCn + oc
                        if k >= 2:
                            gp.wait_ge(pe2d.s, k - 1)
                        gp.dma_start(
                            out=WDT[:, (k % 2) * ICn * 128 : (k % 2 + 1) * ICn * 128],
                            in_=wd_t.ap()[oc].rearrange("p i j -> p (i j)"),
                        ).then_inc(ds[k % 2].s, 16)
                        ds[k % 2].n += 16

            @blk.tensor
            def _(te):
                for tb in range(2):
                    for ic in range(ICn):
                        k = tb * ICn + ic
                        sl = k % 2
                        te.wait_ge(ws[sl].s, 32 * (k // 2 + 1))
                        if k >= 2:
                            te.wait_ge(dve2a.s, k - 1)
                        woff = sl * MCn * 128
                        for mc in range(MCn):
                            te.matmul(
                                GPS[sl][:, :],
                                WGT[:, woff + 128 * mc : woff + 128 * (mc + 1)],
                                xn_fm[:, TOK * mc + 512 * tb : TOK * mc + 512 * (tb + 1)],
                                start=(mc == 0), stop=(mc == MCn - 1),
                            )
                        for mc in range(MCn):
                            ins = te.matmul(
                                UPS[sl][:, :],
                                WUT[:, woff + 128 * mc : woff + 128 * (mc + 1)],
                                xn_fm[:, TOK * mc + 512 * tb : TOK * mc + 512 * (tb + 1)],
                                start=(mc == 0), stop=(mc == MCn - 1),
                            )
                        ins.then_inc(pe2.s, 1)
                        pe2.n += 1
                    for oc in range(OCn):
                        k = tb * OCn + oc
                        te.wait_ge(ds[k % 2].s, 16 * (k // 2 + 1))
                        if oc == 0:
                            te.wait_ge(dve2b.s, ICn * (tb + 1))
                        if k >= 4:
                            te.wait_ge(dve2d.s, k - 3)
                        for ic in range(ICn):
                            ins = te.matmul(
                                DPS[k % 4][:, :],
                                WDT[:, (k % 2) * ICn * 128 + 128 * ic :
                                    (k % 2) * ICn * 128 + 128 * (ic + 1)],
                                fuse[:, 512 * ic : 512 * (ic + 1)],
                                start=(ic == 0), stop=(ic == ICn - 1),
                            )
                        ins.then_inc(pe2d.s, 1)
                        pe2d.n += 1

            @blk.vector
            def _(v):
                for tb in range(2):
                    for ic in range(ICn):
                        k = tb * ICn + ic
                        sl = k % 2
                        if ic > 0:
                            v.wait_ge(act2.s, k)
                            v.tensor_mul(
                                fuse[:, 512 * (ic - 1) : 512 * ic],
                                GATE[:, 512 * ((k - 1) % 2) : 512 * ((k - 1) % 2 + 1)],
                                US[:, 512 * ((k - 1) % 2) : 512 * ((k - 1) % 2 + 1)],
                            ).then_inc(dve2b.s, 1)
                            dve2b.n += 1
                        v.wait_ge(pe2.s, k + 1)
                        if ic == 0 and tb == 1:
                            v.wait_ge(pe2d.s, OCn)
                        v.tensor_scalar(
                            GS[:, 512 * sl : 512 * (sl + 1)],
                            GPS[sl][:, :], SG[:, ic : ic + 1], None, ALU.mult)
                        v.tensor_scalar(
                            US[:, 512 * sl : 512 * (sl + 1)],
                            UPS[sl][:, :], SU[:, ic : ic + 1], None,
                            ALU.mult).then_inc(dve2a.s, 1)
                        dve2a.n += 1
                    k = tb * ICn + ICn - 1
                    v.wait_ge(act2.s, k + 1)
                    v.tensor_mul(
                        fuse[:, 512 * (ICn - 1) : 512 * ICn],
                        GATE[:, 512 * (k % 2) : 512 * (k % 2 + 1)],
                        US[:, 512 * (k % 2) : 512 * (k % 2 + 1)],
                    ).then_inc(dve2b.s, 1)
                    dve2b.n += 1
                    for oc in range(OCn):
                        k = tb * OCn + oc
                        v.wait_ge(pe2d.s, k + 1)
                        if k >= 2:
                            v.wait_ge(act2d.s, k - 1)
                        v.tensor_scalar(
                            DSC[:, 512 * (k % 2) : 512 * (k % 2 + 1)],
                            DPS[k % 4][:, :], SD[:, oc : oc + 1], None,
                            ALU.mult).then_inc(dve2d.s, 1)
                        dve2d.n += 1

            @blk.scalar
            def _(sc):
                for tb in range(2):
                    for ic in range(ICn):
                        k = tb * ICn + ic
                        sl = k % 2
                        sc.wait_ge(dve2a.s, k + 1)
                        if k >= 2:
                            sc.wait_ge(dve2b.s, k - 1)
                        sc.activation(
                            out=GATE[:, 512 * sl : 512 * (sl + 1)],
                            in_=GS[:, 512 * sl : 512 * (sl + 1)],
                            func=AF.Silu).then_inc(act2.s, 1)
                        act2.n += 1
                    for oc in range(OCn):
                        k = tb * OCn + oc
                        sc.wait_ge(dve2d.s, k + 1)
                        sc.activation(
                            out=g_fm[:, TOK * oc + 512 * tb : TOK * oc + 512 * (tb + 1)],
                            in_=DSC[:, 512 * (k % 2) : 512 * (k % 2 + 1)],
                            func=AF.Silu).then_inc(act2d.s, 1)
                        act2d.n += 1

        if "t_g" in tap_t:
            with nc.Block("blkBtap") as blk:
                @blk.sync
                def _(sy):
                    sy.dma_start(out=tap_t["t_g"][:, :],
                                 in_=g_fm[:, :]).then_inc(st_tap.s, 16)
                    st_tap.n += 16

    pB.close()
    xn_stack.close()

    if max_phase <= 2:
        pF0 = ExitStack()
        _fin(nc, pF0, st_sy)
        pF0.close()
        st.close()
        return nc

    qS = ExitStack()
    QhatA = qS.enter_context(nc.sbuf_tensor("QhatA", [128, T], BF16))
    QhatB = qS.enter_context(nc.sbuf_tensor("QhatB", [128, T], BF16))
    Khat = qS.enter_context(nc.sbuf_tensor("Khat", [128, T], BF16))
    VT = qS.enter_context(nc.sbuf_tensor("VT", [128, NCH * 65], BF16))
    ctx_tm = qS.enter_context(nc.sbuf_tensor("ctx_tm", [128, NCH * 256], BF16))
    if True:
        pC = ExitStack()
        RC = pC.enter_context(nc.sbuf_tensor("RC", [128, NCH * 32], F32))
        RS_ = pC.enter_context(nc.sbuf_tensor("RS_", [128, NCH * 32], F32))
        SLAB = pC.enter_context(nc.sbuf_tensor("SLAB", [128, 2 * MCn * 512], BF16))
        QTM = pC.enter_context(nc.sbuf_tensor("QTM", [128, 2 * 256], BF16))
        KTM = pC.enter_context(nc.sbuf_tensor("KTM", [128, 2 * 128], BF16))
        TT1 = pC.enter_context(nc.sbuf_tensor("TT1", [128, 320], F32))
        TT2 = pC.enter_context(nc.sbuf_tensor("TT2", [128, 320], F32))
        TT3 = pC.enter_context(nc.sbuf_tensor("TT3", [128, 320], F32))
        TT4 = pC.enter_context(nc.sbuf_tensor("TT4", [128, 320], F32))
        SQS = pC.enter_context(nc.sbuf_tensor("SQS", [128, 2 * 256], F32))
        SQK = pC.enter_context(nc.sbuf_tensor("SQK", [128, 2 * 64], F32))
        QN = pC.enter_context(nc.sbuf_tensor("QN", [128, NCH * 4], F32))
        KN = pC.enter_context(nc.sbuf_tensor("KN", [128, NCH], F32))
        MS = pC.enter_context(nc.sbuf_tensor("MS", [128, 1], F32))
        MV = pC.enter_context(nc.sbuf_tensor("MV", [1, 8], F32))
        ONES1 = pC.enter_context(nc.sbuf_tensor("ONES1", [1, 128], F32))
        QKVP0 = pC.enter_context(nc.psum_tensor("QKVP0", [128, 512], F32))
        QKVP1 = pC.enter_context(nc.psum_tensor("QKVP1", [128, 512], F32))
        TPS30 = pC.enter_context(nc.psum_tensor("TPS30", [128, 1024], BF16))
        TPS31 = pC.enter_context(nc.psum_tensor("TPS31", [128, 1024], BF16))
        TPS32 = pC.enter_context(nc.psum_tensor("TPS32", [128, 1024], BF16))
        PSMF = pC.enter_context(nc.psum_tensor("PSMF", [128, 512], F32))
        if True:
            QKVP = (QKVP0, QKVP1)
            TPS3 = (TPS30, TPS31, TPS32)
            NHS = 16
            ld1 = ld_sy.n

            with nc.Block("blkC") as blk:
                ini3 = sem("ini3")
                ssl = (sem("ssl0"), sem("ssl1"))
                pe3 = sem("pe3")
                dve3r = sem("dve3r")
                act3v = sem("act3v")
                dve3n = sem("dve3n")
                pe3t = sem("pe3t")
                dve3e = sem("dve3e")
                mseq = sem("mseq")

                @blk.sync
                def _(sy):
                    sy.dma_start(out=RC[:, :],
                                 in_=rc_in[:, :]).then_inc(ld_sy.s, 16)
                    sy.dma_start(out=RS_[:, :],
                                 in_=rs_in[:, :]).then_inc(ld_sy.s, 16)
                    ld_sy.n += 32
                    sy.wait_ge(cc.s, 1)
                    for hs in range(NHS):
                        r, hh = hs // 2, hs % 2
                        if hs >= 2:
                            sy.wait_ge(pe3.s, 4 * (hs - 1))
                        for q in range(4):
                            sy.dma_start(
                                out=SLAB[:, (hs % 2) * MCn * 512 + 2048 * q :
                                     (hs % 2) * MCn * 512 + 2048 * (q + 1)],
                                in_=AP(xn_full_d,
                                       (2048 * r + 512 * q) * TOK + 512 * hh,
                                       [[TOK, 128], [128 * TOK, 4], [1, 512]]),
                            ).then_inc(ssl[hs % 2].s, 16)
                            ssl[hs % 2].n += 16

                @blk.tensor
                def _(te):
                    te.wait_ge(ld_gp.s, gp_final)
                    for g in range(NCH):
                        hs, tc, pp = g // 4, g % 4, g % 2
                        te.wait_ge(ssl[hs % 2].s, 64 * (hs // 2 + 1))
                        if g >= 2:
                            te.wait_ge(dve3r.s, g - 1)
                            if C_LVL >= 2:
                                te.wait_ge(act3v.s, 2 * (g - 2) + 1)
                        for mc in range(MCn):
                            ins = te.matmul(
                                QKVP[pp][:, 0:384],
                                SLAB[:, (hs % 2) * MCn * 512 + 512 * mc + 128 * tc :
                                     (hs % 2) * MCn * 512 + 512 * mc + 128 * (tc + 1)],
                                WALL[:, 384 * mc : 384 * (mc + 1)],
                                start=(mc == 0), stop=(mc == MCn - 1),
                            )
                        ins.then_inc(pe3.s, 1)
                        pe3.n += 1
                        if g >= 1:
                            gg = g - 1
                            te.wait_ge(dve3r.s, gg + 1)
                            if gg >= 1:
                                te.wait_ge(dve3e.s, 3 * gg)
                            qb = (gg % 2) * 256
                            kb = (gg % 2) * 128
                            te.transpose(TPS3[0][:, 0:128],
                                         QTM[:, qb : qb + 128],
                                         IDB[:, :]).then_inc(pe3t.s, 1)
                            te.transpose(TPS3[1][:, 0:128],
                                         QTM[:, qb + 128 : qb + 256],
                                         IDB[:, :]).then_inc(pe3t.s, 1)
                            te.transpose(TPS3[2][:, 0:128],
                                         KTM[:, kb : kb + 128],
                                         IDB[:, :]).then_inc(pe3t.s, 1)
                            pe3t.n += 3
                    gg = NCH - 1
                    te.wait_ge(dve3r.s, gg + 1)
                    te.wait_ge(dve3e.s, 3 * gg)
                    qb = (gg % 2) * 256
                    kb = (gg % 2) * 128
                    te.transpose(TPS3[0][:, 0:128], QTM[:, qb : qb + 128],
                                 IDB[:, :]).then_inc(pe3t.s, 1)
                    te.transpose(TPS3[1][:, 0:128], QTM[:, qb + 128 : qb + 256],
                                 IDB[:, :]).then_inc(pe3t.s, 1)
                    te.transpose(TPS3[2][:, 0:128], KTM[:, kb : kb + 128],
                                 IDB[:, :]).then_inc(pe3t.s, 1)
                    pe3t.n += 3
                    if C_LVL < 5 or _os.environ.get("C_NO_M"):
                        return
                    te.wait_ge(mseq.s, 1)
                    te.transpose(PSMF[0:1, 0:128], MS[:, 0:1], IDF[:, :]
                                 ).then_inc(mseq.s, 1)
                    te.wait_ge(mseq.s, 3)
                    te.transpose(PSMF[0:1, 0:128], MS[:, 0:1], IDF[:, :]
                                 ).then_inc(mseq.s, 1)
                    te.wait_ge(mseq.s, 7)
                    te.matmul(PSMF[:, 0:1], ONES1[0:1, :], MV[0:1, 4:5],
                              start=True, stop=True).then_inc(mseq.s, 1)
                    mseq.n = 8

                @blk.vector
                def _(v):
                    v.memset(VT[:, :], 1.0)
                    v.memset(ONES1[0:1, :], 1.0).then_inc(ini3.s, 1)
                    ini3.n += 1
                    v.wait_ge(ld_sy.s, ld1 + 32)
                    for g in range(NCH):
                        pp = g % 2
                        fh = AP(QKVP[pp], 0, [[512, 128], [64, 4], [1, 32]])
                        sh = AP(QKVP[pp], 32, [[512, 128], [64, 4], [1, 32]])
                        C4 = AP(RC, 32 * g, [[NCH * 32, 128], [0, 4], [1, 32]])
                        S4 = AP(RS_, 32 * g, [[NCH * 32, 128], [0, 4], [1, 32]])
                        qoff = (g % 2) * 256
                        qfh = AP(QTM, qoff, [[2 * 256, 128], [64, 4], [1, 32]])
                        qsh = AP(QTM, qoff + 32, [[2 * 256, 128], [64, 4], [1, 32]])
                        v.wait_ge(pe3.s, g + 1)
                        if g >= 2:
                            v.wait_ge(pe3t.s, 3 * (g - 1))
                            if C_LVL >= 2:
                                v.wait_ge(act3v.s, 2 * (g - 2) + 2)
                        kb = (g % 2) * 128
                        if C_LVL < 3:
                            v.tensor_copy(QTM[:, qoff : qoff + 256],
                                          QKVP[pp][:, 0:256])
                            v.tensor_copy(KTM[:, kb : kb + 128],
                                          QKVP[pp][:, 256:384]).then_inc(dve3r.s, 1)
                            dve3r.n += 1
                            if C_LVL < 4:
                                v.tensor_copy(KN[:, g : g + 1],
                                              QN[:, 0:1]).then_inc(dve3n.s, 1)
                                dve3n.n += 1
                            _skip_rope = True
                        else:
                            _skip_rope = False
                        if not _skip_rope:
                            kfh = AP(QKVP[pp], 256, [[512, 128], [1, 32]])
                            ksh = AP(QKVP[pp], 288, [[512, 128], [1, 32]])
                            C1 = AP(RC, 32 * g, [[NCH * 32, 128], [1, 32]])
                            S1 = AP(RS_, 32 * g, [[NCH * 32, 128], [1, 32]])
                            qo = (g % 2) * 160
                            ko = qo + 128
                            v.tensor_mul(TT1[:, qo : qo + 128], fh, C4)
                            v.tensor_mul(TT2[:, qo : qo + 128], sh, S4)
                            v.tensor_mul(TT3[:, qo : qo + 128], sh, C4)
                            VS(v, v.tensor_mul(TT4[:, qo : qo + 128], fh, S4))
                            v.tensor_sub(qfh, TT1[:, qo : qo + 128],
                                         TT2[:, qo : qo + 128])
                            v.tensor_add(qsh, TT3[:, qo : qo + 128],
                                         TT4[:, qo : qo + 128])
                            v.tensor_mul(TT1[:, ko : ko + 32], kfh, C1)
                            v.tensor_mul(TT2[:, ko : ko + 32], ksh, S1)
                            v.tensor_mul(TT3[:, ko : ko + 32], ksh, C1)
                            VS(v, v.tensor_mul(TT4[:, ko : ko + 32], kfh, S1))
                            v.tensor_sub(KTM[:, kb : kb + 32],
                                         TT1[:, ko : ko + 32],
                                         TT2[:, ko : ko + 32])
                            VS(v, v.tensor_add(KTM[:, kb + 32 : kb + 64],
                                               TT3[:, ko : ko + 32],
                                               TT4[:, ko : ko + 32]))
                            v.tensor_copy(KTM[:, kb + 64 : kb + 128],
                                          KTM[:, kb : kb + 64]).then_inc(dve3r.s, 1)
                            dve3r.n += 1
                        if C_LVL >= 4:
                            v.wait_ge(act3v.s, 2 * g + 2)
                            v.reduce_sum(
                                QN[:, 4 * g : 4 * (g + 1)],
                                AP(SQS, (g % 2) * 256,
                                   [[2 * 256, 128], [64, 4], [1, 64]]),
                                axis=AX.X)
                            _ins = v.reduce_sum(
                                KN[:, g : g + 1],
                                AP(SQK, (g % 2) * 64, [[2 * 64, 128], [1, 64]]),
                                axis=AX.X)
                            if g == NCH - 1:
                                VS(v, _ins)
                            else:
                                _ins.then_inc(dve3n.s, 1)
                                dve3n.n += 1
                        if g >= 1:
                            gg = g - 1
                            v.wait_ge(pe3t.s, 3 * gg + 3)
                            v.tensor_copy(QhatA[:, 128 * gg : 128 * (gg + 1)],
                                          TPS3[0][:, 0:128]).then_inc(dve3e.s, 1)
                            v.tensor_copy(QhatB[:, 128 * gg : 128 * (gg + 1)],
                                          TPS3[1][:, 0:128]).then_inc(dve3e.s, 1)
                            v.tensor_copy(Khat[:, 128 * gg : 128 * (gg + 1)],
                                          TPS3[2][:, 0:128]).then_inc(dve3e.s, 1)
                            dve3e.n += 3
                    gg = NCH - 1
                    v.wait_ge(pe3t.s, 3 * gg + 3)
                    v.tensor_copy(QhatA[:, 128 * gg : 128 * (gg + 1)],
                                  TPS3[0][:, 0:128]).then_inc(dve3e.s, 1)
                    v.tensor_copy(QhatB[:, 128 * gg : 128 * (gg + 1)],
                                  TPS3[1][:, 0:128]).then_inc(dve3e.s, 1)
                    v.tensor_copy(Khat[:, 128 * gg : 128 * (gg + 1)],
                                  TPS3[2][:, 0:128]).then_inc(dve3e.s, 1)
                    dve3e.n += 3
                    if C_LVL < 5 or _os.environ.get("C_NO_M"):
                        v.memset(NMB[:, 0:1], -10.0)
                        return
                    v.reduce_max(MS[:, 0:1], QN[:, :],
                                 axis=AX.X).then_inc(mseq.s, 1)
                    v.wait_ge(mseq.s, 2)
                    VS(v, v.reduce_max(MV[0:1, 0:1], PSMF[0:1, 0:128],
                                       axis=AX.X))
                    v.reduce_max(MS[:, 0:1], KN[:, :],
                                 axis=AX.X).then_inc(mseq.s, 1)
                    v.wait_ge(mseq.s, 4)
                    VS(v, v.reduce_max(MV[0:1, 1:2], PSMF[0:1, 0:128],
                                       axis=AX.X))
                    v.tensor_mul(MV[0:1, 2:3], MV[0:1, 0:1],
                                 MV[0:1, 1:2]).then_inc(mseq.s, 1)
                    v.wait_ge(mseq.s, 6)
                    v.tensor_scalar_mul(MV[0:1, 4:5], MV[0:1, 3:4],
                                        -0.125).then_inc(mseq.s, 1)
                    v.wait_ge(mseq.s, 8)
                    v.tensor_copy(NMB[:, 0:1], PSMF[:, 0:1])

                @blk.scalar
                def _(sc):
                    if C_LVL < 2:
                        return
                    sc.wait_ge(ini3.s, 1)
                    for g in range(NCH):
                        pp = g % 2
                        sc.wait_ge(pe3.s, g + 1)
                        if g >= 2 and C_LVL >= 4:
                            sc.wait_ge(dve3n.s, g - 1)
                        C_SUB = _os.environ.get("C_SUB", "")
                        if C_SUB == "ps1":
                            sc.activation(out=SQK[:, 0:64],
                                          in_=AP(QKVP[pp], 320, [[512, 128], [1, 64]]),
                                          func=AF.Copy).then_inc(act3v.s, 1)
                            sc.wait_ge(dve3r.s, g + 1)
                            sc.activation(out=SQK[0:1, 0:1], in_=SQK[0:1, 1:2],
                                          func=AF.Copy).then_inc(act3v.s, 1)
                            act3v.n += 2
                            continue
                        if C_SUB == "sb1":
                            sc.activation(out=SQK[:, 0:64],
                                          in_=QTM[:, (g % 2) * 256 : (g % 2) * 256 + 64],
                                          func=AF.Copy).then_inc(act3v.s, 1)
                            sc.wait_ge(dve3r.s, g + 1)
                            sc.activation(out=SQK[0:1, 0:1], in_=SQK[0:1, 1:2],
                                          func=AF.Copy).then_inc(act3v.s, 1)
                            act3v.n += 2
                            continue
                        if C_SUB == "min":
                            sc.activation(out=SQK[0:1, 0:1], in_=SQK[0:1, 1:2],
                                          func=AF.Copy).then_inc(act3v.s, 1)
                            sc.wait_ge(dve3r.s, g + 1)
                            sc.activation(out=SQK[0:1, 0:1], in_=SQK[0:1, 1:2],
                                          func=AF.Copy).then_inc(act3v.s, 1)
                            act3v.n += 2
                            continue
                        sc.wait_ge(dve3r.s, g + 1)
                        if C_SUB != "sq":
                            sc.activation(
                                out=AP(VT, 65 * g, [[NCH * 65, 128], [1, 64]]),
                                in_=AP(QKVP[pp], 320, [[512, 128], [1, 64]]),
                                func=AF.Copy).then_inc(act3v.s, 1)
                        else:
                            sc.activation(
                                out=SQK[:, (g % 2) * 64 : (g % 2 + 1) * 64],
                                in_=AP(QKVP[pp], 320, [[512, 128], [1, 64]]),
                                func=AF.Copy).then_inc(act3v.s, 1)
                        if C_SUB != "v":
                            sc.activation(
                                out=SQS[:, (g % 2) * 256 : (g % 2 + 1) * 256],
                                in_=QTM[:, (g % 2) * 256 : (g % 2 + 1) * 256],
                                func=AF.Square)
                            sc.activation(
                                out=SQK[:, (g % 2) * 64 : (g % 2 + 1) * 64],
                                in_=KTM[:, (g % 2) * 128 : (g % 2) * 128 + 64],
                                func=AF.Square).then_inc(act3v.s, 1)
                        else:
                            sc.activation(
                                out=SQK[:, (g % 2) * 64 : (g % 2 + 1) * 64],
                                in_=KTM[:, (g % 2) * 128 : (g % 2) * 128 + 64],
                                func=AF.Square).then_inc(act3v.s, 1)
                        act3v.n += 2
                    if C_LVL < 5 or _os.environ.get("C_NO_M"):
                        return
                    sc.wait_ge(mseq.s, 5)
                    SS(sc, sc.activation(out=MV[0:1, 3:4], in_=MV[0:1, 2:3],
                                         func=AF.Ln))
                    sc.activation(out=MV[0:1, 3:4], in_=MV[0:1, 3:4],
                                  func=AF.Exp, scale=0.5).then_inc(mseq.s, 1)

            if any(t in tap_t for t in ("t_qa", "t_qb", "t_kh", "t_v", "t_nm")):
                with nc.Block("blkCtap") as blk:
                    @blk.sync
                    def _(sy):
                        for nm, buf in (("t_qa", QhatA), ("t_qb", QhatB),
                                        ("t_kh", Khat), ("t_v", VT),
                                        ("t_nm", NMB)):
                            if nm in tap_t:
                                sy.dma_start(out=tap_t[nm][:, :],
                                             in_=buf[:, :]).then_inc(st_tap.s, 16)
                                st_tap.n += 16

        pC.close()
        if max_phase <= 3:
            qS.close()
            pF0 = ExitStack()
            _fin(nc, pF0, st_sy)
            pF0.close()
            st.close()
            return nc
        Wo2 = qS.enter_context(nc.sbuf_tensor("Wo2", [128, 2 * D], BF16))

        pD = ExitStack()
        MSK = pD.enter_context(nc.sbuf_tensor("MSK", [128, 2 * 1024], F32))
        PT = pD.enter_context(nc.sbuf_tensor("PT", [128, 2 * 1024], BF16))
        SM = pD.enter_context(nc.sbuf_tensor("SM", [128, 2 * 1024], F32))
        INV = pD.enter_context(nc.sbuf_tensor("INV", [128, 8], F32))
        SPS0 = pD.enter_context(nc.psum_tensor("SPS0", [128, 1024], F32))
        SPS1 = pD.enter_context(nc.psum_tensor("SPS1", [128, 1024], F32))
        CPS0 = pD.enter_context(nc.psum_tensor("CPS0", [128, 512], F32))
        CPS1 = pD.enter_context(nc.psum_tensor("CPS1", [128, 512], F32))
        CPS2 = pD.enter_context(nc.psum_tensor("CPS2", [128, 512], F32))
        CPS3 = pD.enter_context(nc.psum_tensor("CPS3", [128, 512], F32))
        if True:
            SPS = (SPS0, SPS1)
            CPS = (CPS0, CPS1, CPS2, CPS3)
            SCOL = (0, 512, 256, 768)
            sched = []
            kbi = 0
            for J in range(NJ):
                for kb in range(2 * J + 2):
                    sched.append((J, kb, kbi))
                    kbi += 1
            ld2 = ld_sy.n
            gp_wo2 = ld_gp.n + 32

            with nc.Block("blkD") as blk:
                pe4s = sem("pe4s")
                act4 = sem("act4")
                pe4a = sem("pe4a")
                dve4m = sem("dve4m")
                dve4c = sem("dve4c")

                @blk.gpsimd
                def _(gp):
                    for cc2 in range(2):
                        gp.dma_start(
                            out=Wo2[:, D * cc2 : D * (cc2 + 1)],
                            in_=wo_in[128 * cc2 : 128 * (cc2 + 1), :],
                        ).then_inc(ld_gp.s, 16)
                        ld_gp.n += 16

                @blk.sync
                def _(sy):
                    sy.dma_start(
                        out=MSK[:, :],
                        in_=AP(mk_in, 0, [[1024, 128], [128 * 1024, 2], [1, 1024]]),
                    ).then_inc(ld_sy.s, 16)
                    ld_sy.n += 16

                @blk.tensor
                def _(te):
                    def av_group(J, kb, i):
                        pp = i % 2
                        te.wait_ge(act4.s, i + 1)
                        for h in range(NH):
                            for qs in range(2):
                                ins = te.matmul(
                                    CPS[2 * (J % 2) + qs][:, 65 * h : 65 * (h + 1)],
                                    PT[:, 1024 * pp + SCOL[h] + 128 * qs :
                                       1024 * pp + SCOL[h] + 128 * (qs + 1)],
                                    VT[:, 65 * kb : 65 * (kb + 1)],
                                    start=(kb == 0 and h == 0),
                                    stop=(kb == 2 * J + 1),
                                    skip_group_check=True,
                                )
                        ins.then_inc(pe4a.s, 1)
                        pe4a.n += 1

                    for (J, kb, i) in sched:
                        p = i % 2
                        if kb == 0 and J >= 2:
                            te.wait_ge(dve4c.s, J - 1)
                        if i >= 2:
                            te.wait_ge(act4.s, i - 1)
                        for h in range(NH):
                            rb = 64 * (h % 2)
                            Q = QhatA if h < 2 else QhatB
                            ins = te.matmul(
                                SPS[p][:, SCOL[h] : SCOL[h] + 256],
                                Khat[rb : rb + 64, 128 * kb : 128 * (kb + 1)],
                                Q[rb : rb + 64, 256 * J : 256 * (J + 1)],
                                start=True, stop=True,
                                tile_position=(rb, 0),
                            )
                        ins.then_inc(pe4s.s, 1)
                        pe4s.n += 1
                        if i >= 1:
                            av_group(*sched[i - 1])
                    av_group(*sched[-1])

                @blk.vector
                def _(v):
                    v.wait_ge(ld_sy.s, ld2 + 16)
                    sm_last_use = [-1, -1]
                    dcnt = 0
                    for (J, kb, i) in sched:
                        p = i % 2
                        if kb >= 2 * J:
                            rel = kb - 2 * J
                            v.wait_ge(pe4s.s, i + 1)
                            if sm_last_use[p] >= 0:
                                v.wait_ge(act4.s, sm_last_use[p] + 1)
                            v.tensor_add(
                                SM[:, 1024 * p : 1024 * (p + 1)],
                                SPS[p][:, :],
                                MSK[:, 1024 * rel : 1024 * (rel + 1)],
                            ).then_inc(dve4m.s, 1)
                            dve4m.n += 1
                            sm_last_use[p] = i
                            dcnt += 1
                        if kb == 2 * J + 1:
                            v.wait_ge(pe4a.s, i + 1)
                            for qs in range(2):
                                cp = CPS[2 * (J % 2) + qs]
                                VS(v, v.reciprocal(
                                    INV[:, 4 * qs : 4 * (qs + 1)],
                                    AP(cp, 64, [[512, 128], [65, 4]])))
                                for h in range(NH):
                                    ins = v.tensor_scalar(
                                        ctx_tm[:, 256 * (2 * J + qs) + 64 * h :
                                               256 * (2 * J + qs) + 64 * (h + 1)],
                                        cp[:, 65 * h : 65 * h + 64],
                                        INV[:, 4 * qs + h : 4 * qs + h + 1],
                                        None, ALU.mult)
                            ins.then_inc(dve4c.s, 1)
                            dve4c.n += 1

                @blk.scalar
                def _(sc):
                    sc.wait_ge(ld_sy.s, ld2 + 16)
                    dcnt = 0
                    for (J, kb, i) in sched:
                        p = i % 2
                        if i >= 2:
                            sc.wait_ge(pe4a.s, i - 1)
                        if kb >= 2 * J:
                            dcnt += 1
                            sc.wait_ge(dve4m.s, dcnt)
                            src = SM[:, 1024 * p : 1024 * (p + 1)]
                        else:
                            sc.wait_ge(pe4s.s, i + 1)
                            src = SPS[p][:, :]
                        sc.activation(
                            out=PT[:, 1024 * p : 1024 * (p + 1)],
                            in_=src, func=AF.Exp,
                            bias=NMB[:, 0:1], scale=0.125,
                        ).then_inc(act4.s, 1)
                        act4.n += 1

            if "t_ctx" in tap_t:
                with nc.Block("blkDtap") as blk:
                    @blk.sync
                    def _(sy):
                        sy.dma_start(out=tap_t["t_ctx"][:, :],
                                     in_=ctx_tm[:, :]).then_inc(st_tap.s, 16)
                        st_tap.n += 16

        pD.close()
        if max_phase <= 4:
            qS.close()
            pF0 = ExitStack()
            _fin(nc, pF0, st_sy)
            pF0.close()
            st.close()
            return nc
        pE = ExitStack()
        CFM = pE.enter_context(nc.sbuf_tensor("CFM", [128, 2 * T], BF16))
        SPB = pE.enter_context(nc.sbuf_tensor("SPB", [128, 2 * D], F32))
        WPS0 = pE.enter_context(nc.psum_tensor("WPS0", [128, 512], F32))
        WPS1 = pE.enter_context(nc.psum_tensor("WPS1", [128, 512], F32))
        WPS2 = pE.enter_context(nc.psum_tensor("WPS2", [128, 512], F32))
        WPS3 = pE.enter_context(nc.psum_tensor("WPS3", [128, 512], F32))
        CTP0 = pE.enter_context(nc.psum_tensor("CTP0", [128, 1024], BF16))
        CTP1 = pE.enter_context(nc.psum_tensor("CTP1", [128, 1024], BF16))
        if True:
            WPS = (WPS0, WPS1, WPS2, WPS3)
            CTP = (CTP0, CTP1)
            st1 = st_sy.n

            with nc.Block("blkE") as blk:
                sts = (sem("sts0"), sem("sts1"))
                pe5t = sem("pe5t")
                dve5e = sem("dve5e")
                pe5w = sem("pe5w")
                ev5v = sem("ev5v")
                ev5a = sem("ev5a")

                eorder = [tc for tc in range(NCH) if tc % 8 < 4] + \
                         [tc for tc in range(NCH) if tc % 8 >= 4]

                def sp_target(tc):
                    cdst, w = tc // 8, tc % 8
                    if w < 4:
                        return sp_a, 512 * cdst + 128 * w
                    return sp_b, 512 * cdst + 128 * (w - 4)

                a_n = [0, 0]
                for idx in range(NCH // 2):
                    a_n[idx % 2] += 4 * 16

                @blk.tensor
                def _(te):
                    te.wait_ge(ld_gp.s, gp_wo2)
                    for idx, tc in enumerate(eorder):
                        for hf in range(2):
                            k = 2 * idx + hf
                            if k >= 2:
                                te.wait_ge(dve5e.s, k - 1)
                            te.transpose(
                                CTP[k % 2][:, 0:128],
                                ctx_tm[:, 256 * tc + 128 * hf :
                                       256 * tc + 128 * (hf + 1)],
                                IDB[:, :]).then_inc(pe5t.s, 1)
                            pe5t.n += 1
                        te.wait_ge(dve5e.s, 2 * idx + 2)
                        for nb in range(4):
                            if idx >= 1:
                                if nb < 2:
                                    te.wait_ge(ev5v.s, 2 * (idx - 1) + nb + 1)
                                else:
                                    te.wait_ge(ev5a.s, 2 * (idx - 1) + nb - 1)
                            for cc2 in range(2):
                                ins = te.matmul(
                                    WPS[nb][:, :],
                                    CFM[:, T * cc2 + 128 * tc : T * cc2 + 128 * (tc + 1)],
                                    Wo2[:, D * cc2 + 512 * nb : D * cc2 + 512 * (nb + 1)],
                                    start=(cc2 == 0), stop=(cc2 == 1),
                                )
                            ins.then_inc(pe5w.s, 1)
                            pe5w.n += 1

                @blk.vector
                def _(v):
                    for idx, tc in enumerate(eorder):
                        for hf in range(2):
                            k = 2 * idx + hf
                            v.wait_ge(pe5t.s, k + 1)
                            v.tensor_copy(
                                CFM[:, T * hf + 128 * tc : T * hf + 128 * (tc + 1)],
                                CTP[k % 2][:, 0:128]).then_inc(dve5e.s, 1)
                            dve5e.n += 1
                        for nb in (0, 1):
                            kk = 4 * idx + nb
                            v.wait_ge(pe5w.s, kk + 1)
                            if idx >= 2:
                                v.wait_ge(sts[idx % 2].s, 64 * (idx // 2))
                            v.tensor_copy(
                                SPB[:, (idx % 2) * D + 512 * nb :
                                    (idx % 2) * D + 512 * (nb + 1)],
                                WPS[nb][:, :]).then_inc(ev5v.s, 1)
                            ev5v.n += 1

                @blk.scalar
                def _(sc):
                    for idx, tc in enumerate(eorder):
                        for nb in (2, 3):
                            kk = 4 * idx + nb
                            sc.wait_ge(pe5w.s, kk + 1)
                            if idx >= 2:
                                sc.wait_ge(sts[idx % 2].s, 64 * (idx // 2))
                            sc.activation(
                                out=SPB[:, (idx % 2) * D + 512 * nb :
                                        (idx % 2) * D + 512 * (nb + 1)],
                                in_=WPS[nb][:, :],
                                func=AF.Copy).then_inc(ev5a.s, 1)
                            ev5a.n += 1

                @blk.sync
                def _(sy):
                    for idx, tc in enumerate(eorder):
                        tgt, row = sp_target(tc)
                        for nb in range(4):
                            if nb < 2:
                                sy.wait_ge(ev5v.s, 2 * idx + nb + 1)
                            else:
                                sy.wait_ge(ev5a.s, 2 * idx + nb - 1)
                            sy.dma_start(
                                out=tgt[row : row + 128,
                                        512 * nb : 512 * (nb + 1)],
                                in_=SPB[:, (idx % 2) * D + 512 * nb :
                                        (idx % 2) * D + 512 * (nb + 1)],
                            ).then_inc(sts[idx % 2].s, 16)
                            sts[idx % 2].n += 16

                @blk.gpsimd
                def _(gp):
                    gp.wait_ge(sts[0].s, a_n[0])
                    gp.wait_ge(sts[1].s, a_n[1])
                    gp.collective_compute(
                        "ReduceScatter", ALU.add,
                        replica_groups=[list(range(NCORES))],
                        ins=[sp_a.ap().opt()],
                        outs=[s_own_a.ap().opt()],
                    ).then_inc(cc.s, 1)
                    cc.n += 1
                    gp.wait_ge(sts[0].s, sts[0].n)
                    gp.wait_ge(sts[1].s, sts[1].n)
                    gp.collective_compute(
                        "ReduceScatter", ALU.add,
                        replica_groups=[list(range(NCORES))],
                        ins=[sp_b.ap().opt()],
                        outs=[s_own_b.ap().opt()],
                    ).then_inc(cc.s, 1)
                    cc.n += 1

    pE.close()
    qS.close()
    if max_phase <= 5:
        pF0 = ExitStack()
        _fin(nc, pF0, st_sy)
        pF0.close()
        st.close()
        return nc
    pF = ExitStack()
    SOB = pF.enter_context(nc.sbuf_tensor("SOB", [128, 2 * D], F32))
    XR = pF.enter_context(nc.sbuf_tensor("XR", [128, 2 * D], F32))
    TGW = pF.enter_context(nc.sbuf_tensor("TGW", [128, D], F32))
    OUTB = pF.enter_context(nc.sbuf_tensor("OUTB", [128, 2 * D], F32))
    CTPF0 = pF.enter_context(nc.psum_tensor("CTPF0", [128, 1024], BF16))
    CTPF1 = pF.enter_context(nc.psum_tensor("CTPF1", [128, 1024], BF16))
    if True:
        CTPF = (CTPF0, CTPF1)
        ld3 = ld_sy.n
        st2 = st_sy.n

        with nc.Block("blkF") as blk:
            pe6 = sem("pe6")
            dve6 = sem("dve6")
            dve6o = sem("dve6o")
            fsl = (sem("fsl0"), sem("fsl1"))
            sto = (sem("sto0"), sem("sto1"))

            @blk.sync
            def _(sy):
                for t in range(NT):
                    if t == 0:
                        sy.wait_ge(cc.s, 2)
                    if t == 4:
                        sy.wait_ge(cc.s, 3)
                    if t >= 2:
                        sy.wait_ge(dve6o.s, 16 * (t - 1))
                    src_d = s_own_a if t < 4 else s_own_b
                    sy.dma_start(
                        out=SOB[:, (t % 2) * D : (t % 2 + 1) * D],
                        in_=src_d[128 * (t % 4) : 128 * (t % 4 + 1), :],
                    ).then_inc(fsl[t % 2].s, 16)
                    sy.dma_start(
                        out=XR[:, (t % 2) * D : (t % 2 + 1) * D],
                        in_=x_own[128 * t : 128 * (t + 1), :],
                    ).then_inc(fsl[t % 2].s, 16)
                    fsl[t % 2].n += 32
                    sy.wait_ge(dve6o.s, 16 * (t + 1))
                    sy.dma_start(
                        out=out_own[128 * t : 128 * (t + 1), :],
                        in_=OUTB[:, (t % 2) * D : (t % 2 + 1) * D],
                    ).then_inc(sto[t % 2].s, 16)
                    sto[t % 2].n += 16
                if "t_sp" in tap_t:
                    sy.dma_start(out=tap_t["t_sp"][:, :],
                                 in_=s_own_d[:, :]).then_inc(st_tap.s, 16)
                    st_tap.n += 16
                sy.wait_ge(sto[0].s, sto[0].n)
                sy.wait_ge(sto[1].s, sto[1].n)
                sy.wait_ge(st_sy.s, st_sy.n)
                if st_tap.n:
                    sy.wait_ge(st_tap.s, st_tap.n)

            @blk.tensor
            def _(te):
                for t in range(NT):
                    for oc in range(OCn):
                        k = OCn * t + oc
                        if k >= 2:
                            te.wait_ge(dve6.s, k - 1)
                        te.transpose(
                            CTPF[k % 2][:, 0:128],
                            g_fm[:, TOK * oc + 128 * t : TOK * oc + 128 * (t + 1)],
                            IDB[:, :]).then_inc(pe6.s, 1)
                        pe6.n += 1

            @blk.vector
            def _(v):
                for t in range(NT):
                    p = t % 2
                    v.wait_ge(fsl[p].s, 32 * (t // 2 + 1))
                    if t >= 2:
                        v.wait_ge(sto[p].s, 16 * (t // 2))
                    for oc in range(OCn):
                        k = OCn * t + oc
                        v.wait_ge(pe6.s, k + 1)
                        ins = v.tensor_mul(
                            TGW[:, 128 * oc : 128 * (oc + 1)],
                            SOB[:, p * D + 128 * oc : p * D + 128 * (oc + 1)],
                            CTPF[k % 2][:, 0:128])
                        ins.then_inc(dve6.s, 1)
                        dve6.n += 1
                    for oc in range(OCn):
                        v.wait_ge(dve6.s, OCn * t + oc + 1)
                        v.tensor_add(
                            OUTB[:, p * D + 128 * oc : p * D + 128 * (oc + 1)],
                            XR[:, p * D + 128 * oc : p * D + 128 * (oc + 1)],
                            TGW[:, 128 * oc : 128 * (oc + 1)]).then_inc(dve6o.s, 1)
                        dve6o.n += 1
                    dve6o.n += 0

    pF.close()
    st.close()
    return nc


def _host_prep(inputs):
    x = np.ascontiguousarray(np.asarray(inputs["L0_attn_x"]).reshape(T, D))
    Wq = np.asarray(inputs["L0_attn_Wq"])
    Wk = np.asarray(inputs["L0_attn_Wk"])
    Wv = np.asarray(inputs["L0_attn_Wv"])
    Wo = np.asarray(inputs["L0_attn_Wo"])
    wg = np.asarray(inputs["L0_ffn_w_gate"])
    wu = np.asarray(inputs["L0_ffn_w_up"])
    wd = np.asarray(inputs["L0_ffn_w_down"])
    sg = np.asarray(inputs["L0_ffn_s_gate"])
    su = np.asarray(inputs["L0_ffn_s_up"])
    sd = np.asarray(inputs["L0_ffn_s_down"])

    wg_t = np.ascontiguousarray(
        wg.reshape(MCn, 128, ICn, 128).transpose(2, 1, 0, 3))
    wu_t = np.ascontiguousarray(
        wu.reshape(MCn, 128, ICn, 128).transpose(2, 1, 0, 3))
    wd_t = np.ascontiguousarray(
        wd.reshape(ICn, 128, OCn, 128).transpose(2, 1, 0, 3))
    sgT = np.ascontiguousarray(sg.reshape(ICn, 128).T)
    suT = np.ascontiguousarray(su.reshape(ICn, 128).T)
    sdT = np.ascontiguousarray(sd.reshape(OCn, 128).T)

    theta = (1.0 / np.power(
        10000.0, np.arange(0, 64, 2, dtype=np.float32) / 64.0)).astype(np.float32)
    pos = np.arange(T, dtype=np.float32)
    ang = np.outer(pos, theta)
    rc = np.cos(ang).astype(np.float32)
    rs = np.sin(ang).astype(np.float32)
    rc_t = np.ascontiguousarray(
        rc.reshape(NCH, 128, 32).transpose(1, 0, 2).reshape(128, NCH * 32))
    rs_t = np.ascontiguousarray(
        rs.reshape(NCH, 128, 32).transpose(1, 0, 2).reshape(128, NCH * 32))

    kk = np.arange(128)[:, None]
    qq = np.arange(256)[None, :]
    mk = np.zeros((2, 128, 1024), dtype=np.float32)
    for rel in range(2):
        m = np.where(128 * rel + kk > qq, NEGBIG, 0.0).astype(np.float32)
        mk[rel] = np.tile(m, (1, NH))

    idb = np.eye(128, dtype=ml_dtypes.bfloat16)
    idf = np.eye(128, dtype=np.float32)

    in_maps = []
    for c in range(NCORES):
        w_all_c = np.ascontiguousarray(np.concatenate(
            [Wq[:, 256 * c : 256 * (c + 1)],
             Wk[:, 64 * c : 64 * (c + 1)],
             Wv[:, 64 * c : 64 * (c + 1)]], axis=1))
        in_maps.append({
            "x_own": np.ascontiguousarray(x[TOK * c : TOK * (c + 1)]),
            "w_all": w_all_c,
            "wo_in": np.ascontiguousarray(Wo[256 * c : 256 * (c + 1), :]),
            "wg_t": wg_t, "wu_t": wu_t, "wd_t": wd_t,
            "sg_in": sgT, "su_in": suT, "sd_in": sdT,
            "rc_in": rc_t, "rs_in": rs_t, "mk_in": mk,
            "idb_in": idb, "idf_in": idf,
        })
    return in_maps


def run(inputs, taps=(), max_phase=9):
    in_maps = _host_prep(inputs)
    nc = build(taps=taps, max_phase=max_phase)
    res = run_bass_kernel_spmd(nc, in_maps, core_ids=list(range(NCORES)))
    return res.results


def kernel(**inputs):
    results = run(inputs)
    out = np.concatenate([results[c]["out_own"] for c in range(NCORES)], axis=0)
    return out.reshape(1, T, D).astype(np.float32)


# version 11
# speedup vs baseline: 1.1253x; 1.0645x over previous
import numpy as np
import ml_dtypes
from contextlib import ExitStack

import concourse.bass as bass
import concourse.mybir as mybir
from concourse.bass_utils import run_bass_kernel_spmd

F32 = mybir.dt.float32
BF16 = mybir.dt.bfloat16
I8 = mybir.dt.int8
F16 = mybir.dt.float16
AF = mybir.ActivationFunctionType
ALU = mybir.AluOpType
AX = mybir.AxisListType
AP = bass.AP

NCORES = 8
T = 8192
D = 2048
TOK = T // NCORES
MCn = 16
ICn = 64
OCn = 16
NH = 4
NJ = 32
NCH = 64
EPS = 1.0e-5
NEGBIG = -1.0e30


def _fin(nc, stack, st_sy):
    with nc.Block("blkFin") as blk:
        @blk.sync
        def _(sy):
            sy.wait_ge(st_sy.s, st_sy.n)


def build(taps=(), max_phase=9):
    import os as _os
    C_LVL = int(_os.environ.get("C_LVL", "9"))
    nc = bass.Bass("TRN2", target_bir_lowering=False)

    x_own = nc.dram_tensor("x_own", [TOK, D], F32, kind="ExternalInput")
    w_all = nc.dram_tensor("w_all", [D, 384], F32, kind="ExternalInput")
    wo_in = nc.dram_tensor("wo_in", [256, D], F32, kind="ExternalInput")
    wg_t = nc.dram_tensor("wg_t", [ICn, 128, MCn, 128], I8, kind="ExternalInput")
    wu_t = nc.dram_tensor("wu_t", [ICn, 128, MCn, 128], I8, kind="ExternalInput")
    wd_t = nc.dram_tensor("wd_t", [OCn, 128, ICn, 128], I8, kind="ExternalInput")
    sg_in = nc.dram_tensor("sg_in", [128, ICn], F32, kind="ExternalInput")
    su_in = nc.dram_tensor("su_in", [128, ICn], F32, kind="ExternalInput")
    sd_in = nc.dram_tensor("sd_in", [128, OCn], F32, kind="ExternalInput")
    rc_in = nc.dram_tensor("rc_in", [128, NCH * 32], F32, kind="ExternalInput")
    rs_in = nc.dram_tensor("rs_in", [128, NCH * 32], F32, kind="ExternalInput")
    mk_in = nc.dram_tensor("mk_in", [2, 128, 1024], F32, kind="ExternalInput")
    idb_in = nc.dram_tensor("idb_in", [128, 128], BF16, kind="ExternalInput")
    idf_in = nc.dram_tensor("idf_in", [128, 128], F32, kind="ExternalInput")

    out_own = nc.dram_tensor("out_own", [TOK, D], F32, kind="ExternalOutput")

    xn_own_d = nc.dram_tensor("xn_own_d", [D, TOK], BF16)
    xn_full_d = nc.dram_tensor(
        "xn_full_d", [NCORES * D, TOK], BF16,
        addr_space="Local" if _os.environ.get("SIM_LOCAL") else "Shared")
    sp_a = nc.dram_tensor("sp_a", [T // 2, D], F16)
    sp_b = nc.dram_tensor("sp_b", [T // 2, D], F16)
    s_own_a = nc.dram_tensor("s_own_a", [TOK // 2, D], F16)
    s_own_b = nc.dram_tensor("s_own_b", [TOK // 2, D], F16)

    tap_t = {}
    for name, shape, dt in (
        ("t_xn", [128, MCn * TOK], BF16),
        ("t_g", [128, OCn * TOK], BF16),
        ("t_qa", [128, T], BF16),
        ("t_qb", [128, T], BF16),
        ("t_kh", [128, T], BF16),
        ("t_v", [128, NCH * 65], BF16),
        ("t_nm", [128, 1], F32),
        ("t_ctx", [128, NCH * 256], BF16),
        ("t_sp", [TOK, D], F32),
    ):
        if name in taps:
            tap_t[name] = nc.dram_tensor(name, shape, dt, kind="ExternalOutput")

    st = ExitStack()

    def sem(name):
        class Ctr:
            pass
        c = Ctr()
        c.s = st.enter_context(nc.semaphore(name))
        c.n = 0
        return c

    ld_sy = sem("ld_sy")
    ld_gp = sem("ld_gp")
    st_sy = sem("st_sy")
    st_tap = sem("st_tap")
    cc = sem("cc")
    vsy = sem("vsy")
    ssy = sem("ssy")

    def VS(eng, ins):
        ins.then_inc(vsy.s, 1)
        vsy.n += 1
        eng.wait_ge(vsy.s, vsy.n)

    def SS(eng, ins):
        ins.then_inc(ssy.s, 1)
        ssy.n += 1
        eng.wait_ge(ssy.s, ssy.n)

    g_fm = st.enter_context(nc.sbuf_tensor("g_fm", [128, OCn * TOK], BF16))
    WALL = st.enter_context(nc.sbuf_tensor("WALL", [128, MCn * 384], BF16))
    SG = st.enter_context(nc.sbuf_tensor("SG", [128, ICn], F32))
    SU = st.enter_context(nc.sbuf_tensor("SU", [128, ICn], F32))
    SD = st.enter_context(nc.sbuf_tensor("SD", [128, OCn], F32))
    IDB = st.enter_context(nc.sbuf_tensor("IDB", [128, 128], BF16))
    IDF = st.enter_context(nc.sbuf_tensor("IDF", [128, 128], F32))
    NMB = st.enter_context(nc.sbuf_tensor("NMB", [128, 1], F32))

    xn_stack = ExitStack()
    xn_fm = xn_stack.enter_context(
        nc.sbuf_tensor("xn_fm", [128, MCn * TOK], BF16))

    NT = TOK // 128
    pA = ExitStack()
    XT = pA.enter_context(nc.sbuf_tensor("XT", [128, 2 * D], F32))
    XSQ = pA.enter_context(nc.sbuf_tensor("XSQ", [128, 2 * D], F32))
    XN = pA.enter_context(nc.sbuf_tensor("XN", [128, 2 * D], BF16))
    LNS = pA.enter_context(nc.sbuf_tensor("LNS", [128, 10], F32))
    TPS0 = pA.enter_context(nc.psum_tensor("TPS0", [128, 1024], BF16))
    TPS1 = pA.enter_context(nc.psum_tensor("TPS1", [128, 1024], BF16))
    if True:
        TPS = (TPS0, TPS1)
        ld0 = ld_sy.n

        with nc.Block("blkA") as blk:
            pe1 = sem("pe1")
            act1 = sem("act1")
            act1r = sem("act1r")
            dve1v = sem("dve1v")
            dve1n = sem("dve1n")
            dve1e = sem("dve1e")
            xs = (sem("xs0"), sem("xs1"))

            @blk.sync
            def _(sy):
                for buf, src in ((SG, sg_in), (SU, su_in), (SD, sd_in),
                                 (IDB, idb_in), (IDF, idf_in)):
                    sy.dma_start(out=buf[:, :], in_=src[:, :]).then_inc(ld_sy.s, 16)
                    ld_sy.n += 16
                for t in range(NT):
                    if t >= 2:
                        sy.wait_ge(dve1n.s, t - 1)
                        sy.wait_ge(act1.s, t - 1)
                    sy.dma_start(
                        out=XT[:, (t % 2) * D : (t % 2 + 1) * D],
                        in_=x_own[128 * t : 128 * (t + 1), :],
                    ).then_inc(xs[t % 2].s, 16)
                    xs[t % 2].n += 16

            @blk.gpsimd
            def _(gp):
                gp.dma_start(
                    out=WALL[:, :],
                    in_=AP(w_all, 0, [[384, 128], [128 * 384, MCn], [1, 384]]),
                ).then_inc(ld_gp.s, 16)
                ld_gp.n += 16

            @blk.scalar
            def _(sc):
                for t in range(NT):
                    p = t % 2
                    sc.wait_ge(xs[p].s, 16 * (t // 2 + 1))
                    if t >= 2:
                        sc.wait_ge(dve1v.s, t - 1)
                    sc.activation(
                        out=XSQ[:, p * D : (p + 1) * D],
                        in_=XT[:, p * D : (p + 1) * D], func=AF.Square,
                    ).then_inc(act1.s, 1)
                    act1.n += 1
                    sc.wait_ge(dve1v.s, t + 1)
                    SS(sc, sc.activation(out=LNS[:, 6 + p : 7 + p],
                                         in_=LNS[:, 4 + p : 5 + p], func=AF.Ln))
                    sc.activation(out=LNS[:, 6 + p : 7 + p],
                                  in_=LNS[:, 6 + p : 7 + p],
                                  func=AF.Exp, scale=-0.5).then_inc(act1r.s, 1)
                    act1r.n += 1

            @blk.vector
            def _(v):
                for t in range(NT):
                    p = t % 2
                    v.wait_ge(xs[p].s, 16 * (t // 2 + 1))
                    if t >= 2:
                        v.wait_ge(act1r.s, t - 1)
                    VS(v, v.reduce_sum(LNS[:, p : p + 1],
                                       XT[:, p * D : (p + 1) * D], axis=AX.X))
                    VS(v, v.tensor_scalar_mul(LNS[:, p : p + 1],
                                              LNS[:, p : p + 1], 1.0 / D))
                    v.wait_ge(act1.s, t + 1)
                    VS(v, v.reduce_sum(LNS[:, 2 + p : 3 + p],
                                       XSQ[:, p * D : (p + 1) * D], axis=AX.X))
                    VS(v, v.tensor_scalar_mul(LNS[:, 2 + p : 3 + p],
                                              LNS[:, 2 + p : 3 + p], 1.0 / D))
                    VS(v, v.tensor_mul(LNS[:, 4 + p : 5 + p], LNS[:, p : p + 1],
                                       LNS[:, p : p + 1]))
                    VS(v, v.tensor_sub(LNS[:, 4 + p : 5 + p],
                                       LNS[:, 2 + p : 3 + p],
                                       LNS[:, 4 + p : 5 + p]))
                    v.tensor_scalar_add(LNS[:, 4 + p : 5 + p],
                                        LNS[:, 4 + p : 5 + p],
                                        EPS).then_inc(dve1v.s, 1)
                    dve1v.n += 1
                    v.wait_ge(act1r.s, t + 1)
                    VS(v, v.scalar_tensor_tensor(
                        LNS[:, 8 + p : 9 + p], LNS[:, p : p + 1], -1.0,
                        LNS[:, 6 + p : 7 + p], ALU.mult, ALU.mult))
                    if t >= 2:
                        v.wait_ge(pe1.s, 16 * (t - 1))
                    v.tensor_scalar(
                        XN[:, p * D : (p + 1) * D],
                        XT[:, p * D : (p + 1) * D],
                        LNS[:, 6 + p : 7 + p], LNS[:, 8 + p : 9 + p],
                        ALU.mult, ALU.add,
                    ).then_inc(dve1n.s, 1)
                    dve1n.n += 1
                    for j in range(MCn):
                        v.wait_ge(pe1.s, 16 * t + j + 1)
                        v.tensor_copy(
                            xn_fm[:, TOK * j + 128 * t : TOK * j + 128 * (t + 1)],
                            TPS[j % 2][:, 0:128],
                        ).then_inc(dve1e.s, 1)
                        dve1e.n += 1

            @blk.tensor
            def _(te):
                te.wait_ge(ld_sy.s, ld0 + 5 * 16)
                for t in range(NT):
                    p = t % 2
                    te.wait_ge(dve1n.s, t + 1)
                    for j in range(MCn):
                        k = 16 * t + j
                        if k >= 2:
                            te.wait_ge(dve1e.s, k - 1)
                        te.transpose(
                            TPS[j % 2][:, 0:128],
                            XN[:, p * D + 128 * j : p * D + 128 * (j + 1)],
                            IDB[:, :],
                        ).then_inc(pe1.s, 1)
                        pe1.n += 1

    pA.close()
    pB = ExitStack()
    fuse = pB.enter_context(nc.sbuf_tensor("fuse", [128, ICn * 512], BF16))
    WGT = pB.enter_context(nc.sbuf_tensor("WGT", [128, 2 * MCn * 128], BF16))
    WUT = pB.enter_context(nc.sbuf_tensor("WUT", [128, 2 * MCn * 128], BF16))
    GS = pB.enter_context(nc.sbuf_tensor("GS", [128, 2 * 512], F32))
    US = pB.enter_context(nc.sbuf_tensor("US", [128, 2 * 512], F32))
    GATE = pB.enter_context(nc.sbuf_tensor("GATE", [128, 2 * 512], BF16))
    WDT = pB.enter_context(nc.sbuf_tensor("WDT", [128, 2 * ICn * 128], BF16))
    DSC = pB.enter_context(nc.sbuf_tensor("DSC", [128, 2 * 512], F32))
    GPS0 = pB.enter_context(nc.psum_tensor("GPS0", [128, 512], F32))
    GPS1 = pB.enter_context(nc.psum_tensor("GPS1", [128, 512], F32))
    UPS0 = pB.enter_context(nc.psum_tensor("UPS0", [128, 512], F32))
    UPS1 = pB.enter_context(nc.psum_tensor("UPS1", [128, 512], F32))
    DPS0 = pB.enter_context(nc.psum_tensor("DPS0", [128, 512], F32))
    DPS1 = pB.enter_context(nc.psum_tensor("DPS1", [128, 512], F32))
    DPS2 = pB.enter_context(nc.psum_tensor("DPS2", [128, 512], F32))
    DPS3 = pB.enter_context(nc.psum_tensor("DPS3", [128, 512], F32))
    if True:
        GPS = (GPS0, GPS1)
        UPS = (UPS0, UPS1)
        DPS = (DPS0, DPS1, DPS2, DPS3)

        st0 = st_sy.n
        gp_final = ld_gp.n

        with nc.Block("blkB") as blk:
            ws = (sem("ws0"), sem("ws1"))
            ds = (sem("ds0"), sem("ds1"))
            pe2 = sem("pe2")
            dve2a = sem("dve2a")
            act2 = sem("act2")
            dve2b = sem("dve2b")
            pe2d = sem("pe2d")
            dve2d = sem("dve2d")
            act2d = sem("act2d")

            @blk.sync
            def _(sy):
                for mc in range(MCn):
                    sy.dma_start(
                        out=xn_own_d[128 * mc : 128 * (mc + 1), :],
                        in_=xn_fm[:, TOK * mc : TOK * (mc + 1)],
                    ).then_inc(st_sy.s, 16)
                    st_sy.n += 16
                if "t_xn" in tap_t:
                    sy.dma_start(out=tap_t["t_xn"][:, :],
                                 in_=xn_fm[:, :]).then_inc(st_tap.s, 16)
                    st_tap.n += 16

            @blk.gpsimd
            def _(gp):
                first = True
                for tb in range(2):
                    for ic in range(ICn):
                        k = tb * ICn + ic
                        if k >= 2:
                            gp.wait_ge(pe2.s, k - 1)
                        sl = k % 2
                        gp.dma_start(
                            out=WGT[:, sl * MCn * 128 : (sl + 1) * MCn * 128],
                            in_=wg_t.ap()[ic].rearrange("p m j -> p (m j)"),
                        ).then_inc(ws[sl].s, 16)
                        gp.dma_start(
                            out=WUT[:, sl * MCn * 128 : (sl + 1) * MCn * 128],
                            in_=wu_t.ap()[ic].rearrange("p m j -> p (m j)"),
                        ).then_inc(ws[sl].s, 16)
                        ws[sl].n += 32
                        if first:
                            first = False
                            import os as _os
                            gp.wait_ge(st_sy.s, st0 + 16 * MCn)
                            if _os.environ.get("C_NO_CC"):
                                gp.sem_inc(cc.s, 1)
                            else:
                                gp.collective_compute(
                                    "AllGather", ALU.bypass,
                                    replica_groups=[list(range(NCORES))],
                                    ins=[xn_own_d.ap().opt()],
                                    outs=[xn_full_d.ap().opt()],
                                ).then_inc(cc.s, 1)
                            cc.n += 1
                            if _os.environ.get("SERIAL_CC"):
                                gp.wait_ge(cc.s, 1)
                    for oc in range(OCn):
                        k = tb * OCn + oc
                        if k >= 2:
                            gp.wait_ge(pe2d.s, k - 1)
                        gp.dma_start(
                            out=WDT[:, (k % 2) * ICn * 128 : (k % 2 + 1) * ICn * 128],
                            in_=wd_t.ap()[oc].rearrange("p i j -> p (i j)"),
                        ).then_inc(ds[k % 2].s, 16)
                        ds[k % 2].n += 16

            @blk.tensor
            def _(te):
                for tb in range(2):
                    for ic in range(ICn):
                        k = tb * ICn + ic
                        sl = k % 2
                        te.wait_ge(ws[sl].s, 32 * (k // 2 + 1))
                        if k >= 2:
                            te.wait_ge(dve2a.s, k - 1)
                        woff = sl * MCn * 128
                        for mc in range(MCn):
                            te.matmul(
                                GPS[sl][:, :],
                                WGT[:, woff + 128 * mc : woff + 128 * (mc + 1)],
                                xn_fm[:, TOK * mc + 512 * tb : TOK * mc + 512 * (tb + 1)],
                                start=(mc == 0), stop=(mc == MCn - 1),
                            )
                        for mc in range(MCn):
                            ins = te.matmul(
                                UPS[sl][:, :],
                                WUT[:, woff + 128 * mc : woff + 128 * (mc + 1)],
                                xn_fm[:, TOK * mc + 512 * tb : TOK * mc + 512 * (tb + 1)],
                                start=(mc == 0), stop=(mc == MCn - 1),
                            )
                        ins.then_inc(pe2.s, 1)
                        pe2.n += 1
                    for oc in range(OCn):
                        k = tb * OCn + oc
                        te.wait_ge(ds[k % 2].s, 16 * (k // 2 + 1))
                        if oc == 0:
                            te.wait_ge(dve2b.s, ICn * (tb + 1))
                        if k >= 4:
                            te.wait_ge(dve2d.s, k - 3)
                        for ic in range(ICn):
                            ins = te.matmul(
                                DPS[k % 4][:, :],
                                WDT[:, (k % 2) * ICn * 128 + 128 * ic :
                                    (k % 2) * ICn * 128 + 128 * (ic + 1)],
                                fuse[:, 512 * ic : 512 * (ic + 1)],
                                start=(ic == 0), stop=(ic == ICn - 1),
                            )
                        ins.then_inc(pe2d.s, 1)
                        pe2d.n += 1

            @blk.vector
            def _(v):
                for tb in range(2):
                    for ic in range(ICn):
                        k = tb * ICn + ic
                        sl = k % 2
                        if ic > 0:
                            v.wait_ge(act2.s, k)
                            v.tensor_mul(
                                fuse[:, 512 * (ic - 1) : 512 * ic],
                                GATE[:, 512 * ((k - 1) % 2) : 512 * ((k - 1) % 2 + 1)],
                                US[:, 512 * ((k - 1) % 2) : 512 * ((k - 1) % 2 + 1)],
                            ).then_inc(dve2b.s, 1)
                            dve2b.n += 1
                        v.wait_ge(pe2.s, k + 1)
                        if ic == 0 and tb == 1:
                            v.wait_ge(pe2d.s, OCn)
                        v.tensor_scalar(
                            GS[:, 512 * sl : 512 * (sl + 1)],
                            GPS[sl][:, :], SG[:, ic : ic + 1], None, ALU.mult)
                        v.tensor_scalar(
                            US[:, 512 * sl : 512 * (sl + 1)],
                            UPS[sl][:, :], SU[:, ic : ic + 1], None,
                            ALU.mult).then_inc(dve2a.s, 1)
                        dve2a.n += 1
                    k = tb * ICn + ICn - 1
                    v.wait_ge(act2.s, k + 1)
                    v.tensor_mul(
                        fuse[:, 512 * (ICn - 1) : 512 * ICn],
                        GATE[:, 512 * (k % 2) : 512 * (k % 2 + 1)],
                        US[:, 512 * (k % 2) : 512 * (k % 2 + 1)],
                    ).then_inc(dve2b.s, 1)
                    dve2b.n += 1
                    for oc in range(OCn):
                        k = tb * OCn + oc
                        v.wait_ge(pe2d.s, k + 1)
                        if k >= 2:
                            v.wait_ge(act2d.s, k - 1)
                        v.tensor_scalar(
                            DSC[:, 512 * (k % 2) : 512 * (k % 2 + 1)],
                            DPS[k % 4][:, :], SD[:, oc : oc + 1], None,
                            ALU.mult).then_inc(dve2d.s, 1)
                        dve2d.n += 1

            @blk.scalar
            def _(sc):
                for tb in range(2):
                    for ic in range(ICn):
                        k = tb * ICn + ic
                        sl = k % 2
                        sc.wait_ge(dve2a.s, k + 1)
                        if k >= 2:
                            sc.wait_ge(dve2b.s, k - 1)
                        sc.activation(
                            out=GATE[:, 512 * sl : 512 * (sl + 1)],
                            in_=GS[:, 512 * sl : 512 * (sl + 1)],
                            func=AF.Silu).then_inc(act2.s, 1)
                        act2.n += 1
                    for oc in range(OCn):
                        k = tb * OCn + oc
                        sc.wait_ge(dve2d.s, k + 1)
                        sc.activation(
                            out=g_fm[:, TOK * oc + 512 * tb : TOK * oc + 512 * (tb + 1)],
                            in_=DSC[:, 512 * (k % 2) : 512 * (k % 2 + 1)],
                            func=AF.Silu).then_inc(act2d.s, 1)
                        act2d.n += 1

        if "t_g" in tap_t:
            with nc.Block("blkBtap") as blk:
                @blk.sync
                def _(sy):
                    sy.dma_start(out=tap_t["t_g"][:, :],
                                 in_=g_fm[:, :]).then_inc(st_tap.s, 16)
                    st_tap.n += 16

    pB.close()
    xn_stack.close()

    if max_phase <= 2:
        pF0 = ExitStack()
        _fin(nc, pF0, st_sy)
        pF0.close()
        st.close()
        return nc

    qS = ExitStack()
    QhatA = qS.enter_context(nc.sbuf_tensor("QhatA", [128, T], BF16))
    QhatB = qS.enter_context(nc.sbuf_tensor("QhatB", [128, T], BF16))
    Khat = qS.enter_context(nc.sbuf_tensor("Khat", [128, T], BF16))
    VT = qS.enter_context(nc.sbuf_tensor("VT", [128, NCH * 65], BF16))
    ctx_tm = qS.enter_context(nc.sbuf_tensor("ctx_tm", [128, NCH * 256], BF16))
    if True:
        pC = ExitStack()
        RC = pC.enter_context(nc.sbuf_tensor("RC", [128, NCH * 32], F32))
        RS_ = pC.enter_context(nc.sbuf_tensor("RS_", [128, NCH * 32], F32))
        SLAB = pC.enter_context(nc.sbuf_tensor("SLAB", [128, 2 * MCn * 512], BF16))
        QTM = pC.enter_context(nc.sbuf_tensor("QTM", [128, 2 * 256], BF16))
        KTM = pC.enter_context(nc.sbuf_tensor("KTM", [128, 2 * 128], BF16))
        TT1 = pC.enter_context(nc.sbuf_tensor("TT1", [128, 320], F32))
        TT2 = pC.enter_context(nc.sbuf_tensor("TT2", [128, 320], F32))
        TT3 = pC.enter_context(nc.sbuf_tensor("TT3", [128, 320], F32))
        TT4 = pC.enter_context(nc.sbuf_tensor("TT4", [128, 320], F32))
        SQS = pC.enter_context(nc.sbuf_tensor("SQS", [128, 2 * 256], F32))
        SQK = pC.enter_context(nc.sbuf_tensor("SQK", [128, 2 * 64], F32))
        QN = pC.enter_context(nc.sbuf_tensor("QN", [128, NCH * 4], F32))
        KN = pC.enter_context(nc.sbuf_tensor("KN", [128, NCH], F32))
        MS = pC.enter_context(nc.sbuf_tensor("MS", [128, 1], F32))
        MV = pC.enter_context(nc.sbuf_tensor("MV", [1, 8], F32))
        ONES1 = pC.enter_context(nc.sbuf_tensor("ONES1", [1, 128], F32))
        QKVP0 = pC.enter_context(nc.psum_tensor("QKVP0", [128, 512], F32))
        QKVP1 = pC.enter_context(nc.psum_tensor("QKVP1", [128, 512], F32))
        TPS30 = pC.enter_context(nc.psum_tensor("TPS30", [128, 1024], BF16))
        TPS31 = pC.enter_context(nc.psum_tensor("TPS31", [128, 1024], BF16))
        TPS32 = pC.enter_context(nc.psum_tensor("TPS32", [128, 1024], BF16))
        PSMF = pC.enter_context(nc.psum_tensor("PSMF", [128, 512], F32))
        if True:
            QKVP = (QKVP0, QKVP1)
            TPS3 = (TPS30, TPS31, TPS32)
            NHS = 16
            ld1 = ld_sy.n

            with nc.Block("blkC") as blk:
                ini3 = sem("ini3")
                ssl = (sem("ssl0"), sem("ssl1"))
                pe3 = sem("pe3")
                dve3r = sem("dve3r")
                act3v = sem("act3v")
                dve3n = sem("dve3n")
                pe3t = sem("pe3t")
                dve3e = sem("dve3e")
                mseq = sem("mseq")

                @blk.sync
                def _(sy):
                    sy.dma_start(out=RC[:, :],
                                 in_=rc_in[:, :]).then_inc(ld_sy.s, 16)
                    sy.dma_start(out=RS_[:, :],
                                 in_=rs_in[:, :]).then_inc(ld_sy.s, 16)
                    ld_sy.n += 32
                    sy.wait_ge(cc.s, 1)
                    for hs in range(NHS):
                        r, hh = hs // 2, hs % 2
                        if hs >= 2:
                            sy.wait_ge(pe3.s, 4 * (hs - 1))
                        for q in range(4):
                            sy.dma_start(
                                out=SLAB[:, (hs % 2) * MCn * 512 + 2048 * q :
                                     (hs % 2) * MCn * 512 + 2048 * (q + 1)],
                                in_=AP(xn_full_d,
                                       (2048 * r + 512 * q) * TOK + 512 * hh,
                                       [[TOK, 128], [128 * TOK, 4], [1, 512]]),
                            ).then_inc(ssl[hs % 2].s, 16)
                            ssl[hs % 2].n += 16

                @blk.tensor
                def _(te):
                    te.wait_ge(ld_gp.s, gp_final)
                    for g in range(NCH):
                        hs, tc, pp = g // 4, g % 4, g % 2
                        te.wait_ge(ssl[hs % 2].s, 64 * (hs // 2 + 1))
                        if g >= 2:
                            te.wait_ge(dve3r.s, g - 1)
                            if C_LVL >= 2:
                                te.wait_ge(act3v.s, 2 * (g - 2) + 1)
                        for mc in range(MCn):
                            ins = te.matmul(
                                QKVP[pp][:, 0:384],
                                SLAB[:, (hs % 2) * MCn * 512 + 512 * mc + 128 * tc :
                                     (hs % 2) * MCn * 512 + 512 * mc + 128 * (tc + 1)],
                                WALL[:, 384 * mc : 384 * (mc + 1)],
                                start=(mc == 0), stop=(mc == MCn - 1),
                            )
                        ins.then_inc(pe3.s, 1)
                        pe3.n += 1
                        if g >= 1:
                            gg = g - 1
                            te.wait_ge(dve3r.s, gg + 1)
                            if gg >= 1:
                                te.wait_ge(dve3e.s, 3 * gg)
                            qb = (gg % 2) * 256
                            kb = (gg % 2) * 128
                            te.transpose(TPS3[0][:, 0:128],
                                         QTM[:, qb : qb + 128],
                                         IDB[:, :]).then_inc(pe3t.s, 1)
                            te.transpose(TPS3[1][:, 0:128],
                                         QTM[:, qb + 128 : qb + 256],
                                         IDB[:, :]).then_inc(pe3t.s, 1)
                            te.transpose(TPS3[2][:, 0:128],
                                         KTM[:, kb : kb + 128],
                                         IDB[:, :]).then_inc(pe3t.s, 1)
                            pe3t.n += 3
                    gg = NCH - 1
                    te.wait_ge(dve3r.s, gg + 1)
                    te.wait_ge(dve3e.s, 3 * gg)
                    qb = (gg % 2) * 256
                    kb = (gg % 2) * 128
                    te.transpose(TPS3[0][:, 0:128], QTM[:, qb : qb + 128],
                                 IDB[:, :]).then_inc(pe3t.s, 1)
                    te.transpose(TPS3[1][:, 0:128], QTM[:, qb + 128 : qb + 256],
                                 IDB[:, :]).then_inc(pe3t.s, 1)
                    te.transpose(TPS3[2][:, 0:128], KTM[:, kb : kb + 128],
                                 IDB[:, :]).then_inc(pe3t.s, 1)
                    pe3t.n += 3
                    if C_LVL < 5 or _os.environ.get("C_NO_M"):
                        return
                    te.wait_ge(mseq.s, 1)
                    te.transpose(PSMF[0:1, 0:128], MS[:, 0:1], IDF[:, :]
                                 ).then_inc(mseq.s, 1)
                    te.wait_ge(mseq.s, 3)
                    te.transpose(PSMF[0:1, 0:128], MS[:, 0:1], IDF[:, :]
                                 ).then_inc(mseq.s, 1)
                    te.wait_ge(mseq.s, 7)
                    te.matmul(PSMF[:, 0:1], ONES1[0:1, :], MV[0:1, 4:5],
                              start=True, stop=True).then_inc(mseq.s, 1)
                    mseq.n = 8

                @blk.vector
                def _(v):
                    v.memset(VT[:, :], 1.0)
                    v.memset(ONES1[0:1, :], 1.0).then_inc(ini3.s, 1)
                    ini3.n += 1
                    v.wait_ge(ld_sy.s, ld1 + 32)
                    for g in range(NCH):
                        pp = g % 2
                        fh = AP(QKVP[pp], 0, [[512, 128], [64, 4], [1, 32]])
                        sh = AP(QKVP[pp], 32, [[512, 128], [64, 4], [1, 32]])
                        C4 = AP(RC, 32 * g, [[NCH * 32, 128], [0, 4], [1, 32]])
                        S4 = AP(RS_, 32 * g, [[NCH * 32, 128], [0, 4], [1, 32]])
                        qoff = (g % 2) * 256
                        qfh = AP(QTM, qoff, [[2 * 256, 128], [64, 4], [1, 32]])
                        qsh = AP(QTM, qoff + 32, [[2 * 256, 128], [64, 4], [1, 32]])
                        v.wait_ge(pe3.s, g + 1)
                        if g >= 2:
                            v.wait_ge(pe3t.s, 3 * (g - 1))
                            if C_LVL >= 2:
                                v.wait_ge(act3v.s, 2 * (g - 2) + 2)
                        kb = (g % 2) * 128
                        if C_LVL < 3:
                            v.tensor_copy(QTM[:, qoff : qoff + 256],
                                          QKVP[pp][:, 0:256])
                            v.tensor_copy(KTM[:, kb : kb + 128],
                                          QKVP[pp][:, 256:384]).then_inc(dve3r.s, 1)
                            dve3r.n += 1
                            if C_LVL < 4:
                                v.tensor_copy(KN[:, g : g + 1],
                                              QN[:, 0:1]).then_inc(dve3n.s, 1)
                                dve3n.n += 1
                            _skip_rope = True
                        else:
                            _skip_rope = False
                        if not _skip_rope:
                            kfh = AP(QKVP[pp], 256, [[512, 128], [1, 32]])
                            ksh = AP(QKVP[pp], 288, [[512, 128], [1, 32]])
                            C1 = AP(RC, 32 * g, [[NCH * 32, 128], [1, 32]])
                            S1 = AP(RS_, 32 * g, [[NCH * 32, 128], [1, 32]])
                            qo = (g % 2) * 160
                            ko = qo + 128
                            v.tensor_mul(TT1[:, qo : qo + 128], fh, C4)
                            v.tensor_mul(TT2[:, qo : qo + 128], sh, S4)
                            v.tensor_mul(TT3[:, qo : qo + 128], sh, C4)
                            VS(v, v.tensor_mul(TT4[:, qo : qo + 128], fh, S4))
                            v.tensor_sub(qfh, TT1[:, qo : qo + 128],
                                         TT2[:, qo : qo + 128])
                            v.tensor_add(qsh, TT3[:, qo : qo + 128],
                                         TT4[:, qo : qo + 128])
                            v.tensor_mul(TT1[:, ko : ko + 32], kfh, C1)
                            v.tensor_mul(TT2[:, ko : ko + 32], ksh, S1)
                            v.tensor_mul(TT3[:, ko : ko + 32], ksh, C1)
                            VS(v, v.tensor_mul(TT4[:, ko : ko + 32], kfh, S1))
                            v.tensor_sub(KTM[:, kb : kb + 32],
                                         TT1[:, ko : ko + 32],
                                         TT2[:, ko : ko + 32])
                            VS(v, v.tensor_add(KTM[:, kb + 32 : kb + 64],
                                               TT3[:, ko : ko + 32],
                                               TT4[:, ko : ko + 32]))
                            v.tensor_copy(KTM[:, kb + 64 : kb + 128],
                                          KTM[:, kb : kb + 64]).then_inc(dve3r.s, 1)
                            dve3r.n += 1
                        if C_LVL >= 4:
                            v.wait_ge(act3v.s, 2 * g + 2)
                            v.reduce_sum(
                                QN[:, 4 * g : 4 * (g + 1)],
                                AP(SQS, (g % 2) * 256,
                                   [[2 * 256, 128], [64, 4], [1, 64]]),
                                axis=AX.X)
                            _ins = v.reduce_sum(
                                KN[:, g : g + 1],
                                AP(SQK, (g % 2) * 64, [[2 * 64, 128], [1, 64]]),
                                axis=AX.X)
                            if g == NCH - 1:
                                VS(v, _ins)
                            else:
                                _ins.then_inc(dve3n.s, 1)
                                dve3n.n += 1
                        if g >= 1:
                            gg = g - 1
                            v.wait_ge(pe3t.s, 3 * gg + 3)
                            v.tensor_copy(QhatA[:, 128 * gg : 128 * (gg + 1)],
                                          TPS3[0][:, 0:128]).then_inc(dve3e.s, 1)
                            v.tensor_copy(QhatB[:, 128 * gg : 128 * (gg + 1)],
                                          TPS3[1][:, 0:128]).then_inc(dve3e.s, 1)
                            v.tensor_copy(Khat[:, 128 * gg : 128 * (gg + 1)],
                                          TPS3[2][:, 0:128]).then_inc(dve3e.s, 1)
                            dve3e.n += 3
                    gg = NCH - 1
                    v.wait_ge(pe3t.s, 3 * gg + 3)
                    v.tensor_copy(QhatA[:, 128 * gg : 128 * (gg + 1)],
                                  TPS3[0][:, 0:128]).then_inc(dve3e.s, 1)
                    v.tensor_copy(QhatB[:, 128 * gg : 128 * (gg + 1)],
                                  TPS3[1][:, 0:128]).then_inc(dve3e.s, 1)
                    v.tensor_copy(Khat[:, 128 * gg : 128 * (gg + 1)],
                                  TPS3[2][:, 0:128]).then_inc(dve3e.s, 1)
                    dve3e.n += 3
                    if C_LVL < 5 or _os.environ.get("C_NO_M"):
                        v.memset(NMB[:, 0:1], -10.0)
                        return
                    v.reduce_max(MS[:, 0:1], QN[:, :],
                                 axis=AX.X).then_inc(mseq.s, 1)
                    v.wait_ge(mseq.s, 2)
                    VS(v, v.reduce_max(MV[0:1, 0:1], PSMF[0:1, 0:128],
                                       axis=AX.X))
                    v.reduce_max(MS[:, 0:1], KN[:, :],
                                 axis=AX.X).then_inc(mseq.s, 1)
                    v.wait_ge(mseq.s, 4)
                    VS(v, v.reduce_max(MV[0:1, 1:2], PSMF[0:1, 0:128],
                                       axis=AX.X))
                    v.tensor_mul(MV[0:1, 2:3], MV[0:1, 0:1],
                                 MV[0:1, 1:2]).then_inc(mseq.s, 1)
                    v.wait_ge(mseq.s, 6)
                    v.tensor_scalar_mul(MV[0:1, 4:5], MV[0:1, 3:4],
                                        -0.125).then_inc(mseq.s, 1)
                    v.wait_ge(mseq.s, 8)
                    v.tensor_copy(NMB[:, 0:1], PSMF[:, 0:1])

                @blk.scalar
                def _(sc):
                    if C_LVL < 2:
                        return
                    sc.wait_ge(ini3.s, 1)
                    for g in range(NCH):
                        pp = g % 2
                        sc.wait_ge(pe3.s, g + 1)
                        if g >= 2 and C_LVL >= 4:
                            sc.wait_ge(dve3n.s, g - 1)
                        C_SUB = _os.environ.get("C_SUB", "")
                        if C_SUB == "ps1":
                            sc.activation(out=SQK[:, 0:64],
                                          in_=AP(QKVP[pp], 320, [[512, 128], [1, 64]]),
                                          func=AF.Copy).then_inc(act3v.s, 1)
                            sc.wait_ge(dve3r.s, g + 1)
                            sc.activation(out=SQK[0:1, 0:1], in_=SQK[0:1, 1:2],
                                          func=AF.Copy).then_inc(act3v.s, 1)
                            act3v.n += 2
                            continue
                        if C_SUB == "sb1":
                            sc.activation(out=SQK[:, 0:64],
                                          in_=QTM[:, (g % 2) * 256 : (g % 2) * 256 + 64],
                                          func=AF.Copy).then_inc(act3v.s, 1)
                            sc.wait_ge(dve3r.s, g + 1)
                            sc.activation(out=SQK[0:1, 0:1], in_=SQK[0:1, 1:2],
                                          func=AF.Copy).then_inc(act3v.s, 1)
                            act3v.n += 2
                            continue
                        if C_SUB == "min":
                            sc.activation(out=SQK[0:1, 0:1], in_=SQK[0:1, 1:2],
                                          func=AF.Copy).then_inc(act3v.s, 1)
                            sc.wait_ge(dve3r.s, g + 1)
                            sc.activation(out=SQK[0:1, 0:1], in_=SQK[0:1, 1:2],
                                          func=AF.Copy).then_inc(act3v.s, 1)
                            act3v.n += 2
                            continue
                        sc.wait_ge(dve3r.s, g + 1)
                        if C_SUB != "sq":
                            sc.activation(
                                out=AP(VT, 65 * g, [[NCH * 65, 128], [1, 64]]),
                                in_=AP(QKVP[pp], 320, [[512, 128], [1, 64]]),
                                func=AF.Copy).then_inc(act3v.s, 1)
                        else:
                            sc.activation(
                                out=SQK[:, (g % 2) * 64 : (g % 2 + 1) * 64],
                                in_=AP(QKVP[pp], 320, [[512, 128], [1, 64]]),
                                func=AF.Copy).then_inc(act3v.s, 1)
                        if C_SUB != "v":
                            sc.activation(
                                out=SQS[:, (g % 2) * 256 : (g % 2 + 1) * 256],
                                in_=QTM[:, (g % 2) * 256 : (g % 2 + 1) * 256],
                                func=AF.Square)
                            sc.activation(
                                out=SQK[:, (g % 2) * 64 : (g % 2 + 1) * 64],
                                in_=KTM[:, (g % 2) * 128 : (g % 2) * 128 + 64],
                                func=AF.Square).then_inc(act3v.s, 1)
                        else:
                            sc.activation(
                                out=SQK[:, (g % 2) * 64 : (g % 2 + 1) * 64],
                                in_=KTM[:, (g % 2) * 128 : (g % 2) * 128 + 64],
                                func=AF.Square).then_inc(act3v.s, 1)
                        act3v.n += 2
                    if C_LVL < 5 or _os.environ.get("C_NO_M"):
                        return
                    sc.wait_ge(mseq.s, 5)
                    SS(sc, sc.activation(out=MV[0:1, 3:4], in_=MV[0:1, 2:3],
                                         func=AF.Ln))
                    sc.activation(out=MV[0:1, 3:4], in_=MV[0:1, 3:4],
                                  func=AF.Exp, scale=0.5).then_inc(mseq.s, 1)

            if any(t in tap_t for t in ("t_qa", "t_qb", "t_kh", "t_v", "t_nm")):
                with nc.Block("blkCtap") as blk:
                    @blk.sync
                    def _(sy):
                        for nm, buf in (("t_qa", QhatA), ("t_qb", QhatB),
                                        ("t_kh", Khat), ("t_v", VT),
                                        ("t_nm", NMB)):
                            if nm in tap_t:
                                sy.dma_start(out=tap_t[nm][:, :],
                                             in_=buf[:, :]).then_inc(st_tap.s, 16)
                                st_tap.n += 16

        pC.close()
        if max_phase <= 3:
            qS.close()
            pF0 = ExitStack()
            _fin(nc, pF0, st_sy)
            pF0.close()
            st.close()
            return nc
        Wo2 = qS.enter_context(nc.sbuf_tensor("Wo2", [128, 2 * D], BF16))

        pD = ExitStack()
        MSK = pD.enter_context(nc.sbuf_tensor("MSK", [128, 2 * 1024], F32))
        PT = pD.enter_context(nc.sbuf_tensor("PT", [128, 2 * 1024], BF16))
        SM = pD.enter_context(nc.sbuf_tensor("SM", [128, 2 * 1024], F32))
        INV = pD.enter_context(nc.sbuf_tensor("INV", [128, 8], F32))
        SPS0 = pD.enter_context(nc.psum_tensor("SPS0", [128, 1024], F32))
        SPS1 = pD.enter_context(nc.psum_tensor("SPS1", [128, 1024], F32))
        CPS0 = pD.enter_context(nc.psum_tensor("CPS0", [128, 512], F32))
        CPS1 = pD.enter_context(nc.psum_tensor("CPS1", [128, 512], F32))
        CPS2 = pD.enter_context(nc.psum_tensor("CPS2", [128, 512], F32))
        CPS3 = pD.enter_context(nc.psum_tensor("CPS3", [128, 512], F32))
        if True:
            SPS = (SPS0, SPS1)
            CPS = (CPS0, CPS1, CPS2, CPS3)
            SCOL = (0, 512, 256, 768)
            sched = []
            kbi = 0
            for J in range(NJ):
                for kb in range(2 * J + 2):
                    sched.append((J, kb, kbi))
                    kbi += 1
            ld2 = ld_sy.n
            gp_wo2 = ld_gp.n + 32

            with nc.Block("blkD") as blk:
                pe4s = sem("pe4s")
                act4 = sem("act4")
                pe4a = sem("pe4a")
                dve4m = sem("dve4m")
                dve4c = sem("dve4c")

                @blk.gpsimd
                def _(gp):
                    for cc2 in range(2):
                        gp.dma_start(
                            out=Wo2[:, D * cc2 : D * (cc2 + 1)],
                            in_=wo_in[128 * cc2 : 128 * (cc2 + 1), :],
                        ).then_inc(ld_gp.s, 16)
                        ld_gp.n += 16

                @blk.sync
                def _(sy):
                    sy.dma_start(
                        out=MSK[:, :],
                        in_=AP(mk_in, 0, [[1024, 128], [128 * 1024, 2], [1, 1024]]),
                    ).then_inc(ld_sy.s, 16)
                    ld_sy.n += 16

                @blk.tensor
                def _(te):
                    def av_group(J, kb, i):
                        pp = i % 2
                        te.wait_ge(act4.s, i + 1)
                        for h in range(NH):
                            for qs in range(2):
                                ins = te.matmul(
                                    CPS[2 * (J % 2) + qs][:, 65 * h : 65 * (h + 1)],
                                    PT[:, 1024 * pp + SCOL[h] + 128 * qs :
                                       1024 * pp + SCOL[h] + 128 * (qs + 1)],
                                    VT[:, 65 * kb : 65 * (kb + 1)],
                                    start=(kb == 0 and h == 0),
                                    stop=(kb == 2 * J + 1),
                                    skip_group_check=True,
                                )
                        ins.then_inc(pe4a.s, 1)
                        pe4a.n += 1

                    for (J, kb, i) in sched:
                        p = i % 2
                        if kb == 0 and J >= 2:
                            te.wait_ge(dve4c.s, J - 1)
                        if i >= 2:
                            te.wait_ge(act4.s, i - 1)
                        for h in range(NH):
                            rb = 64 * (h % 2)
                            Q = QhatA if h < 2 else QhatB
                            ins = te.matmul(
                                SPS[p][:, SCOL[h] : SCOL[h] + 256],
                                Khat[rb : rb + 64, 128 * kb : 128 * (kb + 1)],
                                Q[rb : rb + 64, 256 * J : 256 * (J + 1)],
                                start=True, stop=True,
                                tile_position=(rb, 0),
                            )
                        ins.then_inc(pe4s.s, 1)
                        pe4s.n += 1
                        if i >= 1:
                            av_group(*sched[i - 1])
                    av_group(*sched[-1])

                @blk.vector
                def _(v):
                    v.wait_ge(ld_sy.s, ld2 + 16)
                    sm_last_use = [-1, -1]
                    dcnt = 0
                    for (J, kb, i) in sched:
                        p = i % 2
                        if kb >= 2 * J:
                            rel = kb - 2 * J
                            v.wait_ge(pe4s.s, i + 1)
                            if sm_last_use[p] >= 0:
                                v.wait_ge(act4.s, sm_last_use[p] + 1)
                            v.tensor_add(
                                SM[:, 1024 * p : 1024 * (p + 1)],
                                SPS[p][:, :],
                                MSK[:, 1024 * rel : 1024 * (rel + 1)],
                            ).then_inc(dve4m.s, 1)
                            dve4m.n += 1
                            sm_last_use[p] = i
                            dcnt += 1
                        if kb == 2 * J + 1:
                            v.wait_ge(pe4a.s, i + 1)
                            for qs in range(2):
                                cp = CPS[2 * (J % 2) + qs]
                                VS(v, v.reciprocal(
                                    INV[:, 4 * qs : 4 * (qs + 1)],
                                    AP(cp, 64, [[512, 128], [65, 4]])))
                                for h in range(NH):
                                    ins = v.tensor_scalar(
                                        ctx_tm[:, 256 * (2 * J + qs) + 64 * h :
                                               256 * (2 * J + qs) + 64 * (h + 1)],
                                        cp[:, 65 * h : 65 * h + 64],
                                        INV[:, 4 * qs + h : 4 * qs + h + 1],
                                        None, ALU.mult)
                            ins.then_inc(dve4c.s, 1)
                            dve4c.n += 1

                @blk.scalar
                def _(sc):
                    sc.wait_ge(ld_sy.s, ld2 + 16)
                    dcnt = 0
                    for (J, kb, i) in sched:
                        p = i % 2
                        if i >= 2:
                            sc.wait_ge(pe4a.s, i - 1)
                        if kb >= 2 * J:
                            dcnt += 1
                            sc.wait_ge(dve4m.s, dcnt)
                            src = SM[:, 1024 * p : 1024 * (p + 1)]
                        else:
                            sc.wait_ge(pe4s.s, i + 1)
                            src = SPS[p][:, :]
                        sc.activation(
                            out=PT[:, 1024 * p : 1024 * (p + 1)],
                            in_=src, func=AF.Exp,
                            bias=NMB[:, 0:1], scale=0.125,
                        ).then_inc(act4.s, 1)
                        act4.n += 1

            if "t_ctx" in tap_t:
                with nc.Block("blkDtap") as blk:
                    @blk.sync
                    def _(sy):
                        sy.dma_start(out=tap_t["t_ctx"][:, :],
                                     in_=ctx_tm[:, :]).then_inc(st_tap.s, 16)
                        st_tap.n += 16

        pD.close()
        if max_phase <= 4:
            qS.close()
            pF0 = ExitStack()
            _fin(nc, pF0, st_sy)
            pF0.close()
            st.close()
            return nc
        pE = ExitStack()
        CFM = pE.enter_context(nc.sbuf_tensor("CFM", [128, 2 * T], BF16))
        SPB = pE.enter_context(nc.sbuf_tensor("SPB", [128, 2 * D], F16))
        WPS0 = pE.enter_context(nc.psum_tensor("WPS0", [128, 512], F32))
        WPS1 = pE.enter_context(nc.psum_tensor("WPS1", [128, 512], F32))
        WPS2 = pE.enter_context(nc.psum_tensor("WPS2", [128, 512], F32))
        WPS3 = pE.enter_context(nc.psum_tensor("WPS3", [128, 512], F32))
        CTP0 = pE.enter_context(nc.psum_tensor("CTP0", [128, 1024], BF16))
        CTP1 = pE.enter_context(nc.psum_tensor("CTP1", [128, 1024], BF16))
        if True:
            WPS = (WPS0, WPS1, WPS2, WPS3)
            CTP = (CTP0, CTP1)
            st1 = st_sy.n

            with nc.Block("blkE") as blk:
                sts = (sem("sts0"), sem("sts1"))
                pe5t = sem("pe5t")
                dve5e = sem("dve5e")
                pe5w = sem("pe5w")
                ev5v = sem("ev5v")
                ev5a = sem("ev5a")

                eorder = [tc for tc in range(NCH) if tc % 8 < 4] + \
                         [tc for tc in range(NCH) if tc % 8 >= 4]

                def sp_target(tc):
                    cdst, w = tc // 8, tc % 8
                    if w < 4:
                        return sp_a, 512 * cdst + 128 * w
                    return sp_b, 512 * cdst + 128 * (w - 4)

                a_n = [0, 0]
                for idx in range(NCH // 2):
                    a_n[idx % 2] += 4 * 16

                @blk.tensor
                def _(te):
                    te.wait_ge(ld_gp.s, gp_wo2)
                    for idx, tc in enumerate(eorder):
                        for hf in range(2):
                            k = 2 * idx + hf
                            if k >= 2:
                                te.wait_ge(dve5e.s, k - 1)
                            te.transpose(
                                CTP[k % 2][:, 0:128],
                                ctx_tm[:, 256 * tc + 128 * hf :
                                       256 * tc + 128 * (hf + 1)],
                                IDB[:, :]).then_inc(pe5t.s, 1)
                            pe5t.n += 1
                        te.wait_ge(dve5e.s, 2 * idx + 2)
                        for nb in range(4):
                            if idx >= 1:
                                if nb < 2:
                                    te.wait_ge(ev5v.s, 2 * (idx - 1) + nb + 1)
                                else:
                                    te.wait_ge(ev5a.s, 2 * (idx - 1) + nb - 1)
                            for cc2 in range(2):
                                ins = te.matmul(
                                    WPS[nb][:, :],
                                    CFM[:, T * cc2 + 128 * tc : T * cc2 + 128 * (tc + 1)],
                                    Wo2[:, D * cc2 + 512 * nb : D * cc2 + 512 * (nb + 1)],
                                    start=(cc2 == 0), stop=(cc2 == 1),
                                )
                            ins.then_inc(pe5w.s, 1)
                            pe5w.n += 1

                @blk.vector
                def _(v):
                    for idx, tc in enumerate(eorder):
                        for hf in range(2):
                            k = 2 * idx + hf
                            v.wait_ge(pe5t.s, k + 1)
                            v.tensor_copy(
                                CFM[:, T * hf + 128 * tc : T * hf + 128 * (tc + 1)],
                                CTP[k % 2][:, 0:128]).then_inc(dve5e.s, 1)
                            dve5e.n += 1
                        for nb in (0, 1):
                            kk = 4 * idx + nb
                            v.wait_ge(pe5w.s, kk + 1)
                            if idx >= 2:
                                v.wait_ge(sts[idx % 2].s, 64 * (idx // 2))
                            v.tensor_copy(
                                SPB[:, (idx % 2) * D + 512 * nb :
                                    (idx % 2) * D + 512 * (nb + 1)],
                                WPS[nb][:, :]).then_inc(ev5v.s, 1)
                            ev5v.n += 1

                @blk.scalar
                def _(sc):
                    for idx, tc in enumerate(eorder):
                        for nb in (2, 3):
                            kk = 4 * idx + nb
                            sc.wait_ge(pe5w.s, kk + 1)
                            if idx >= 2:
                                sc.wait_ge(sts[idx % 2].s, 64 * (idx // 2))
                            sc.activation(
                                out=SPB[:, (idx % 2) * D + 512 * nb :
                                        (idx % 2) * D + 512 * (nb + 1)],
                                in_=WPS[nb][:, :],
                                func=AF.Copy).then_inc(ev5a.s, 1)
                            ev5a.n += 1

                @blk.sync
                def _(sy):
                    for idx, tc in enumerate(eorder):
                        tgt, row = sp_target(tc)
                        for nb in range(4):
                            if nb < 2:
                                sy.wait_ge(ev5v.s, 2 * idx + nb + 1)
                            else:
                                sy.wait_ge(ev5a.s, 2 * idx + nb - 1)
                            sy.dma_start(
                                out=tgt[row : row + 128,
                                        512 * nb : 512 * (nb + 1)],
                                in_=SPB[:, (idx % 2) * D + 512 * nb :
                                        (idx % 2) * D + 512 * (nb + 1)],
                            ).then_inc(sts[idx % 2].s, 16)
                            sts[idx % 2].n += 16

                @blk.gpsimd
                def _(gp):
                    gp.wait_ge(sts[0].s, a_n[0])
                    gp.wait_ge(sts[1].s, a_n[1])
                    gp.collective_compute(
                        "ReduceScatter", ALU.add,
                        replica_groups=[list(range(NCORES))],
                        ins=[sp_a.ap().opt()],
                        outs=[s_own_a.ap().opt()],
                    ).then_inc(cc.s, 1)
                    cc.n += 1
                    gp.wait_ge(sts[0].s, sts[0].n)
                    gp.wait_ge(sts[1].s, sts[1].n)
                    gp.collective_compute(
                        "ReduceScatter", ALU.add,
                        replica_groups=[list(range(NCORES))],
                        ins=[sp_b.ap().opt()],
                        outs=[s_own_b.ap().opt()],
                    ).then_inc(cc.s, 1)
                    cc.n += 1

    pE.close()
    qS.close()
    if max_phase <= 5:
        pF0 = ExitStack()
        _fin(nc, pF0, st_sy)
        pF0.close()
        st.close()
        return nc
    pF = ExitStack()
    SOB = pF.enter_context(nc.sbuf_tensor("SOB", [128, 2 * D], F32))
    XR = pF.enter_context(nc.sbuf_tensor("XR", [128, 2 * D], F32))
    TGW = pF.enter_context(nc.sbuf_tensor("TGW", [128, D], F32))
    OUTB = pF.enter_context(nc.sbuf_tensor("OUTB", [128, 2 * D], F32))
    CTPF0 = pF.enter_context(nc.psum_tensor("CTPF0", [128, 1024], BF16))
    CTPF1 = pF.enter_context(nc.psum_tensor("CTPF1", [128, 1024], BF16))
    if True:
        CTPF = (CTPF0, CTPF1)
        ld3 = ld_sy.n
        st2 = st_sy.n

        with nc.Block("blkF") as blk:
            pe6 = sem("pe6")
            dve6 = sem("dve6")
            dve6o = sem("dve6o")
            fsl = (sem("fsl0"), sem("fsl1"))
            fsg = (sem("fsg0"), sem("fsg1"))
            sto = (sem("sto0"), sem("sto1"))

            @blk.gpsimd
            def _(gp):
                for t in range(NT):
                    if t == 0:
                        gp.wait_ge(cc.s, 2)
                    if t == 4:
                        gp.wait_ge(cc.s, 3)
                    if t >= 2:
                        gp.wait_ge(dve6o.s, 16 * (t - 1))
                    src_d = s_own_a if t < 4 else s_own_b
                    gp.dma_start(
                        out=SOB[:, (t % 2) * D : (t % 2 + 1) * D],
                        in_=src_d[128 * (t % 4) : 128 * (t % 4 + 1), :],
                    ).then_inc(fsg[t % 2].s, 16)
                    fsg[t % 2].n += 16

            @blk.sync
            def _(sy):
                for t in range(NT):
                    if t >= 2:
                        sy.wait_ge(dve6o.s, 16 * (t - 1))
                    sy.dma_start(
                        out=XR[:, (t % 2) * D : (t % 2 + 1) * D],
                        in_=x_own[128 * t : 128 * (t + 1), :],
                    ).then_inc(fsl[t % 2].s, 16)
                    fsl[t % 2].n += 16
                    sy.wait_ge(dve6o.s, 16 * (t + 1))
                    sy.dma_start(
                        out=out_own[128 * t : 128 * (t + 1), :],
                        in_=OUTB[:, (t % 2) * D : (t % 2 + 1) * D],
                    ).then_inc(sto[t % 2].s, 16)
                    sto[t % 2].n += 16
                if "t_sp" in tap_t:
                    sy.dma_start(out=tap_t["t_sp"][:, :],
                                 in_=s_own_d[:, :]).then_inc(st_tap.s, 16)
                    st_tap.n += 16
                sy.wait_ge(sto[0].s, sto[0].n)
                sy.wait_ge(sto[1].s, sto[1].n)
                sy.wait_ge(st_sy.s, st_sy.n)
                if st_tap.n:
                    sy.wait_ge(st_tap.s, st_tap.n)

            @blk.tensor
            def _(te):
                for t in range(NT):
                    for oc in range(OCn):
                        k = OCn * t + oc
                        if k >= 2:
                            te.wait_ge(dve6.s, k - 1)
                        te.transpose(
                            CTPF[k % 2][:, 0:128],
                            g_fm[:, TOK * oc + 128 * t : TOK * oc + 128 * (t + 1)],
                            IDB[:, :]).then_inc(pe6.s, 1)
                        pe6.n += 1

            @blk.vector
            def _(v):
                for t in range(NT):
                    p = t % 2
                    v.wait_ge(fsl[p].s, 16 * (t // 2 + 1))
                    v.wait_ge(fsg[p].s, 16 * (t // 2 + 1))
                    if t >= 2:
                        v.wait_ge(sto[p].s, 16 * (t // 2))
                    for oc in range(OCn):
                        k = OCn * t + oc
                        v.wait_ge(pe6.s, k + 1)
                        ins = v.tensor_mul(
                            TGW[:, 128 * oc : 128 * (oc + 1)],
                            SOB[:, p * D + 128 * oc : p * D + 128 * (oc + 1)],
                            CTPF[k % 2][:, 0:128])
                        ins.then_inc(dve6.s, 1)
                        dve6.n += 1
                    for oc in range(OCn):
                        v.wait_ge(dve6.s, OCn * t + oc + 1)
                        v.tensor_add(
                            OUTB[:, p * D + 128 * oc : p * D + 128 * (oc + 1)],
                            XR[:, p * D + 128 * oc : p * D + 128 * (oc + 1)],
                            TGW[:, 128 * oc : 128 * (oc + 1)]).then_inc(dve6o.s, 1)
                        dve6o.n += 1
                    dve6o.n += 0

    pF.close()
    st.close()
    return nc


def _host_prep(inputs):
    x = np.ascontiguousarray(np.asarray(inputs["L0_attn_x"]).reshape(T, D))
    Wq = np.asarray(inputs["L0_attn_Wq"])
    Wk = np.asarray(inputs["L0_attn_Wk"])
    Wv = np.asarray(inputs["L0_attn_Wv"])
    Wo = np.asarray(inputs["L0_attn_Wo"])
    wg = np.asarray(inputs["L0_ffn_w_gate"])
    wu = np.asarray(inputs["L0_ffn_w_up"])
    wd = np.asarray(inputs["L0_ffn_w_down"])
    sg = np.asarray(inputs["L0_ffn_s_gate"])
    su = np.asarray(inputs["L0_ffn_s_up"])
    sd = np.asarray(inputs["L0_ffn_s_down"])

    wg_t = np.ascontiguousarray(
        wg.reshape(MCn, 128, ICn, 128).transpose(2, 1, 0, 3))
    wu_t = np.ascontiguousarray(
        wu.reshape(MCn, 128, ICn, 128).transpose(2, 1, 0, 3))
    wd_t = np.ascontiguousarray(
        wd.reshape(ICn, 128, OCn, 128).transpose(2, 1, 0, 3))
    sgT = np.ascontiguousarray(sg.reshape(ICn, 128).T)
    suT = np.ascontiguousarray(su.reshape(ICn, 128).T)
    sdT = np.ascontiguousarray(sd.reshape(OCn, 128).T)

    theta = (1.0 / np.power(
        10000.0, np.arange(0, 64, 2, dtype=np.float32) / 64.0)).astype(np.float32)
    pos = np.arange(T, dtype=np.float32)
    ang = np.outer(pos, theta)
    rc = np.cos(ang).astype(np.float32)
    rs = np.sin(ang).astype(np.float32)
    rc_t = np.ascontiguousarray(
        rc.reshape(NCH, 128, 32).transpose(1, 0, 2).reshape(128, NCH * 32))
    rs_t = np.ascontiguousarray(
        rs.reshape(NCH, 128, 32).transpose(1, 0, 2).reshape(128, NCH * 32))

    kk = np.arange(128)[:, None]
    qq = np.arange(256)[None, :]
    mk = np.zeros((2, 128, 1024), dtype=np.float32)
    for rel in range(2):
        m = np.where(128 * rel + kk > qq, NEGBIG, 0.0).astype(np.float32)
        mk[rel] = np.tile(m, (1, NH))

    idb = np.eye(128, dtype=ml_dtypes.bfloat16)
    idf = np.eye(128, dtype=np.float32)

    in_maps = []
    for c in range(NCORES):
        w_all_c = np.ascontiguousarray(np.concatenate(
            [Wq[:, 256 * c : 256 * (c + 1)],
             Wk[:, 64 * c : 64 * (c + 1)],
             Wv[:, 64 * c : 64 * (c + 1)]], axis=1))
        in_maps.append({
            "x_own": np.ascontiguousarray(x[TOK * c : TOK * (c + 1)]),
            "w_all": w_all_c,
            "wo_in": np.ascontiguousarray(Wo[256 * c : 256 * (c + 1), :]),
            "wg_t": wg_t, "wu_t": wu_t, "wd_t": wd_t,
            "sg_in": sgT, "su_in": suT, "sd_in": sdT,
            "rc_in": rc_t, "rs_in": rs_t, "mk_in": mk,
            "idb_in": idb, "idf_in": idf,
        })
    return in_maps


def run(inputs, taps=(), max_phase=9):
    in_maps = _host_prep(inputs)
    nc = build(taps=taps, max_phase=max_phase)
    res = run_bass_kernel_spmd(nc, in_maps, core_ids=list(range(NCORES)))
    return res.results


def kernel(**inputs):
    results = run(inputs)
    out = np.concatenate([results[c]["out_own"] for c in range(NCORES)], axis=0)
    return out.reshape(1, T, D).astype(np.float32)


# version 12 (final)
# speedup vs baseline: 1.1358x; 1.0093x over previous
import numpy as np
import ml_dtypes
from contextlib import ExitStack

import concourse.bass as bass
import concourse.mybir as mybir
from concourse.bass_utils import run_bass_kernel_spmd

F32 = mybir.dt.float32
BF16 = mybir.dt.bfloat16
I8 = mybir.dt.int8
F16 = mybir.dt.float16
AF = mybir.ActivationFunctionType
ALU = mybir.AluOpType
AX = mybir.AxisListType
AP = bass.AP

NCORES = 8
T = 8192
D = 2048
TOK = T // NCORES
MCn = 16
ICn = 64
OCn = 16
NH = 4
NJ = 32
NCH = 64
EPS = 1.0e-5
NEGBIG = -1.0e30


def _fin(nc, stack, st_sy):
    with nc.Block("blkFin") as blk:
        @blk.sync
        def _(sy):
            sy.wait_ge(st_sy.s, st_sy.n)


def build(taps=(), max_phase=9):
    import os as _os
    C_LVL = int(_os.environ.get("C_LVL", "9"))
    nc = bass.Bass("TRN2", target_bir_lowering=False)

    x_own = nc.dram_tensor("x_own", [TOK, D], F32, kind="ExternalInput")
    w_all = nc.dram_tensor("w_all", [D, 384], F32, kind="ExternalInput")
    wo_in = nc.dram_tensor("wo_in", [256, D], F32, kind="ExternalInput")
    wg_t = nc.dram_tensor("wg_t", [ICn, 128, MCn, 128], I8, kind="ExternalInput")
    wu_t = nc.dram_tensor("wu_t", [ICn, 128, MCn, 128], I8, kind="ExternalInput")
    wd_t = nc.dram_tensor("wd_t", [OCn, 128, ICn, 128], I8, kind="ExternalInput")
    sg_in = nc.dram_tensor("sg_in", [128, ICn], F32, kind="ExternalInput")
    su_in = nc.dram_tensor("su_in", [128, ICn], F32, kind="ExternalInput")
    sd_in = nc.dram_tensor("sd_in", [128, OCn], F32, kind="ExternalInput")
    rc_in = nc.dram_tensor("rc_in", [128, NCH * 32], F32, kind="ExternalInput")
    rs_in = nc.dram_tensor("rs_in", [128, NCH * 32], F32, kind="ExternalInput")
    mk_in = nc.dram_tensor("mk_in", [2, 128, 1024], F32, kind="ExternalInput")
    idb_in = nc.dram_tensor("idb_in", [128, 128], BF16, kind="ExternalInput")
    idf_in = nc.dram_tensor("idf_in", [128, 128], F32, kind="ExternalInput")

    out_own = nc.dram_tensor("out_own", [TOK, D], F32, kind="ExternalOutput")

    xn_own_d = nc.dram_tensor("xn_own_d", [D, TOK], BF16)
    xn_full_d = nc.dram_tensor(
        "xn_full_d", [NCORES * D, TOK], BF16,
        addr_space="Local" if _os.environ.get("SIM_LOCAL") else "Shared")
    sp_a = nc.dram_tensor("sp_a", [T // 2, D], F16)
    sp_b = nc.dram_tensor("sp_b", [T // 2, D], F16)
    s_own_a = nc.dram_tensor("s_own_a", [TOK // 2, D], F16)
    s_own_b = nc.dram_tensor("s_own_b", [TOK // 2, D], F16)

    tap_t = {}
    for name, shape, dt in (
        ("t_xn", [128, MCn * TOK], BF16),
        ("t_g", [128, OCn * TOK], BF16),
        ("t_qa", [128, T], BF16),
        ("t_qb", [128, T], BF16),
        ("t_kh", [128, T], BF16),
        ("t_v", [128, NCH * 65], BF16),
        ("t_nm", [128, 1], F32),
        ("t_ctx", [128, NCH * 256], BF16),
        ("t_sp", [TOK, D], F32),
    ):
        if name in taps:
            tap_t[name] = nc.dram_tensor(name, shape, dt, kind="ExternalOutput")

    st = ExitStack()

    def sem(name):
        class Ctr:
            pass
        c = Ctr()
        c.s = st.enter_context(nc.semaphore(name))
        c.n = 0
        return c

    ld_sy = sem("ld_sy")
    ld_gp = sem("ld_gp")
    st_sy = sem("st_sy")
    st_tap = sem("st_tap")
    cc = sem("cc")
    vsy = sem("vsy")
    ssy = sem("ssy")

    def VS(eng, ins):
        ins.then_inc(vsy.s, 1)
        vsy.n += 1
        eng.wait_ge(vsy.s, vsy.n)

    def SS(eng, ins):
        ins.then_inc(ssy.s, 1)
        ssy.n += 1
        eng.wait_ge(ssy.s, ssy.n)

    g_fm = st.enter_context(nc.sbuf_tensor("g_fm", [128, OCn * TOK], BF16))
    WALL = st.enter_context(nc.sbuf_tensor("WALL", [128, MCn * 384], BF16))
    SG = st.enter_context(nc.sbuf_tensor("SG", [128, ICn], F32))
    SU = st.enter_context(nc.sbuf_tensor("SU", [128, ICn], F32))
    SD = st.enter_context(nc.sbuf_tensor("SD", [128, OCn], F32))
    IDB = st.enter_context(nc.sbuf_tensor("IDB", [128, 128], BF16))
    IDF = st.enter_context(nc.sbuf_tensor("IDF", [128, 128], F32))
    NMB = st.enter_context(nc.sbuf_tensor("NMB", [128, 1], F32))

    xn_stack = ExitStack()
    xn_fm = xn_stack.enter_context(
        nc.sbuf_tensor("xn_fm", [128, MCn * TOK], BF16))

    NT = TOK // 128
    pA = ExitStack()
    XT = pA.enter_context(nc.sbuf_tensor("XT", [128, 2 * D], F32))
    XSQ = pA.enter_context(nc.sbuf_tensor("XSQ", [128, 2 * D], F32))
    XN = pA.enter_context(nc.sbuf_tensor("XN", [128, 2 * D], BF16))
    LNS = pA.enter_context(nc.sbuf_tensor("LNS", [128, 10], F32))
    TPS0 = pA.enter_context(nc.psum_tensor("TPS0", [128, 1024], BF16))
    TPS1 = pA.enter_context(nc.psum_tensor("TPS1", [128, 1024], BF16))
    if True:
        TPS = (TPS0, TPS1)
        ld0 = ld_sy.n

        with nc.Block("blkA") as blk:
            pe1 = sem("pe1")
            act1 = sem("act1")
            act1r = sem("act1r")
            dve1v = sem("dve1v")
            dve1n = sem("dve1n")
            dve1e = sem("dve1e")
            xs = (sem("xs0"), sem("xs1"))

            @blk.sync
            def _(sy):
                for buf, src in ((SG, sg_in), (SU, su_in), (SD, sd_in),
                                 (IDB, idb_in), (IDF, idf_in)):
                    sy.dma_start(out=buf[:, :], in_=src[:, :]).then_inc(ld_sy.s, 16)
                    ld_sy.n += 16
                for t in range(NT):
                    if t >= 2:
                        sy.wait_ge(dve1n.s, t - 1)
                        sy.wait_ge(act1.s, t - 1)
                    sy.dma_start(
                        out=XT[:, (t % 2) * D : (t % 2 + 1) * D],
                        in_=x_own[128 * t : 128 * (t + 1), :],
                    ).then_inc(xs[t % 2].s, 16)
                    xs[t % 2].n += 16

            @blk.gpsimd
            def _(gp):
                gp.dma_start(
                    out=WALL[:, :],
                    in_=AP(w_all, 0, [[384, 128], [128 * 384, MCn], [1, 384]]),
                ).then_inc(ld_gp.s, 16)
                ld_gp.n += 16

            @blk.scalar
            def _(sc):
                for t in range(NT):
                    p = t % 2
                    sc.wait_ge(xs[p].s, 16 * (t // 2 + 1))
                    if t >= 2:
                        sc.wait_ge(dve1v.s, t - 1)
                    sc.activation(
                        out=XSQ[:, p * D : (p + 1) * D],
                        in_=XT[:, p * D : (p + 1) * D], func=AF.Square,
                    ).then_inc(act1.s, 1)
                    act1.n += 1
                    sc.wait_ge(dve1v.s, t + 1)
                    SS(sc, sc.activation(out=LNS[:, 6 + p : 7 + p],
                                         in_=LNS[:, 4 + p : 5 + p], func=AF.Ln))
                    sc.activation(out=LNS[:, 6 + p : 7 + p],
                                  in_=LNS[:, 6 + p : 7 + p],
                                  func=AF.Exp, scale=-0.5).then_inc(act1r.s, 1)
                    act1r.n += 1

            @blk.vector
            def _(v):
                for t in range(NT):
                    p = t % 2
                    v.wait_ge(xs[p].s, 16 * (t // 2 + 1))
                    if t >= 2:
                        v.wait_ge(act1r.s, t - 1)
                    VS(v, v.reduce_sum(LNS[:, p : p + 1],
                                       XT[:, p * D : (p + 1) * D], axis=AX.X))
                    VS(v, v.tensor_scalar_mul(LNS[:, p : p + 1],
                                              LNS[:, p : p + 1], 1.0 / D))
                    v.wait_ge(act1.s, t + 1)
                    VS(v, v.reduce_sum(LNS[:, 2 + p : 3 + p],
                                       XSQ[:, p * D : (p + 1) * D], axis=AX.X))
                    VS(v, v.tensor_scalar_mul(LNS[:, 2 + p : 3 + p],
                                              LNS[:, 2 + p : 3 + p], 1.0 / D))
                    VS(v, v.tensor_mul(LNS[:, 4 + p : 5 + p], LNS[:, p : p + 1],
                                       LNS[:, p : p + 1]))
                    VS(v, v.tensor_sub(LNS[:, 4 + p : 5 + p],
                                       LNS[:, 2 + p : 3 + p],
                                       LNS[:, 4 + p : 5 + p]))
                    v.tensor_scalar_add(LNS[:, 4 + p : 5 + p],
                                        LNS[:, 4 + p : 5 + p],
                                        EPS).then_inc(dve1v.s, 1)
                    dve1v.n += 1
                    v.wait_ge(act1r.s, t + 1)
                    VS(v, v.scalar_tensor_tensor(
                        LNS[:, 8 + p : 9 + p], LNS[:, p : p + 1], -1.0,
                        LNS[:, 6 + p : 7 + p], ALU.mult, ALU.mult))
                    if t >= 2:
                        v.wait_ge(pe1.s, 16 * (t - 1))
                    v.tensor_scalar(
                        XN[:, p * D : (p + 1) * D],
                        XT[:, p * D : (p + 1) * D],
                        LNS[:, 6 + p : 7 + p], LNS[:, 8 + p : 9 + p],
                        ALU.mult, ALU.add,
                    ).then_inc(dve1n.s, 1)
                    dve1n.n += 1
                    for j in range(MCn):
                        v.wait_ge(pe1.s, 16 * t + j + 1)
                        v.tensor_copy(
                            xn_fm[:, TOK * j + 128 * t : TOK * j + 128 * (t + 1)],
                            TPS[j % 2][:, 0:128],
                        ).then_inc(dve1e.s, 1)
                        dve1e.n += 1

            @blk.tensor
            def _(te):
                te.wait_ge(ld_sy.s, ld0 + 5 * 16)
                for t in range(NT):
                    p = t % 2
                    te.wait_ge(dve1n.s, t + 1)
                    for j in range(MCn):
                        k = 16 * t + j
                        if k >= 2:
                            te.wait_ge(dve1e.s, k - 1)
                        te.transpose(
                            TPS[j % 2][:, 0:128],
                            XN[:, p * D + 128 * j : p * D + 128 * (j + 1)],
                            IDB[:, :],
                        ).then_inc(pe1.s, 1)
                        pe1.n += 1

    pA.close()
    pB = ExitStack()
    fuse = pB.enter_context(nc.sbuf_tensor("fuse", [128, ICn * 512], BF16))
    WGT = pB.enter_context(nc.sbuf_tensor("WGT", [128, 2 * MCn * 128], BF16))
    WUT = pB.enter_context(nc.sbuf_tensor("WUT", [128, 2 * MCn * 128], BF16))
    GS = pB.enter_context(nc.sbuf_tensor("GS", [128, 2 * 512], F32))
    US = pB.enter_context(nc.sbuf_tensor("US", [128, 2 * 512], F32))
    GATE = pB.enter_context(nc.sbuf_tensor("GATE", [128, 2 * 512], BF16))
    WDT = pB.enter_context(nc.sbuf_tensor("WDT", [128, 2 * ICn * 128], BF16))
    DSC = pB.enter_context(nc.sbuf_tensor("DSC", [128, 2 * 512], F32))
    GPS0 = pB.enter_context(nc.psum_tensor("GPS0", [128, 512], F32))
    GPS1 = pB.enter_context(nc.psum_tensor("GPS1", [128, 512], F32))
    UPS0 = pB.enter_context(nc.psum_tensor("UPS0", [128, 512], F32))
    UPS1 = pB.enter_context(nc.psum_tensor("UPS1", [128, 512], F32))
    DPS0 = pB.enter_context(nc.psum_tensor("DPS0", [128, 512], F32))
    DPS1 = pB.enter_context(nc.psum_tensor("DPS1", [128, 512], F32))
    DPS2 = pB.enter_context(nc.psum_tensor("DPS2", [128, 512], F32))
    DPS3 = pB.enter_context(nc.psum_tensor("DPS3", [128, 512], F32))
    if True:
        GPS = (GPS0, GPS1)
        UPS = (UPS0, UPS1)
        DPS = (DPS0, DPS1, DPS2, DPS3)

        st0 = st_sy.n
        gp_final = ld_gp.n

        with nc.Block("blkB") as blk:
            ws = (sem("ws0"), sem("ws1"))
            ds = (sem("ds0"), sem("ds1"))
            pe2 = sem("pe2")
            dve2a = sem("dve2a")
            act2 = sem("act2")
            dve2b = sem("dve2b")
            pe2d = sem("pe2d")
            dve2d = sem("dve2d")
            act2d = sem("act2d")

            @blk.sync
            def _(sy):
                for mc in range(MCn):
                    sy.dma_start(
                        out=xn_own_d[128 * mc : 128 * (mc + 1), :],
                        in_=xn_fm[:, TOK * mc : TOK * (mc + 1)],
                    ).then_inc(st_sy.s, 16)
                    st_sy.n += 16
                if "t_xn" in tap_t:
                    sy.dma_start(out=tap_t["t_xn"][:, :],
                                 in_=xn_fm[:, :]).then_inc(st_tap.s, 16)
                    st_tap.n += 16

            @blk.gpsimd
            def _(gp):
                first = True
                for tb in range(2):
                    for ic in range(ICn):
                        k = tb * ICn + ic
                        if k >= 2:
                            gp.wait_ge(pe2.s, k - 1)
                        sl = k % 2
                        gp.dma_start(
                            out=WGT[:, sl * MCn * 128 : (sl + 1) * MCn * 128],
                            in_=wg_t.ap()[ic].rearrange("p m j -> p (m j)"),
                        ).then_inc(ws[sl].s, 16)
                        gp.dma_start(
                            out=WUT[:, sl * MCn * 128 : (sl + 1) * MCn * 128],
                            in_=wu_t.ap()[ic].rearrange("p m j -> p (m j)"),
                        ).then_inc(ws[sl].s, 16)
                        ws[sl].n += 32
                        if first:
                            first = False
                            import os as _os
                            gp.wait_ge(st_sy.s, st0 + 16 * MCn)
                            if _os.environ.get("C_NO_CC"):
                                gp.sem_inc(cc.s, 1)
                            else:
                                gp.collective_compute(
                                    "AllGather", ALU.bypass,
                                    replica_groups=[list(range(NCORES))],
                                    ins=[xn_own_d.ap().opt()],
                                    outs=[xn_full_d.ap().opt()],
                                ).then_inc(cc.s, 1)
                            cc.n += 1
                            if _os.environ.get("SERIAL_CC"):
                                gp.wait_ge(cc.s, 1)
                    for oc in range(OCn):
                        k = tb * OCn + oc
                        if k >= 2:
                            gp.wait_ge(pe2d.s, k - 1)
                        gp.dma_start(
                            out=WDT[:, (k % 2) * ICn * 128 : (k % 2 + 1) * ICn * 128],
                            in_=wd_t.ap()[oc].rearrange("p i j -> p (i j)"),
                        ).then_inc(ds[k % 2].s, 16)
                        ds[k % 2].n += 16

            @blk.tensor
            def _(te):
                for tb in range(2):
                    for ic in range(ICn):
                        k = tb * ICn + ic
                        sl = k % 2
                        te.wait_ge(ws[sl].s, 32 * (k // 2 + 1))
                        if k >= 2:
                            te.wait_ge(dve2a.s, k - 1)
                        woff = sl * MCn * 128
                        for mc in range(MCn):
                            te.matmul(
                                GPS[sl][:, :],
                                WGT[:, woff + 128 * mc : woff + 128 * (mc + 1)],
                                xn_fm[:, TOK * mc + 512 * tb : TOK * mc + 512 * (tb + 1)],
                                start=(mc == 0), stop=(mc == MCn - 1),
                            )
                        for mc in range(MCn):
                            ins = te.matmul(
                                UPS[sl][:, :],
                                WUT[:, woff + 128 * mc : woff + 128 * (mc + 1)],
                                xn_fm[:, TOK * mc + 512 * tb : TOK * mc + 512 * (tb + 1)],
                                start=(mc == 0), stop=(mc == MCn - 1),
                            )
                        ins.then_inc(pe2.s, 1)
                        pe2.n += 1
                    for oc in range(OCn):
                        k = tb * OCn + oc
                        te.wait_ge(ds[k % 2].s, 16 * (k // 2 + 1))
                        if oc == 0:
                            te.wait_ge(dve2b.s, ICn * (tb + 1))
                        if k >= 4:
                            te.wait_ge(dve2d.s, k - 3)
                        for ic in range(ICn):
                            ins = te.matmul(
                                DPS[k % 4][:, :],
                                WDT[:, (k % 2) * ICn * 128 + 128 * ic :
                                    (k % 2) * ICn * 128 + 128 * (ic + 1)],
                                fuse[:, 512 * ic : 512 * (ic + 1)],
                                start=(ic == 0), stop=(ic == ICn - 1),
                            )
                        ins.then_inc(pe2d.s, 1)
                        pe2d.n += 1

            @blk.vector
            def _(v):
                for tb in range(2):
                    for ic in range(ICn):
                        k = tb * ICn + ic
                        sl = k % 2
                        if ic > 0:
                            v.wait_ge(act2.s, k)
                            v.tensor_mul(
                                fuse[:, 512 * (ic - 1) : 512 * ic],
                                GATE[:, 512 * ((k - 1) % 2) : 512 * ((k - 1) % 2 + 1)],
                                US[:, 512 * ((k - 1) % 2) : 512 * ((k - 1) % 2 + 1)],
                            ).then_inc(dve2b.s, 1)
                            dve2b.n += 1
                        v.wait_ge(pe2.s, k + 1)
                        if ic == 0 and tb == 1:
                            v.wait_ge(pe2d.s, OCn)
                        v.tensor_scalar(
                            GS[:, 512 * sl : 512 * (sl + 1)],
                            GPS[sl][:, :], SG[:, ic : ic + 1], None, ALU.mult)
                        v.tensor_scalar(
                            US[:, 512 * sl : 512 * (sl + 1)],
                            UPS[sl][:, :], SU[:, ic : ic + 1], None,
                            ALU.mult).then_inc(dve2a.s, 1)
                        dve2a.n += 1
                    k = tb * ICn + ICn - 1
                    v.wait_ge(act2.s, k + 1)
                    v.tensor_mul(
                        fuse[:, 512 * (ICn - 1) : 512 * ICn],
                        GATE[:, 512 * (k % 2) : 512 * (k % 2 + 1)],
                        US[:, 512 * (k % 2) : 512 * (k % 2 + 1)],
                    ).then_inc(dve2b.s, 1)
                    dve2b.n += 1
                    for oc in range(OCn):
                        k = tb * OCn + oc
                        v.wait_ge(pe2d.s, k + 1)
                        if k >= 2:
                            v.wait_ge(act2d.s, k - 1)
                        v.tensor_scalar(
                            DSC[:, 512 * (k % 2) : 512 * (k % 2 + 1)],
                            DPS[k % 4][:, :], SD[:, oc : oc + 1], None,
                            ALU.mult).then_inc(dve2d.s, 1)
                        dve2d.n += 1

            @blk.scalar
            def _(sc):
                for tb in range(2):
                    for ic in range(ICn):
                        k = tb * ICn + ic
                        sl = k % 2
                        sc.wait_ge(dve2a.s, k + 1)
                        if k >= 2:
                            sc.wait_ge(dve2b.s, k - 1)
                        sc.activation(
                            out=GATE[:, 512 * sl : 512 * (sl + 1)],
                            in_=GS[:, 512 * sl : 512 * (sl + 1)],
                            func=AF.Silu).then_inc(act2.s, 1)
                        act2.n += 1
                    for oc in range(OCn):
                        k = tb * OCn + oc
                        sc.wait_ge(dve2d.s, k + 1)
                        sc.activation(
                            out=g_fm[:, TOK * oc + 512 * tb : TOK * oc + 512 * (tb + 1)],
                            in_=DSC[:, 512 * (k % 2) : 512 * (k % 2 + 1)],
                            func=AF.Silu).then_inc(act2d.s, 1)
                        act2d.n += 1

        if "t_g" in tap_t:
            with nc.Block("blkBtap") as blk:
                @blk.sync
                def _(sy):
                    sy.dma_start(out=tap_t["t_g"][:, :],
                                 in_=g_fm[:, :]).then_inc(st_tap.s, 16)
                    st_tap.n += 16

    pB.close()
    xn_stack.close()

    if max_phase <= 2:
        pF0 = ExitStack()
        _fin(nc, pF0, st_sy)
        pF0.close()
        st.close()
        return nc

    qS = ExitStack()
    ctx_tm = qS.enter_context(nc.sbuf_tensor("ctx_tm", [128, NCH * 256], BF16))
    Wo2 = qS.enter_context(nc.sbuf_tensor("Wo2", [128, 2 * D], BF16))
    qk = ExitStack()
    QhatA = qk.enter_context(nc.sbuf_tensor("QhatA", [128, T], BF16))
    QhatB = qk.enter_context(nc.sbuf_tensor("QhatB", [128, T], BF16))
    Khat = qk.enter_context(nc.sbuf_tensor("Khat", [128, T], BF16))
    VT = qk.enter_context(nc.sbuf_tensor("VT", [128, NCH * 65], BF16))
    if True:
        pC = ExitStack()
        RC = pC.enter_context(nc.sbuf_tensor("RC", [128, NCH * 32], F32))
        RS_ = pC.enter_context(nc.sbuf_tensor("RS_", [128, NCH * 32], F32))
        SLAB = pC.enter_context(nc.sbuf_tensor("SLAB", [128, 2 * MCn * 512], BF16))
        QTM = pC.enter_context(nc.sbuf_tensor("QTM", [128, 2 * 256], BF16))
        KTM = pC.enter_context(nc.sbuf_tensor("KTM", [128, 2 * 128], BF16))
        TT1 = pC.enter_context(nc.sbuf_tensor("TT1", [128, 320], F32))
        TT2 = pC.enter_context(nc.sbuf_tensor("TT2", [128, 320], F32))
        TT3 = pC.enter_context(nc.sbuf_tensor("TT3", [128, 320], F32))
        TT4 = pC.enter_context(nc.sbuf_tensor("TT4", [128, 320], F32))
        SQS = pC.enter_context(nc.sbuf_tensor("SQS", [128, 2 * 256], F32))
        SQK = pC.enter_context(nc.sbuf_tensor("SQK", [128, 2 * 64], F32))
        QN = pC.enter_context(nc.sbuf_tensor("QN", [128, NCH * 4], F32))
        KN = pC.enter_context(nc.sbuf_tensor("KN", [128, NCH], F32))
        MS = pC.enter_context(nc.sbuf_tensor("MS", [128, 1], F32))
        MV = pC.enter_context(nc.sbuf_tensor("MV", [1, 8], F32))
        ONES1 = pC.enter_context(nc.sbuf_tensor("ONES1", [1, 128], F32))
        QKVP0 = pC.enter_context(nc.psum_tensor("QKVP0", [128, 512], F32))
        QKVP1 = pC.enter_context(nc.psum_tensor("QKVP1", [128, 512], F32))
        TPS30 = pC.enter_context(nc.psum_tensor("TPS30", [128, 1024], BF16))
        TPS31 = pC.enter_context(nc.psum_tensor("TPS31", [128, 1024], BF16))
        TPS32 = pC.enter_context(nc.psum_tensor("TPS32", [128, 1024], BF16))
        PSMF = pC.enter_context(nc.psum_tensor("PSMF", [128, 512], F32))
        if True:
            QKVP = (QKVP0, QKVP1)
            TPS3 = (TPS30, TPS31, TPS32)
            NHS = 16
            ld1 = ld_sy.n

            with nc.Block("blkC") as blk:
                ini3 = sem("ini3")
                ssl = (sem("ssl0"), sem("ssl1"))
                pe3 = sem("pe3")
                dve3r = sem("dve3r")
                act3v = sem("act3v")
                dve3n = sem("dve3n")
                pe3t = sem("pe3t")
                dve3e = sem("dve3e")
                mseq = sem("mseq")

                @blk.sync
                def _(sy):
                    sy.dma_start(out=RC[:, :],
                                 in_=rc_in[:, :]).then_inc(ld_sy.s, 16)
                    sy.dma_start(out=RS_[:, :],
                                 in_=rs_in[:, :]).then_inc(ld_sy.s, 16)
                    ld_sy.n += 32
                    sy.wait_ge(cc.s, 1)
                    for hs in range(NHS):
                        r, hh = hs // 2, hs % 2
                        if hs >= 2:
                            sy.wait_ge(pe3.s, 4 * (hs - 1))
                        for q in range(4):
                            sy.dma_start(
                                out=SLAB[:, (hs % 2) * MCn * 512 + 2048 * q :
                                     (hs % 2) * MCn * 512 + 2048 * (q + 1)],
                                in_=AP(xn_full_d,
                                       (2048 * r + 512 * q) * TOK + 512 * hh,
                                       [[TOK, 128], [128 * TOK, 4], [1, 512]]),
                            ).then_inc(ssl[hs % 2].s, 16)
                            ssl[hs % 2].n += 16

                @blk.tensor
                def _(te):
                    te.wait_ge(ld_gp.s, gp_final)
                    for g in range(NCH):
                        hs, tc, pp = g // 4, g % 4, g % 2
                        te.wait_ge(ssl[hs % 2].s, 64 * (hs // 2 + 1))
                        if g >= 2:
                            te.wait_ge(dve3r.s, g - 1)
                            if C_LVL >= 2:
                                te.wait_ge(act3v.s, 2 * (g - 2) + 1)
                        for mc in range(MCn):
                            ins = te.matmul(
                                QKVP[pp][:, 0:384],
                                SLAB[:, (hs % 2) * MCn * 512 + 512 * mc + 128 * tc :
                                     (hs % 2) * MCn * 512 + 512 * mc + 128 * (tc + 1)],
                                WALL[:, 384 * mc : 384 * (mc + 1)],
                                start=(mc == 0), stop=(mc == MCn - 1),
                            )
                        ins.then_inc(pe3.s, 1)
                        pe3.n += 1
                        if g >= 1:
                            gg = g - 1
                            te.wait_ge(dve3r.s, gg + 1)
                            if gg >= 1:
                                te.wait_ge(dve3e.s, 3 * gg)
                            qb = (gg % 2) * 256
                            kb = (gg % 2) * 128
                            te.transpose(TPS3[0][:, 0:128],
                                         QTM[:, qb : qb + 128],
                                         IDB[:, :]).then_inc(pe3t.s, 1)
                            te.transpose(TPS3[1][:, 0:128],
                                         QTM[:, qb + 128 : qb + 256],
                                         IDB[:, :]).then_inc(pe3t.s, 1)
                            te.transpose(TPS3[2][:, 0:128],
                                         KTM[:, kb : kb + 128],
                                         IDB[:, :]).then_inc(pe3t.s, 1)
                            pe3t.n += 3
                    gg = NCH - 1
                    te.wait_ge(dve3r.s, gg + 1)
                    te.wait_ge(dve3e.s, 3 * gg)
                    qb = (gg % 2) * 256
                    kb = (gg % 2) * 128
                    te.transpose(TPS3[0][:, 0:128], QTM[:, qb : qb + 128],
                                 IDB[:, :]).then_inc(pe3t.s, 1)
                    te.transpose(TPS3[1][:, 0:128], QTM[:, qb + 128 : qb + 256],
                                 IDB[:, :]).then_inc(pe3t.s, 1)
                    te.transpose(TPS3[2][:, 0:128], KTM[:, kb : kb + 128],
                                 IDB[:, :]).then_inc(pe3t.s, 1)
                    pe3t.n += 3
                    if C_LVL < 5 or _os.environ.get("C_NO_M"):
                        return
                    te.wait_ge(mseq.s, 1)
                    te.transpose(PSMF[0:1, 0:128], MS[:, 0:1], IDF[:, :]
                                 ).then_inc(mseq.s, 1)
                    te.wait_ge(mseq.s, 3)
                    te.transpose(PSMF[0:1, 0:128], MS[:, 0:1], IDF[:, :]
                                 ).then_inc(mseq.s, 1)
                    te.wait_ge(mseq.s, 7)
                    te.matmul(PSMF[:, 0:1], ONES1[0:1, :], MV[0:1, 4:5],
                              start=True, stop=True).then_inc(mseq.s, 1)
                    mseq.n = 8

                @blk.vector
                def _(v):
                    v.memset(VT[:, :], 1.0)
                    v.memset(ONES1[0:1, :], 1.0).then_inc(ini3.s, 1)
                    ini3.n += 1
                    v.wait_ge(ld_sy.s, ld1 + 32)
                    for g in range(NCH):
                        pp = g % 2
                        fh = AP(QKVP[pp], 0, [[512, 128], [64, 4], [1, 32]])
                        sh = AP(QKVP[pp], 32, [[512, 128], [64, 4], [1, 32]])
                        C4 = AP(RC, 32 * g, [[NCH * 32, 128], [0, 4], [1, 32]])
                        S4 = AP(RS_, 32 * g, [[NCH * 32, 128], [0, 4], [1, 32]])
                        qoff = (g % 2) * 256
                        qfh = AP(QTM, qoff, [[2 * 256, 128], [64, 4], [1, 32]])
                        qsh = AP(QTM, qoff + 32, [[2 * 256, 128], [64, 4], [1, 32]])
                        v.wait_ge(pe3.s, g + 1)
                        if g >= 2:
                            v.wait_ge(pe3t.s, 3 * (g - 1))
                            if C_LVL >= 2:
                                v.wait_ge(act3v.s, 2 * (g - 2) + 2)
                        kb = (g % 2) * 128
                        if C_LVL < 3:
                            v.tensor_copy(QTM[:, qoff : qoff + 256],
                                          QKVP[pp][:, 0:256])
                            v.tensor_copy(KTM[:, kb : kb + 128],
                                          QKVP[pp][:, 256:384]).then_inc(dve3r.s, 1)
                            dve3r.n += 1
                            if C_LVL < 4:
                                v.tensor_copy(KN[:, g : g + 1],
                                              QN[:, 0:1]).then_inc(dve3n.s, 1)
                                dve3n.n += 1
                            _skip_rope = True
                        else:
                            _skip_rope = False
                        if not _skip_rope:
                            kfh = AP(QKVP[pp], 256, [[512, 128], [1, 32]])
                            ksh = AP(QKVP[pp], 288, [[512, 128], [1, 32]])
                            C1 = AP(RC, 32 * g, [[NCH * 32, 128], [1, 32]])
                            S1 = AP(RS_, 32 * g, [[NCH * 32, 128], [1, 32]])
                            qo = (g % 2) * 160
                            ko = qo + 128
                            v.tensor_mul(TT1[:, qo : qo + 128], fh, C4)
                            v.tensor_mul(TT2[:, qo : qo + 128], sh, S4)
                            v.tensor_mul(TT3[:, qo : qo + 128], sh, C4)
                            VS(v, v.tensor_mul(TT4[:, qo : qo + 128], fh, S4))
                            v.tensor_sub(qfh, TT1[:, qo : qo + 128],
                                         TT2[:, qo : qo + 128])
                            v.tensor_add(qsh, TT3[:, qo : qo + 128],
                                         TT4[:, qo : qo + 128])
                            v.tensor_mul(TT1[:, ko : ko + 32], kfh, C1)
                            v.tensor_mul(TT2[:, ko : ko + 32], ksh, S1)
                            v.tensor_mul(TT3[:, ko : ko + 32], ksh, C1)
                            VS(v, v.tensor_mul(TT4[:, ko : ko + 32], kfh, S1))
                            v.tensor_sub(KTM[:, kb : kb + 32],
                                         TT1[:, ko : ko + 32],
                                         TT2[:, ko : ko + 32])
                            VS(v, v.tensor_add(KTM[:, kb + 32 : kb + 64],
                                               TT3[:, ko : ko + 32],
                                               TT4[:, ko : ko + 32]))
                            v.tensor_copy(KTM[:, kb + 64 : kb + 128],
                                          KTM[:, kb : kb + 64]).then_inc(dve3r.s, 1)
                            dve3r.n += 1
                        if C_LVL >= 4:
                            v.wait_ge(act3v.s, 2 * g + 2)
                            v.reduce_sum(
                                QN[:, 4 * g : 4 * (g + 1)],
                                AP(SQS, (g % 2) * 256,
                                   [[2 * 256, 128], [64, 4], [1, 64]]),
                                axis=AX.X)
                            _ins = v.reduce_sum(
                                KN[:, g : g + 1],
                                AP(SQK, (g % 2) * 64, [[2 * 64, 128], [1, 64]]),
                                axis=AX.X)
                            if g == NCH - 1:
                                VS(v, _ins)
                            else:
                                _ins.then_inc(dve3n.s, 1)
                                dve3n.n += 1
                        if g >= 1:
                            gg = g - 1
                            v.wait_ge(pe3t.s, 3 * gg + 3)
                            v.tensor_copy(QhatA[:, 128 * gg : 128 * (gg + 1)],
                                          TPS3[0][:, 0:128]).then_inc(dve3e.s, 1)
                            v.tensor_copy(QhatB[:, 128 * gg : 128 * (gg + 1)],
                                          TPS3[1][:, 0:128]).then_inc(dve3e.s, 1)
                            v.tensor_copy(Khat[:, 128 * gg : 128 * (gg + 1)],
                                          TPS3[2][:, 0:128]).then_inc(dve3e.s, 1)
                            dve3e.n += 3
                    gg = NCH - 1
                    v.wait_ge(pe3t.s, 3 * gg + 3)
                    v.tensor_copy(QhatA[:, 128 * gg : 128 * (gg + 1)],
                                  TPS3[0][:, 0:128]).then_inc(dve3e.s, 1)
                    v.tensor_copy(QhatB[:, 128 * gg : 128 * (gg + 1)],
                                  TPS3[1][:, 0:128]).then_inc(dve3e.s, 1)
                    v.tensor_copy(Khat[:, 128 * gg : 128 * (gg + 1)],
                                  TPS3[2][:, 0:128]).then_inc(dve3e.s, 1)
                    dve3e.n += 3
                    if C_LVL < 5 or _os.environ.get("C_NO_M"):
                        v.memset(NMB[:, 0:1], -10.0)
                        return
                    v.reduce_max(MS[:, 0:1], QN[:, :],
                                 axis=AX.X).then_inc(mseq.s, 1)
                    v.wait_ge(mseq.s, 2)
                    VS(v, v.reduce_max(MV[0:1, 0:1], PSMF[0:1, 0:128],
                                       axis=AX.X))
                    v.reduce_max(MS[:, 0:1], KN[:, :],
                                 axis=AX.X).then_inc(mseq.s, 1)
                    v.wait_ge(mseq.s, 4)
                    VS(v, v.reduce_max(MV[0:1, 1:2], PSMF[0:1, 0:128],
                                       axis=AX.X))
                    v.tensor_mul(MV[0:1, 2:3], MV[0:1, 0:1],
                                 MV[0:1, 1:2]).then_inc(mseq.s, 1)
                    v.wait_ge(mseq.s, 6)
                    v.tensor_scalar_mul(MV[0:1, 4:5], MV[0:1, 3:4],
                                        -0.125).then_inc(mseq.s, 1)
                    v.wait_ge(mseq.s, 8)
                    v.tensor_copy(NMB[:, 0:1], PSMF[:, 0:1])

                @blk.scalar
                def _(sc):
                    if C_LVL < 2:
                        return
                    sc.wait_ge(ini3.s, 1)
                    for g in range(NCH):
                        pp = g % 2
                        sc.wait_ge(pe3.s, g + 1)
                        if g >= 2 and C_LVL >= 4:
                            sc.wait_ge(dve3n.s, g - 1)
                        C_SUB = _os.environ.get("C_SUB", "")
                        if C_SUB == "ps1":
                            sc.activation(out=SQK[:, 0:64],
                                          in_=AP(QKVP[pp], 320, [[512, 128], [1, 64]]),
                                          func=AF.Copy).then_inc(act3v.s, 1)
                            sc.wait_ge(dve3r.s, g + 1)
                            sc.activation(out=SQK[0:1, 0:1], in_=SQK[0:1, 1:2],
                                          func=AF.Copy).then_inc(act3v.s, 1)
                            act3v.n += 2
                            continue
                        if C_SUB == "sb1":
                            sc.activation(out=SQK[:, 0:64],
                                          in_=QTM[:, (g % 2) * 256 : (g % 2) * 256 + 64],
                                          func=AF.Copy).then_inc(act3v.s, 1)
                            sc.wait_ge(dve3r.s, g + 1)
                            sc.activation(out=SQK[0:1, 0:1], in_=SQK[0:1, 1:2],
                                          func=AF.Copy).then_inc(act3v.s, 1)
                            act3v.n += 2
                            continue
                        if C_SUB == "min":
                            sc.activation(out=SQK[0:1, 0:1], in_=SQK[0:1, 1:2],
                                          func=AF.Copy).then_inc(act3v.s, 1)
                            sc.wait_ge(dve3r.s, g + 1)
                            sc.activation(out=SQK[0:1, 0:1], in_=SQK[0:1, 1:2],
                                          func=AF.Copy).then_inc(act3v.s, 1)
                            act3v.n += 2
                            continue
                        sc.wait_ge(dve3r.s, g + 1)
                        if C_SUB != "sq":
                            sc.activation(
                                out=AP(VT, 65 * g, [[NCH * 65, 128], [1, 64]]),
                                in_=AP(QKVP[pp], 320, [[512, 128], [1, 64]]),
                                func=AF.Copy).then_inc(act3v.s, 1)
                        else:
                            sc.activation(
                                out=SQK[:, (g % 2) * 64 : (g % 2 + 1) * 64],
                                in_=AP(QKVP[pp], 320, [[512, 128], [1, 64]]),
                                func=AF.Copy).then_inc(act3v.s, 1)
                        if C_SUB != "v":
                            sc.activation(
                                out=SQS[:, (g % 2) * 256 : (g % 2 + 1) * 256],
                                in_=QTM[:, (g % 2) * 256 : (g % 2 + 1) * 256],
                                func=AF.Square)
                            sc.activation(
                                out=SQK[:, (g % 2) * 64 : (g % 2 + 1) * 64],
                                in_=KTM[:, (g % 2) * 128 : (g % 2) * 128 + 64],
                                func=AF.Square).then_inc(act3v.s, 1)
                        else:
                            sc.activation(
                                out=SQK[:, (g % 2) * 64 : (g % 2 + 1) * 64],
                                in_=KTM[:, (g % 2) * 128 : (g % 2) * 128 + 64],
                                func=AF.Square).then_inc(act3v.s, 1)
                        act3v.n += 2
                    if C_LVL < 5 or _os.environ.get("C_NO_M"):
                        return
                    sc.wait_ge(mseq.s, 5)
                    SS(sc, sc.activation(out=MV[0:1, 3:4], in_=MV[0:1, 2:3],
                                         func=AF.Ln))
                    sc.activation(out=MV[0:1, 3:4], in_=MV[0:1, 3:4],
                                  func=AF.Exp, scale=0.5).then_inc(mseq.s, 1)

            if any(t in tap_t for t in ("t_qa", "t_qb", "t_kh", "t_v", "t_nm")):
                with nc.Block("blkCtap") as blk:
                    @blk.sync
                    def _(sy):
                        for nm, buf in (("t_qa", QhatA), ("t_qb", QhatB),
                                        ("t_kh", Khat), ("t_v", VT),
                                        ("t_nm", NMB)):
                            if nm in tap_t:
                                sy.dma_start(out=tap_t[nm][:, :],
                                             in_=buf[:, :]).then_inc(st_tap.s, 16)
                                st_tap.n += 16

        pC.close()
        if max_phase <= 3:
            qS.close()
            pF0 = ExitStack()
            _fin(nc, pF0, st_sy)
            pF0.close()
            st.close()
            return nc
        pD = ExitStack()
        MSK = pD.enter_context(nc.sbuf_tensor("MSK", [128, 2 * 1024], F32))
        PT = pD.enter_context(nc.sbuf_tensor("PT", [128, 2 * 1024], BF16))
        SM = pD.enter_context(nc.sbuf_tensor("SM", [128, 2 * 1024], F32))
        INV = pD.enter_context(nc.sbuf_tensor("INV", [128, 8], F32))
        SPS0 = pD.enter_context(nc.psum_tensor("SPS0", [128, 1024], F32))
        SPS1 = pD.enter_context(nc.psum_tensor("SPS1", [128, 1024], F32))
        CPS0 = pD.enter_context(nc.psum_tensor("CPS0", [128, 512], F32))
        CPS1 = pD.enter_context(nc.psum_tensor("CPS1", [128, 512], F32))
        CPS2 = pD.enter_context(nc.psum_tensor("CPS2", [128, 512], F32))
        CPS3 = pD.enter_context(nc.psum_tensor("CPS3", [128, 512], F32))
        if True:
            SPS = (SPS0, SPS1)
            CPS = (CPS0, CPS1, CPS2, CPS3)
            SCOL = (0, 512, 256, 768)
            sched = []
            kbi = 0
            for J in range(NJ):
                for kb in range(2 * J + 2):
                    sched.append((J, kb, kbi))
                    kbi += 1
            ld2 = ld_sy.n
            gp_wo2 = ld_gp.n + 32

            with nc.Block("blkD") as blk:
                pe4s = sem("pe4s")
                act4 = sem("act4")
                pe4a = sem("pe4a")
                dve4m = sem("dve4m")
                dve4c = sem("dve4c")

                @blk.gpsimd
                def _(gp):
                    for cc2 in range(2):
                        gp.dma_start(
                            out=Wo2[:, D * cc2 : D * (cc2 + 1)],
                            in_=wo_in[128 * cc2 : 128 * (cc2 + 1), :],
                        ).then_inc(ld_gp.s, 16)
                        ld_gp.n += 16

                @blk.sync
                def _(sy):
                    sy.dma_start(
                        out=MSK[:, :],
                        in_=AP(mk_in, 0, [[1024, 128], [128 * 1024, 2], [1, 1024]]),
                    ).then_inc(ld_sy.s, 16)
                    ld_sy.n += 16

                @blk.tensor
                def _(te):
                    def av_group(J, kb, i):
                        pp = i % 2
                        te.wait_ge(act4.s, i + 1)
                        for h in range(NH):
                            for qs in range(2):
                                ins = te.matmul(
                                    CPS[2 * (J % 2) + qs][:, 65 * h : 65 * (h + 1)],
                                    PT[:, 1024 * pp + SCOL[h] + 128 * qs :
                                       1024 * pp + SCOL[h] + 128 * (qs + 1)],
                                    VT[:, 65 * kb : 65 * (kb + 1)],
                                    start=(kb == 0 and h == 0),
                                    stop=(kb == 2 * J + 1),
                                    skip_group_check=True,
                                )
                        ins.then_inc(pe4a.s, 1)
                        pe4a.n += 1

                    for (J, kb, i) in sched:
                        p = i % 2
                        if kb == 0 and J >= 2:
                            te.wait_ge(dve4c.s, J - 1)
                        if i >= 2:
                            te.wait_ge(act4.s, i - 1)
                        for h in range(NH):
                            rb = 64 * (h % 2)
                            Q = QhatA if h < 2 else QhatB
                            ins = te.matmul(
                                SPS[p][:, SCOL[h] : SCOL[h] + 256],
                                Khat[rb : rb + 64, 128 * kb : 128 * (kb + 1)],
                                Q[rb : rb + 64, 256 * J : 256 * (J + 1)],
                                start=True, stop=True,
                                tile_position=(rb, 0),
                            )
                        ins.then_inc(pe4s.s, 1)
                        pe4s.n += 1
                        if i >= 1:
                            av_group(*sched[i - 1])
                    av_group(*sched[-1])

                @blk.vector
                def _(v):
                    v.wait_ge(ld_sy.s, ld2 + 16)
                    sm_last_use = [-1, -1]
                    dcnt = 0
                    for (J, kb, i) in sched:
                        p = i % 2
                        if kb >= 2 * J:
                            rel = kb - 2 * J
                            v.wait_ge(pe4s.s, i + 1)
                            if sm_last_use[p] >= 0:
                                v.wait_ge(act4.s, sm_last_use[p] + 1)
                            v.tensor_add(
                                SM[:, 1024 * p : 1024 * (p + 1)],
                                SPS[p][:, :],
                                MSK[:, 1024 * rel : 1024 * (rel + 1)],
                            ).then_inc(dve4m.s, 1)
                            dve4m.n += 1
                            sm_last_use[p] = i
                            dcnt += 1
                        if kb == 2 * J + 1:
                            v.wait_ge(pe4a.s, i + 1)
                            for qs in range(2):
                                cp = CPS[2 * (J % 2) + qs]
                                VS(v, v.reciprocal(
                                    INV[:, 4 * qs : 4 * (qs + 1)],
                                    AP(cp, 64, [[512, 128], [65, 4]])))
                                for h in range(NH):
                                    ins = v.tensor_scalar(
                                        ctx_tm[:, 256 * (2 * J + qs) + 64 * h :
                                               256 * (2 * J + qs) + 64 * (h + 1)],
                                        cp[:, 65 * h : 65 * h + 64],
                                        INV[:, 4 * qs + h : 4 * qs + h + 1],
                                        None, ALU.mult)
                            ins.then_inc(dve4c.s, 1)
                            dve4c.n += 1

                @blk.scalar
                def _(sc):
                    sc.wait_ge(ld_sy.s, ld2 + 16)
                    dcnt = 0
                    for (J, kb, i) in sched:
                        p = i % 2
                        if i >= 2:
                            sc.wait_ge(pe4a.s, i - 1)
                        if kb >= 2 * J:
                            dcnt += 1
                            sc.wait_ge(dve4m.s, dcnt)
                            src = SM[:, 1024 * p : 1024 * (p + 1)]
                        else:
                            sc.wait_ge(pe4s.s, i + 1)
                            src = SPS[p][:, :]
                        sc.activation(
                            out=PT[:, 1024 * p : 1024 * (p + 1)],
                            in_=src, func=AF.Exp,
                            bias=NMB[:, 0:1], scale=0.125,
                        ).then_inc(act4.s, 1)
                        act4.n += 1

            if "t_ctx" in tap_t:
                with nc.Block("blkDtap") as blk:
                    @blk.sync
                    def _(sy):
                        sy.dma_start(out=tap_t["t_ctx"][:, :],
                                     in_=ctx_tm[:, :]).then_inc(st_tap.s, 16)
                        st_tap.n += 16

        pD.close()
        if max_phase <= 4:
            qk.close()
            qS.close()
            pF0 = ExitStack()
            _fin(nc, pF0, st_sy)
            pF0.close()
            st.close()
            return nc
        qk.close()
        pE = ExitStack()
        CFM = pE.enter_context(nc.sbuf_tensor("CFM", [128, 2 * T], BF16))
        SPB = pE.enter_context(nc.sbuf_tensor("SPB", [128, 2 * D], F16))
        SOB = pE.enter_context(nc.sbuf_tensor("SOB", [128, 2 * D], F32))
        XR = pE.enter_context(nc.sbuf_tensor("XR", [128, 2 * D], F32))
        TGW = pE.enter_context(nc.sbuf_tensor("TGW", [128, D], F32))
        OUTB = pE.enter_context(nc.sbuf_tensor("OUTB", [128, 2 * D], F32))
        CTPF0 = pE.enter_context(nc.psum_tensor("CTPF0", [128, 1024], BF16))
        CTPF1 = pE.enter_context(nc.psum_tensor("CTPF1", [128, 1024], BF16))
        WPS0 = pE.enter_context(nc.psum_tensor("WPS0", [128, 512], F32))
        WPS1 = pE.enter_context(nc.psum_tensor("WPS1", [128, 512], F32))
        WPS2 = pE.enter_context(nc.psum_tensor("WPS2", [128, 512], F32))
        WPS3 = pE.enter_context(nc.psum_tensor("WPS3", [128, 512], F32))
        CTP0 = pE.enter_context(nc.psum_tensor("CTP0", [128, 1024], BF16))
        CTP1 = pE.enter_context(nc.psum_tensor("CTP1", [128, 1024], BF16))
        if True:
            WPS = (WPS0, WPS1, WPS2, WPS3)
            CTP = (CTP0, CTP1)
            st1 = st_sy.n

            with nc.Block("blkE") as blk:
                sts = (sem("sts0"), sem("sts1"))
                pe5t = sem("pe5t")
                dve5e = sem("dve5e")
                pe5w = sem("pe5w")
                ev5v = sem("ev5v")
                ev5a = sem("ev5a")
                pe6 = sem("pe6")
                dve6 = sem("dve6")
                dve6o = sem("dve6o")
                fsl = (sem("fsl0"), sem("fsl1"))
                fsg = (sem("fsg0"), sem("fsg1"))
                sto = (sem("sto0"), sem("sto1"))
                CTPF = (CTPF0, CTPF1)

                eorder = [tc for tc in range(NCH) if tc % 8 < 4] + \
                         [tc for tc in range(NCH) if tc % 8 >= 4]

                def sp_target(tc):
                    cdst, w = tc // 8, tc % 8
                    if w < 4:
                        return sp_a, 512 * cdst + 128 * w
                    return sp_b, 512 * cdst + 128 * (w - 4)

                a_n = [0, 0]
                for idx in range(NCH // 2):
                    a_n[idx % 2] += 4 * 16

                @blk.tensor
                def _(te):
                    te.wait_ge(ld_gp.s, gp_wo2)
                    for idx, tc in enumerate(eorder):
                        for hf in range(2):
                            k = 2 * idx + hf
                            if k >= 2:
                                te.wait_ge(dve5e.s, k - 1)
                            te.transpose(
                                CTP[k % 2][:, 0:128],
                                ctx_tm[:, 256 * tc + 128 * hf :
                                       256 * tc + 128 * (hf + 1)],
                                IDB[:, :]).then_inc(pe5t.s, 1)
                            pe5t.n += 1
                        te.wait_ge(dve5e.s, 2 * idx + 2)
                        for nb in range(4):
                            if idx >= 1:
                                if nb < 2:
                                    te.wait_ge(ev5v.s, 2 * (idx - 1) + nb + 1)
                                else:
                                    te.wait_ge(ev5a.s, 2 * (idx - 1) + nb - 1)
                            for cc2 in range(2):
                                ins = te.matmul(
                                    WPS[nb][:, :],
                                    CFM[:, T * cc2 + 128 * tc : T * cc2 + 128 * (tc + 1)],
                                    Wo2[:, D * cc2 + 512 * nb : D * cc2 + 512 * (nb + 1)],
                                    start=(cc2 == 0), stop=(cc2 == 1),
                                )
                            ins.then_inc(pe5w.s, 1)
                            pe5w.n += 1
                    for t in range(NT):
                        for oc in range(OCn):
                            k = OCn * t + oc
                            if k >= 2:
                                te.wait_ge(dve6.s, k - 1)
                            te.transpose(
                                CTPF[k % 2][:, 0:128],
                                g_fm[:, TOK * oc + 128 * t : TOK * oc + 128 * (t + 1)],
                                IDB[:, :]).then_inc(pe6.s, 1)
                            pe6.n += 1

                @blk.vector
                def _(v):
                    for idx, tc in enumerate(eorder):
                        for hf in range(2):
                            k = 2 * idx + hf
                            v.wait_ge(pe5t.s, k + 1)
                            v.tensor_copy(
                                CFM[:, T * hf + 128 * tc : T * hf + 128 * (tc + 1)],
                                CTP[k % 2][:, 0:128]).then_inc(dve5e.s, 1)
                            dve5e.n += 1
                        for nb in (0, 1):
                            kk = 4 * idx + nb
                            v.wait_ge(pe5w.s, kk + 1)
                            if idx >= 2:
                                v.wait_ge(sts[idx % 2].s, 64 * (idx // 2))
                            v.tensor_copy(
                                SPB[:, (idx % 2) * D + 512 * nb :
                                    (idx % 2) * D + 512 * (nb + 1)],
                                WPS[nb][:, :]).then_inc(ev5v.s, 1)
                            ev5v.n += 1
                    for t in range(NT):
                        p = t % 2
                        v.wait_ge(fsl[p].s, 16 * (t // 2 + 1))
                        v.wait_ge(fsg[p].s, 16 * (t // 2 + 1))
                        if t >= 2:
                            v.wait_ge(sto[p].s, 16 * (t // 2))
                        for oc in range(OCn):
                            k = OCn * t + oc
                            v.wait_ge(pe6.s, k + 1)
                            ins = v.tensor_mul(
                                TGW[:, 128 * oc : 128 * (oc + 1)],
                                SOB[:, p * D + 128 * oc : p * D + 128 * (oc + 1)],
                                CTPF[k % 2][:, 0:128])
                            ins.then_inc(dve6.s, 1)
                            dve6.n += 1
                        for oc in range(OCn):
                            v.wait_ge(dve6.s, OCn * t + oc + 1)
                            v.tensor_add(
                                OUTB[:, p * D + 128 * oc : p * D + 128 * (oc + 1)],
                                XR[:, p * D + 128 * oc : p * D + 128 * (oc + 1)],
                                TGW[:, 128 * oc : 128 * (oc + 1)]).then_inc(dve6o.s, 1)
                            dve6o.n += 1
                        dve6o.n += 0


                @blk.scalar
                def _(sc):
                    for idx, tc in enumerate(eorder):
                        for nb in (2, 3):
                            kk = 4 * idx + nb
                            sc.wait_ge(pe5w.s, kk + 1)
                            if idx >= 2:
                                sc.wait_ge(sts[idx % 2].s, 64 * (idx // 2))
                            sc.activation(
                                out=SPB[:, (idx % 2) * D + 512 * nb :
                                        (idx % 2) * D + 512 * (nb + 1)],
                                in_=WPS[nb][:, :],
                                func=AF.Copy).then_inc(ev5a.s, 1)
                            ev5a.n += 1

                @blk.sync
                def _(sy):
                    for idx, tc in enumerate(eorder):
                        tgt, row = sp_target(tc)
                        for nb in range(4):
                            if nb < 2:
                                sy.wait_ge(ev5v.s, 2 * idx + nb + 1)
                            else:
                                sy.wait_ge(ev5a.s, 2 * idx + nb - 1)
                            sy.dma_start(
                                out=tgt[row : row + 128,
                                        512 * nb : 512 * (nb + 1)],
                                in_=SPB[:, (idx % 2) * D + 512 * nb :
                                        (idx % 2) * D + 512 * (nb + 1)],
                            ).then_inc(sts[idx % 2].s, 16)
                            sts[idx % 2].n += 16
                    for t in range(NT):
                        if t >= 2:
                            sy.wait_ge(dve6o.s, 16 * (t - 1))
                        sy.dma_start(
                            out=XR[:, (t % 2) * D : (t % 2 + 1) * D],
                            in_=x_own[128 * t : 128 * (t + 1), :],
                        ).then_inc(fsl[t % 2].s, 16)
                        fsl[t % 2].n += 16
                        sy.wait_ge(dve6o.s, 16 * (t + 1))
                        sy.dma_start(
                            out=out_own[128 * t : 128 * (t + 1), :],
                            in_=OUTB[:, (t % 2) * D : (t % 2 + 1) * D],
                        ).then_inc(sto[t % 2].s, 16)
                        sto[t % 2].n += 16
                    sy.wait_ge(sto[0].s, sto[0].n)
                    sy.wait_ge(sto[1].s, sto[1].n)
                    sy.wait_ge(st_sy.s, st_sy.n)
                    if st_tap.n:
                        sy.wait_ge(st_tap.s, st_tap.n)

                @blk.gpsimd
                def _(gp):
                    gp.wait_ge(sts[0].s, a_n[0])
                    gp.wait_ge(sts[1].s, a_n[1])
                    gp.collective_compute(
                        "ReduceScatter", ALU.add,
                        replica_groups=[list(range(NCORES))],
                        ins=[sp_a.ap().opt()],
                        outs=[s_own_a.ap().opt()],
                    ).then_inc(cc.s, 1)
                    cc.n += 1
                    gp.wait_ge(sts[0].s, sts[0].n)
                    gp.wait_ge(sts[1].s, sts[1].n)
                    gp.collective_compute(
                        "ReduceScatter", ALU.add,
                        replica_groups=[list(range(NCORES))],
                        ins=[sp_b.ap().opt()],
                        outs=[s_own_b.ap().opt()],
                    ).then_inc(cc.s, 1)
                    cc.n += 1
                    for t in range(NT):
                        if t == 0:
                            gp.wait_ge(cc.s, 2)
                        if t == 4:
                            gp.wait_ge(cc.s, 3)
                        if t >= 2:
                            gp.wait_ge(dve6o.s, 16 * (t - 1))
                        src_d = s_own_a if t < 4 else s_own_b
                        gp.dma_start(
                            out=SOB[:, (t % 2) * D : (t % 2 + 1) * D],
                            in_=src_d[128 * (t % 4) : 128 * (t % 4 + 1), :],
                        ).then_inc(fsg[t % 2].s, 16)
                        fsg[t % 2].n += 16


    pE.close()
    qS.close()
    st.close()
    return nc


def _host_prep(inputs):
    x = np.ascontiguousarray(np.asarray(inputs["L0_attn_x"]).reshape(T, D))
    Wq = np.asarray(inputs["L0_attn_Wq"])
    Wk = np.asarray(inputs["L0_attn_Wk"])
    Wv = np.asarray(inputs["L0_attn_Wv"])
    Wo = np.asarray(inputs["L0_attn_Wo"])
    wg = np.asarray(inputs["L0_ffn_w_gate"])
    wu = np.asarray(inputs["L0_ffn_w_up"])
    wd = np.asarray(inputs["L0_ffn_w_down"])
    sg = np.asarray(inputs["L0_ffn_s_gate"])
    su = np.asarray(inputs["L0_ffn_s_up"])
    sd = np.asarray(inputs["L0_ffn_s_down"])

    wg_t = np.ascontiguousarray(
        wg.reshape(MCn, 128, ICn, 128).transpose(2, 1, 0, 3))
    wu_t = np.ascontiguousarray(
        wu.reshape(MCn, 128, ICn, 128).transpose(2, 1, 0, 3))
    wd_t = np.ascontiguousarray(
        wd.reshape(ICn, 128, OCn, 128).transpose(2, 1, 0, 3))
    sgT = np.ascontiguousarray(sg.reshape(ICn, 128).T)
    suT = np.ascontiguousarray(su.reshape(ICn, 128).T)
    sdT = np.ascontiguousarray(sd.reshape(OCn, 128).T)

    theta = (1.0 / np.power(
        10000.0, np.arange(0, 64, 2, dtype=np.float32) / 64.0)).astype(np.float32)
    pos = np.arange(T, dtype=np.float32)
    ang = np.outer(pos, theta)
    rc = np.cos(ang).astype(np.float32)
    rs = np.sin(ang).astype(np.float32)
    rc_t = np.ascontiguousarray(
        rc.reshape(NCH, 128, 32).transpose(1, 0, 2).reshape(128, NCH * 32))
    rs_t = np.ascontiguousarray(
        rs.reshape(NCH, 128, 32).transpose(1, 0, 2).reshape(128, NCH * 32))

    kk = np.arange(128)[:, None]
    qq = np.arange(256)[None, :]
    mk = np.zeros((2, 128, 1024), dtype=np.float32)
    for rel in range(2):
        m = np.where(128 * rel + kk > qq, NEGBIG, 0.0).astype(np.float32)
        mk[rel] = np.tile(m, (1, NH))

    idb = np.eye(128, dtype=ml_dtypes.bfloat16)
    idf = np.eye(128, dtype=np.float32)

    in_maps = []
    for c in range(NCORES):
        w_all_c = np.ascontiguousarray(np.concatenate(
            [Wq[:, 256 * c : 256 * (c + 1)],
             Wk[:, 64 * c : 64 * (c + 1)],
             Wv[:, 64 * c : 64 * (c + 1)]], axis=1))
        in_maps.append({
            "x_own": np.ascontiguousarray(x[TOK * c : TOK * (c + 1)]),
            "w_all": w_all_c,
            "wo_in": np.ascontiguousarray(Wo[256 * c : 256 * (c + 1), :]),
            "wg_t": wg_t, "wu_t": wu_t, "wd_t": wd_t,
            "sg_in": sgT, "su_in": suT, "sd_in": sdT,
            "rc_in": rc_t, "rs_in": rs_t, "mk_in": mk,
            "idb_in": idb, "idf_in": idf,
        })
    return in_maps


def run(inputs, taps=(), max_phase=9):
    in_maps = _host_prep(inputs)
    nc = build(taps=taps, max_phase=max_phase)
    res = run_bass_kernel_spmd(nc, in_maps, core_ids=list(range(NCORES)))
    return res.results


def kernel(**inputs):
    results = run(inputs)
    out = np.concatenate([results[c]["out_own"] for c in range(NCORES)], axis=0)
    return out.reshape(1, T, D).astype(np.float32)
